# version 02
# baseline (speedup 1.0000x reference)
import os
import contextlib
import numpy as np
import ml_dtypes
import concourse.bass as bass
import concourse.mybir as mybir
from concourse.bass_utils import run_bass_kernel_spmd

F32 = mybir.dt.float32
BF16 = mybir.dt.bfloat16
AF = mybir.ActivationFunctionType
ALU = mybir.AluOpType
BF = ml_dtypes.bfloat16

B, S, D = 4, 4096, 2048
T = 2048
NBLK = 4
TB = 512
DC = 16
HQ, HKV, DH = 16, 4, 128
CH = 64
CPB = TB // CH
NE = 8
FC = 16
TH = 2
THW = 1024
TS = 2
TSW = 512
EPS = 1e-5
RN = 1.0 / 2048.0

DEBUG_X1 = bool(os.environ.get("MK_DEBUG_X1"))
DEBUG2 = bool(os.environ.get("MK_DEBUG2"))
SKIP_B = bool(os.environ.get("MK_SKIP_B"))
NO_QHAT = bool(os.environ.get("MK_NO_QHAT"))
NO_LGT = bool(os.environ.get("MK_NO_LGT"))
MKNE = int(os.environ.get("MK_NE", str(NE + 1)))
NOMASK = bool(os.environ.get("MK_NOMASK"))
NOTAIL = bool(os.environ.get("MK_NOTAIL"))
TAILCUT = int(os.environ.get("MK_TAILCUT", "5"))
MKTH = int(os.environ.get("MK_TH", str(TH)))
NBI = int(os.environ.get("MK_NBI", "8"))



def _swz_lhsT(w):
    w = np.ascontiguousarray(w, np.float32)
    K, M = w.shape
    KC, MC = K // 128, M // 128
    return np.ascontiguousarray(
        w.reshape(KC, 128, MC, 128).transpose(2, 1, 0, 3).reshape(MC, 128, KC * 128))


def _swz_x(xt):
    return np.ascontiguousarray(
        xt.reshape(DC, 128, NBLK, TB).transpose(2, 1, 0, 3).reshape(NBLK, 128, DC * TB))


def _lam():
    return np.exp2(-0.5 * (np.arange(HQ, dtype=np.float64) + 1))


def _host_tables():
    t = {}
    lam = _lam()
    r = np.arange(CH, dtype=np.float64)
    diagT = np.zeros((64, HQ * 64), np.float32)
    for h in range(HQ):
        dm = np.exp(-lam[h] * (r[None, :] - r[:, None]))
        diagT[:, h * 64:(h + 1) * 64] = np.where(
            r[None, :] >= r[:, None], dm, 0.0).astype(np.float32)
    t["diagT"] = diagT
    qdecb = np.zeros((128, HQ * 64), np.float32)
    for h in range(HQ):
        qdecb[:, h * 64:(h + 1) * 64] = np.exp(-lam[h] * (r + 1.0))[None, :]
    t["qdecb"] = qdecb
    kdec = np.zeros((64, 16), np.float32)
    for h in range(HQ):
        kdec[:, h] = np.exp(-lam[h] * (CH - 1.0 - r))
    t["kdec"] = kdec
    t["ident"] = np.eye(128, dtype=BF)
    t["onesc"] = np.ones((128, 1), BF)
    t["onesr"] = np.ones((1, 128), BF)
    t["oneh"] = np.eye(8, dtype=np.float32)
    onehr = np.zeros((1, 64), BF)
    for e in range(8):
        onehr[0, e * 8 + e] = 1.0
    t["onehr"] = onehr
    selb = np.zeros((8, 8 * 128), BF)
    for e in range(8):
        selb[e, e * 128:(e + 1) * 128] = 1.0
    t["selb"] = selb
    return t


_BLKD = [float(np.exp(-_lam()[h] * CH)) for h in range(HQ)]


def _rope_tables(pos):
    inv = 10000.0 ** (-(np.arange(64, dtype=np.float64)) / 64.0)
    f = pos[None, :].astype(np.float64) * inv[:, None]
    cos = np.cos(f).astype(np.float32)
    sin = np.sin(f).astype(np.float32)
    return (np.concatenate([cos, cos], 0).astype(BF),
            np.concatenate([-sin, sin], 0).astype(BF))



class Plan:
    def __init__(self):
        self.items = []

    def add(self, eng, fn):
        self.items.append((eng, fn))

    def emit(self, name, eng):
        for e, fn in self.items:
            if e == name:
                fn(eng)


class Ctr:
    def __init__(self, nc, name):
        self.sem = nc.ctx.enter_context(nc.semaphore(name))
        self.val = 0


class Bank:
    def __init__(self, nc, name, dtype=F32, width=512):
        self.t = nc.ctx.enter_context(nc.psum_tensor(name, [128, width], dtype))
        self.rdy = Ctr(nc, name + "_r")
        self.fre = Ctr(nc, name + "_f")


class Ring:

    def __init__(self, nc, plan, stack, name, nslots, width, dtype=BF16):
        self.plan = plan
        self.nslots = nslots
        self.width = width
        self.t = stack.enter_context(nc.sbuf_tensor(name, [128, nslots * width], dtype))
        self.wld = [stack.enter_context(nc.semaphore(f"{name}_l{i}")) for i in range(nslots)]
        self.wfr = [stack.enter_context(nc.semaphore(f"{name}_f{i}")) for i in range(nslots)]
        self.n = 0
        self.fr_val = [0] * nslots

    def load(self, src_ap, w=None):
        slot = self.n % self.nslots
        gen = self.n // self.nslots
        self.n += 1
        need_free = self.fr_val[slot]
        wld, wfr, t, width = self.wld[slot], self.wfr[slot], self.t, self.width
        w = width if w is None else w

        def pool_emit(g, slot=slot, gen=gen, src_ap=src_ap, need_free=need_free,
                      wld=wld, wfr=wfr, t=t, width=width, w=w):
            if gen > 0:
                g.wait_ge(wfr, need_free)
            g.dma_start(t[:, slot * width:slot * width + w], src_ap).then_inc(wld, 16)
        self.plan.add("POOL", pool_emit)
        self.fr_val[slot] += 1
        return RingHandle(self, slot, gen, w)


class RingHandle:
    def __init__(self, ring, slot, gen, w):
        self.ring, self.slot, self.gen, self.w = ring, slot, gen, w

    @property
    def ap(self):
        r = self.ring
        return r.t[:, self.slot * r.width:self.slot * r.width + self.w]

    def wait(self, eng):
        eng.wait_ge(self.ring.wld[self.slot], 16 * (self.gen + 1))

    def free_on(self, instr):
        instr.then_inc(self.ring.wfr[self.slot], 1)



_CACHE = {}


def _build():
    nc = bass.Bass("TRN2", target_bir_lowering=False)
    nc.ctx = contextlib.ExitStack()

    xb = nc.dram_tensor("xb", [NBLK, 128, DC * TB], F32, kind="ExternalInput")
    xp = nc.dram_tensor("xp", [NBLK, 128, DC * TB], F32, kind="ExternalInput")
    wqT = nc.dram_tensor("wqT", [16, 128, 2048], F32, kind="ExternalInput")
    wkT = nc.dram_tensor("wkT", [4, 128, 2048], F32, kind="ExternalInput")
    wvR = nc.dram_tensor("wvR", [4, 128, 2048], F32, kind="ExternalInput")
    woT = nc.dram_tensor("woT", [16, 128, 2048], F32, kind="ExternalInput")
    rtr = nc.dram_tensor("rtr", [128, 128], F32, kind="ExternalInput")
    gW = nc.dram_tensor("gW", [NE + 1, 16, 128, 2048], F32, kind="ExternalInput")
    uW = nc.dram_tensor("uW", [NE + 1, 16, 128, 2048], F32, kind="ExternalInput")
    dW = nc.dram_tensor("dW", [NE + 1, 16, 128, 2048], F32, kind="ExternalInput")
    cosb_d = nc.dram_tensor("cosb", [128, T], BF16, kind="ExternalInput")
    sinb_d = nc.dram_tensor("sinb", [128, T], BF16, kind="ExternalInput")
    cosp_d = nc.dram_tensor("cosp", [128, T], BF16, kind="ExternalInput")
    sinp_d = nc.dram_tensor("sinp", [128, T], BF16, kind="ExternalInput")
    diagT_d = nc.dram_tensor("diagTd", [64, HQ * 64], F32, kind="ExternalInput")
    qdecb_d = nc.dram_tensor("qdecbd", [128, HQ * 64], F32, kind="ExternalInput")
    kdec_d = nc.dram_tensor("kdecd", [64, 16], F32, kind="ExternalInput")
    ident_d = nc.dram_tensor("identd", [128, 128], BF16, kind="ExternalInput")
    onesc_d = nc.dram_tensor("onescd", [128, 1], BF16, kind="ExternalInput")
    onesr_d = nc.dram_tensor("onesrd", [1, 128], BF16, kind="ExternalInput")
    oneh_d = nc.dram_tensor("onehd", [8, 8], F32, kind="ExternalInput")
    onehr_d = nc.dram_tensor("onehrd", [1, 64], BF16, kind="ExternalInput")
    selb_d = nc.dram_tensor("selbd", [8, 8 * 128], BF16, kind="ExternalInput")

    out_d = nc.dram_tensor("out", [128, DC * T], F32, kind="ExternalOutput")
    x1d = nc.dram_tensor("x1d", [DC, 128, T], BF16,
                         kind="ExternalOutput" if DEBUG_X1 else "Internal")

    planA = Plan()
    planB = Plan()

    ident = nc.ctx.enter_context(nc.sbuf_tensor("identt", [128, 128], BF16))
    onesc = nc.ctx.enter_context(nc.sbuf_tensor("onescc", [128, 1], BF16))
    onesr = nc.ctx.enter_context(nc.sbuf_tensor("onesrr", [1, 128], BF16))
    oneh = nc.ctx.enter_context(nc.sbuf_tensor("onehh", [8, 8], F32))
    onehr = nc.ctx.enter_context(nc.sbuf_tensor("onehr", [1, 64], BF16))
    selb = nc.ctx.enter_context(nc.sbuf_tensor("selb", [8, 8 * 128], BF16))
    logit_sb = nc.ctx.enter_context(nc.sbuf_tensor("logit_sb", [8, T], F32))
    mscP = nc.ctx.enter_context(nc.sbuf_tensor("mscP", [8, T], BF16))

    pbk = [Bank(nc, f"pb{i}") for i in range(7)]
    pbt = Bank(nc, "pbt", dtype=BF16, width=1024)

    c_kvs = Ctr(nc, "c_kvs")
    c_khat = Ctr(nc, "c_khat")
    c_ssc = Ctr(nc, "c_ssc")
    c_rope = Ctr(nc, "c_rope")
    c_krope = Ctr(nc, "c_krope")
    c_vtm = Ctr(nc, "c_vtm")
    c_oev = Ctr(nc, "c_oev")
    c_och = Ctr(nc, "c_och")
    c_dch = Ctr(nc, "c_dch")
    c_pre = Ctr(nc, "c_pre")
    c_sq = Ctr(nc, "c_sq")
    c_ln = Ctr(nc, "c_ln")
    c_x1a = Ctr(nc, "c_x1a")
    c_x1st = Ctr(nc, "c_x1st")
    c_msk = Ctr(nc, "c_msk")
    c_hb = Ctr(nc, "c_hb")
    c_silu = Ctr(nc, "c_silu")
    c_msb = Ctr(nc, "c_msb")
    c_msbf = Ctr(nc, "c_msbf")
    c_yw = Ctr(nc, "c_yw")
    c_ln2 = Ctr(nc, "c_ln2")
    c_out = Ctr(nc, "c_out")
    c_ost = Ctr(nc, "c_ost")
    c_x1ld = Ctr(nc, "c_x1ld")
    c_cst = Ctr(nc, "c_cst")
    c_vsy = Ctr(nc, "c_vsy")
    c_sqr = Ctr(nc, "c_sqr")
    c_ostp = [Ctr(nc, "c_ostp0"), Ctr(nc, "c_ostp1")]
    c_l2w = Ctr(nc, "c_l2w")
    c_sq2w = Ctr(nc, "c_sq2w")
    c_st2r = Ctr(nc, "c_st2r")

    def mkvs(n):
        base = c_vsy.val
        c_vsy.val += n
        return base

    def VS(v, base, i):
        v.engine_nop().then_inc(c_vsy.sem, 1)
        v.wait_ge(c_vsy.sem, base + i + 1)

    scopeA = contextlib.ExitStack()
    sbA = lambda n, s, d=BF16: scopeA.enter_context(nc.sbuf_tensor(n, s, d))
    x_sl = [sbA("x_sl0", [128, DC * TB])]
    x_ld = [Ctr(nc, "x_ld0")]
    x_fr = [Ctr(nc, "x_fr0")]
    ring = Ring(nc, planA, scopeA, "wring", 4, 2048)
    q_fm = sbA("q_fm", [128, HQ * TB])
    qhat = sbA("qhat", [128, HQ * TB])
    k_fm = sbA("k_fm", [128, HKV * TB])
    v_tm = sbA("v_tm", [64, CPB * 512])
    khat = sbA("khat", [64, 2 * HQ * 128])
    ssc = sbA("sscb", [64, 2 * HQ * 64])
    o_fm = sbA("o_fm", [128, HQ * TB])
    kv_bf = sbA("kv_bf", [128, HQ * DH])
    pre = sbA("preb", [128, DC * TB])
    sqb = sbA("sqb", [128, 2 * TB])
    rs1 = sbA("rs1", [128, TB], F32)
    rs2 = sbA("rs2", [128, TB], F32)
    rs3 = sbA("rs3", [128, TB], F32)
    lnrow = sbA("lnrow", [1, 7 * TB], F32)
    lnbfr = sbA("lnbfr", [1, 2 * TB])
    rbb = sbA("rbb", [128, 2 * TB])
    cos_o = sbA("cos_o", [128, T])
    sin_o = sbA("sin_o", [128, T])
    cos_p = sbA("cos_p", [128, T])
    sin_p = sbA("sin_p", [128, T])
    diagT = sbA("diagTs", [64, HQ * 64], F32)
    qdecb = sbA("qdecbs", [128, HQ * 64], F32)
    kdec = sbA("kdecs", [64, 16], F32)
    x1b = pre

    def _bc8(ec):
        return bass.AP(qdecb, ec * 64, [[HQ * 64, 128], [0, 8], [1, 64]])

    def emit_const(sy):
        for dst, src in [(ident, ident_d), (onesc, onesc_d), (onesr, onesr_d),
                         (oneh, oneh_d), (onehr, onehr_d), (selb, selb_d),
                         (cos_o, cosb_d), (sin_o, sinb_d),
                         (cos_p, cosp_d), (sin_p, sinp_d), (diagT, diagT_d),
                         (qdecb, qdecb_d), (kdec, kdec_d)]:
            sy.dma_start(dst[:, :], src.ap()).then_inc(c_cst.sem, 16)
    planA.add("SYNC", emit_const)
    c_cst.val += 16 * 13
    CST = c_cst.val

    def emit_kv0(v):
        v.wait_ge(c_cst.sem, CST)
        v.memset(kv_bf[:, :], 0.0)
        v.engine_nop().then_inc(c_kvs.sem, 1)
    planA.add("DVE", emit_kv0)
    c_kvs.val += 1

    n_own_chunk = 0

    for bi in range(min(NBI, 2 * NBLK)):
        warm = bi < NBLK
        blk = bi % NBLK
        xslot = 0
        xsrc = xp if warm else xb
        xs = x_sl[0]
        cos_t, sin_t = (cos_p, sin_p) if warm else (cos_o, sin_o)

        need_xfree = x_fr[0].val

        def pool_x(g, bi=bi, xsrc=xsrc, blk=blk, need_xfree=need_xfree):
            if bi > 0:
                g.wait_ge(x_fr[0].sem, need_xfree)
            g.wait_ge(c_cst.sem, CST)
            g.dma_start(x_sl[0][:, :], xsrc.ap()[blk, :, :]).then_inc(
                x_ld[0].sem, 16)
        planA.add("POOL", pool_x)
        x_ld[0].val += 16
        XNEED = x_ld[0].val

        if not warm:
            for ec in range(HQ):
                h = ring.load(wqT.ap()[ec, :, :])
                bank = pbk[ec % 2]
                gen0 = bank.fre.val

                RDY = bank.rdy.val + 1

                def pe_q(t, h=h, bank=bank, gen0=gen0, xs=xs, XNEED=XNEED,
                         xslot=xslot, RDY=RDY):
                    h.wait(t)
                    t.wait_ge(x_ld[xslot].sem, XNEED)
                    if gen0 > 0:
                        t.wait_ge(bank.fre.sem, gen0)
                    ins = None
                    for dc in range(DC):
                        ins = t.matmul(bank.t[:, :],
                                       h.ap[:, dc * 128:(dc + 1) * 128],
                                       xs[:, dc * TB:(dc + 1) * TB],
                                       start=(dc == 0), stop=(dc == DC - 1))
                    ins.then_inc(bank.rdy.sem, 1)
                    t.wait_ge(bank.rdy.sem, RDY)
                    h.free_on(t.nop())
                planA.add("PE", pe_q)
                bank.rdy.val += 1
                bank.fre.val += 1

                vs0 = mkvs(4)

                def dve_rope(v, bank=bank, ec=ec, blk=blk, RDY=RDY,
                             cos_t=cos_t, sin_t=sin_t, vs0=vs0):
                    v.wait_ge(bank.rdy.sem, RDY)
                    VS(v, vs0, 0)
                    ps = bank.t
                    v.tensor_copy(rs1[64:128, :], ps[0:64, :])
                    v.tensor_copy(rs1[0:64, :], ps[64:128, :])
                    v.tensor_tensor(rs2[:, :], ps[:, :],
                                    cos_t[:, blk * TB:(blk + 1) * TB],
                                    ALU.mult).then_inc(bank.fre.sem, 1)
                    VS(v, vs0, 1)
                    v.tensor_tensor(rs3[:, :], rs1[:, :],
                                    sin_t[:, blk * TB:(blk + 1) * TB], ALU.mult)
                    VS(v, vs0, 2)
                    v.tensor_tensor(q_fm[:, ec * TB:(ec + 1) * TB],
                                    rs2[:, :], rs3[:, :], ALU.add)
                    VS(v, vs0, 3)
                    if NO_QHAT:
                        v.tensor_copy(qhat[:, ec * TB:(ec + 1) * TB],
                                      q_fm[:, ec * TB:(ec + 1) * TB])
                        v.engine_nop().then_inc(c_rope.sem, 1)
                    else:
                        v.tensor_tensor(qhat[:, ec * TB:(ec + 1) * TB],
                                        q_fm[:, ec * TB:(ec + 1) * TB],
                                        _bc8(ec), ALU.mult).then_inc(c_rope.sem, 1)
                planA.add("DVE", dve_rope)
                c_rope.val += 1

        for ec in range(HKV):
            h = ring.load(wkT.ap()[ec, :, :])
            bank = pbk[ec % 2]
            gen0 = bank.fre.val

            RDY = bank.rdy.val + 1

            def pe_k(t, h=h, bank=bank, gen0=gen0, xs=xs, XNEED=XNEED,
                     xslot=xslot, RDY=RDY):
                h.wait(t)
                t.wait_ge(x_ld[xslot].sem, XNEED)
                if gen0 > 0:
                    t.wait_ge(bank.fre.sem, gen0)
                ins = None
                for dc in range(DC):
                    ins = t.matmul(bank.t[:, :],
                                   h.ap[:, dc * 128:(dc + 1) * 128],
                                   xs[:, dc * TB:(dc + 1) * TB],
                                   start=(dc == 0), stop=(dc == DC - 1))
                ins.then_inc(bank.rdy.sem, 1)
                t.wait_ge(bank.rdy.sem, RDY)
                h.free_on(t.nop())
            planA.add("PE", pe_k)
            bank.rdy.val += 1
            bank.fre.val += 1

            vs0 = mkvs(3)

            def dve_krope(v, bank=bank, ec=ec, blk=blk, RDY=RDY,
                          cos_t=cos_t, sin_t=sin_t, vs0=vs0):
                v.wait_ge(bank.rdy.sem, RDY)
                VS(v, vs0, 0)
                ps = bank.t
                v.tensor_copy(rs1[64:128, :], ps[0:64, :])
                v.tensor_copy(rs1[0:64, :], ps[64:128, :])
                v.tensor_tensor(rs2[:, :], ps[:, :],
                                cos_t[:, blk * TB:(blk + 1) * TB],
                                ALU.mult).then_inc(bank.fre.sem, 1)
                VS(v, vs0, 1)
                v.tensor_tensor(rs3[:, :], rs1[:, :],
                                sin_t[:, blk * TB:(blk + 1) * TB], ALU.mult)
                VS(v, vs0, 2)
                v.tensor_tensor(k_fm[:, ec * TB:(ec + 1) * TB],
                                rs2[:, :], rs3[:, :], ALU.add).then_inc(
                    c_krope.sem, 1)
            planA.add("DVE", dve_krope)
            c_krope.val += 1

        vh = [ring.load(wvR.ap()[j, :, :]) for j in range(4)]
        for tc in range(4):
            bank = pbk[2 + tc % 2]
            gen0 = bank.fre.val

            RDY = bank.rdy.val + 1

            def pe_v(t, vh=vh, bank=bank, gen0=gen0, tc=tc, xs=xs, XNEED=XNEED,
                     xslot=xslot, RDY=RDY):
                if tc == 0:
                    for hh in vh:
                        hh.wait(t)
                    t.wait_ge(x_ld[xslot].sem, XNEED)
                if gen0 > 0:
                    t.wait_ge(bank.fre.sem, gen0)
                ins = None
                for dc in range(DC):
                    ins = t.matmul(
                        bank.t[:, :],
                        xs[:, dc * TB + tc * 128: dc * TB + (tc + 1) * 128],
                        vh[dc // 4].ap[:, (dc % 4) * 512:(dc % 4 + 1) * 512],
                        start=(dc == 0), stop=(dc == DC - 1))
                ins.then_inc(bank.rdy.sem, 1)
                if tc == 3:
                    t.wait_ge(bank.rdy.sem, RDY)
                    for hh in vh:
                        hh.free_on(t.nop())
            planA.add("PE", pe_v)
            bank.rdy.val += 1
            bank.fre.val += 1

            FREV = bank.fre.val

            def act_vev(s, bank=bank, tc=tc, RDY=RDY, FREV=FREV):
                s.wait_ge(bank.rdy.sem, RDY)
                s.copy(v_tm[:, (2 * tc) * 512:(2 * tc + 1) * 512],
                       bank.t[0:64, :])
                s.copy(v_tm[:, (2 * tc + 1) * 512:(2 * tc + 2) * 512],
                       bank.t[64:128, :]).then_inc(bank.fre.sem, 1)
                s.wait_ge(bank.fre.sem, FREV)
                s.nop().then_inc(c_vtm.sem, 1)
            planA.add("ACT", act_vev)
            c_vtm.val += 1

        if warm:
            vr_last = pbk[3].rdy.val

            def pe_xfree(t, xslot=xslot, vr_last=vr_last):
                t.wait_ge(pbk[3].rdy.sem, vr_last)
                t.nop().then_inc(x_fr[xslot].sem, 1)
            planA.add("PE", pe_xfree)
            x_fr[xslot].val += 1

        KR = c_krope.val
        VR = c_vtm.val
        for cc in range(0 if NO_LGT else CPB):
            gchunk = bi * CPB + cc
            kv_base = c_kvs.val

            genv = pbt.fre.val

            def pe_tr(t, cc=cc, genv=genv, KR=KR):
                t.wait_ge(c_krope.sem, KR)
                if genv > 0:
                    t.wait_ge(pbt.fre.sem, genv)
                ins = None
                for g in range(4):
                    ins = t.transpose(
                        pbt.t[0:64, g * 128:(g + 1) * 128],
                        k_fm[:, g * TB + cc * 64: g * TB + (cc + 1) * 64],
                        ident[:, :])
                ins.then_inc(pbt.rdy.sem, 1)
            planA.add("PE", pe_tr)
            pbt.rdy.val += 1
            pbt.fre.val += 1
            TRDY = pbt.rdy.val
            DCHW = max(0, gchunk - 1)

            def dve_khat(v, cc=cc, TRDY=TRDY, DCHW=DCHW):
                v.wait_ge(pbt.rdy.sem, TRDY)
                if DCHW > 0:
                    v.wait_ge(c_dch.sem, DCHW)
                ins = None
                for h in range(HQ):
                    g = h // 4
                    ins = v.tensor_scalar(
                        khat[0:64, ((cc % 2) * HQ + h) * 128:
                             ((cc % 2) * HQ + h + 1) * 128],
                        pbt.t[0:64, g * 128:(g + 1) * 128],
                        kdec[:, h:h + 1], None, ALU.mult)
                ins.then_inc(pbt.fre.sem, 1)
                v.engine_nop().then_inc(c_khat.sem, 1)
            planA.add("DVE", dve_khat)
            c_khat.val += 1
            KHAT_CC = c_khat.val

            if not warm:
                QR = c_rope.val
                for hg in range(2):
                    bank = pbk[hg]
                    gen0 = bank.fre.val

                    def pe_st(t, bank=bank, hg=hg, cc=cc, gen0=gen0, QR=QR):
                        t.wait_ge(c_rope.sem, QR)
                        if gen0 > 0:
                            t.wait_ge(bank.fre.sem, gen0)
                        ins = None
                        for hh in range(8):
                            h = hg * 8 + hh
                            g = h // 4
                            ins = t.matmul(
                                bank.t[0:64, hh * 64:(hh + 1) * 64],
                                k_fm[:, g * TB + cc * 64: g * TB + (cc + 1) * 64],
                                q_fm[:, h * TB + cc * 64: h * TB + (cc + 1) * 64],
                                start=True, stop=True)
                        ins.then_inc(bank.rdy.sem, 1)
                    planA.add("PE", pe_st)
                    bank.rdy.val += 1
                    bank.fre.val += 1
                    RDY = bank.rdy.val
                    OCH = max(0, n_own_chunk - 1)

                    def dve_diag(v, bank=bank, hg=hg, cc=cc, RDY=RDY, OCH=OCH):
                        v.wait_ge(bank.rdy.sem, RDY)
                        if OCH > 0:
                            v.wait_ge(c_och.sem, OCH)
                        ins = None
                        for hh in range(8):
                            h = hg * 8 + hh
                            ins = v.tensor_tensor(
                                ssc[0:64, ((cc % 2) * HQ + h) * 64:
                                    ((cc % 2) * HQ + h + 1) * 64],
                                bank.t[0:64, hh * 64:(hh + 1) * 64],
                                diagT[:, h * 64:(h + 1) * 64], ALU.mult)
                        ins.then_inc(bank.fre.sem, 1)
                        v.engine_nop().then_inc(c_ssc.sem, 1)
                    planA.add("DVE", dve_diag)
                    c_ssc.val += 1

                SSC = c_ssc.val
                for hg in range(2):
                    bank = pbk[2 + hg]
                    gen0 = bank.fre.val

                    def pe_o(t, bank=bank, hg=hg, cc=cc,
                             gen0=gen0, SSC=SSC, kv_base=kv_base):
                        t.wait_ge(c_ssc.sem, SSC)
                        t.wait_ge(c_kvs.sem, kv_base)
                        if gen0 > 0:
                            t.wait_ge(bank.fre.sem, gen0)
                        ins = None
                        for hh in range(8):
                            h = hg * 8 + hh
                            g = h // 4
                            t.matmul(
                                bank.t[:, hh * 64:(hh + 1) * 64],
                                v_tm[:, cc * 512 + g * 128:
                                     cc * 512 + (g + 1) * 128],
                                ssc[0:64, ((cc % 2) * HQ + h) * 64:
                                    ((cc % 2) * HQ + h + 1) * 64],
                                start=True, stop=False)
                            ins = t.matmul(
                                bank.t[:, hh * 64:(hh + 1) * 64],
                                kv_bf[:, h * DH:(h + 1) * DH],
                                qhat[:, h * TB + cc * 64: h * TB + (cc + 1) * 64],
                                start=False, stop=True)
                        ins.then_inc(bank.rdy.sem, 1)
                    planA.add("PE", pe_o)
                    bank.rdy.val += 1
                    bank.fre.val += 1
                    RDY = bank.rdy.val

                    FREV = bank.fre.val

                    def act_oev(s, bank=bank, hg=hg, cc=cc, RDY=RDY, FREV=FREV):
                        s.wait_ge(bank.rdy.sem, RDY)
                        dst = bass.AP(o_fm, hg * 8 * TB + cc * 64,
                                      [[HQ * TB, 128], [TB, 8], [1, 64]])
                        s.copy(dst, bank.t[:, :]).then_inc(bank.fre.sem, 1)
                        s.wait_ge(bank.fre.sem, FREV)
                        s.nop().then_inc(c_oev.sem, 1)
                    planA.add("ACT", act_oev)
                    c_oev.val += 1

                o_rdy_last = pbk[3].rdy.val

                def pe_och(t, o_rdy_last=o_rdy_last):
                    t.wait_ge(pbk[3].rdy.sem, o_rdy_last)
                    t.nop().then_inc(c_och.sem, 1)
                planA.add("PE", pe_och)
                c_och.val += 1
                n_own_chunk += 1

            KH = KHAT_CC
            for quad in range(4):
                bank = pbk[6]
                gen0 = bank.fre.val

                def pe_delta(t, bank=bank, quad=quad, cc=cc,
                             gen0=gen0, KH=KH, VR=VR):
                    t.wait_ge(c_khat.sem, KH)
                    t.wait_ge(c_vtm.sem, VR)
                    if gen0 > 0:
                        t.wait_ge(bank.fre.sem, gen0)
                    ins = None
                    for r in range(4):
                        h = quad * 4 + r
                        g = h // 4
                        ins = t.matmul(
                            bank.t[:, r * 128:(r + 1) * 128],
                            khat[0:64, ((cc % 2) * HQ + h) * 128:
                                 ((cc % 2) * HQ + h + 1) * 128],
                            v_tm[:, cc * 512 + g * 128: cc * 512 + (g + 1) * 128],
                            start=True, stop=True)
                    ins.then_inc(bank.rdy.sem, 1)
                planA.add("PE", pe_delta)
                bank.rdy.val += 1
                bank.fre.val += 1
                RDY = bank.rdy.val

                def dve_kv(v, bank=bank, quad=quad, RDY=RDY,
                           kv_base=kv_base):
                    v.wait_ge(bank.rdy.sem, RDY)
                    if kv_base > 1:
                        v.wait_ge(c_kvs.sem, kv_base)
                    ins = None
                    for r in range(4):
                        h = quad * 4 + r
                        ins = v.scalar_tensor_tensor(
                            kv_bf[:, h * DH:(h + 1) * DH],
                            kv_bf[:, h * DH:(h + 1) * DH],
                            _BLKD[h],
                            bank.t[:, r * 128:(r + 1) * 128],
                            ALU.mult, ALU.add)
                        ins.then_inc(c_kvs.sem, 1)
                    v.engine_nop().then_inc(bank.fre.sem, 1)
                planA.add("DVE", dve_kv)
                c_kvs.val += 4

            d_rdy_last = pbk[6].rdy.val

            def pe_dch(t, d_rdy_last=d_rdy_last):
                t.wait_ge(pbk[6].rdy.sem, d_rdy_last)
                t.nop().then_inc(c_dch.sem, 1)
            planA.add("PE", pe_dch)
            c_dch.val += 1

        if warm:
            continue

        OEV = c_oev.val
        X1ST_prev = c_x1st.val
        woh = [ring.load(woT.ap()[dc, :, :]) for dc in range(DC)]
        for dc in range(DC):
            h = woh[dc]
            bank = pbk[dc % 2]
            gen0 = bank.fre.val

            RDY = bank.rdy.val + 1

            def pe_oproj(t, h=h, bank=bank, dc=dc, gen0=gen0, OEV=OEV, RDY=RDY):
                h.wait(t)
                if dc == 0:
                    t.wait_ge(c_oev.sem, OEV)
                if gen0 > 0:
                    t.wait_ge(bank.fre.sem, gen0)
                ins = None
                for ec in range(16):
                    ins = t.matmul(bank.t[:, :],
                                   h.ap[:, ec * 128:(ec + 1) * 128],
                                   o_fm[:, ec * TB:(ec + 1) * TB],
                                   start=(ec == 0), stop=(ec == 15))
                ins.then_inc(bank.rdy.sem, 1)
                t.wait_ge(bank.rdy.sem, RDY)
                h.free_on(t.nop())
            planA.add("PE", pe_oproj)
            bank.rdy.val += 1
            bank.fre.val += 1

            def dve_pre(v, bank=bank, dc=dc, xs=xs, RDY=RDY, X1ST_prev=X1ST_prev):
                v.wait_ge(bank.rdy.sem, RDY)
                if dc == 0 and X1ST_prev > 0:
                    v.wait_ge(c_x1st.sem, X1ST_prev)
                v.tensor_tensor(pre[:, dc * TB:(dc + 1) * TB],
                                xs[:, dc * TB:(dc + 1) * TB],
                                bank.t[:, :], ALU.add).then_inc(
                    bank.fre.sem, 1)
                v.engine_nop().then_inc(c_pre.sem, 1)
            planA.add("DVE", dve_pre)
            c_pre.val += 1

        def dve_xfree(v, xslot=xslot):
            v.engine_nop().then_inc(x_fr[xslot].sem, 1)
        planA.add("DVE", dve_xfree)
        x_fr[xslot].val += 1

        PRE_ALL = c_pre.val
        g4 = pbk[4].fre.val
        g5 = pbk[5].fre.val
        sq_base = c_sq.val
        sqr_base = c_sqr.val
        for dc in range(DC):
            def act_sq(s, dc=dc, PRE_ALL=PRE_ALL, sqr_base=sqr_base):
                s.wait_ge(c_pre.sem, PRE_ALL - (DC - 1) + dc)
                if dc >= 2:
                    s.wait_ge(c_sqr.sem, sqr_base + dc - 1)
                s.square(sqb[:, (dc % 2) * TB:(dc % 2 + 1) * TB],
                         pre[:, dc * TB:(dc + 1) * TB]).then_inc(c_sq.sem, 1)
            planA.add("ACT", act_sq)
            c_sq.val += 1
            SQ = c_sq.val

            def pe_stats(t, dc=dc, SQ=SQ, PRE_ALL=PRE_ALL, g4=g4, g5=g5,
                         sqr_base=sqr_base):
                if dc == 0:
                    t.wait_ge(c_pre.sem, PRE_ALL)
                    if g4 > 0:
                        t.wait_ge(pbk[4].fre.sem, g4)
                    if g5 > 0:
                        t.wait_ge(pbk[5].fre.sem, g5)
                t.matmul(pbk[4].t[0:1, :], onesc[:, :],
                         pre[:, dc * TB:(dc + 1) * TB],
                         start=(dc == 0), stop=(dc == 15))
                t.wait_ge(c_sq.sem, SQ)
                ins = t.matmul(pbk[5].t[0:1, :], onesc[:, :],
                               sqb[:, (dc % 2) * TB:(dc % 2 + 1) * TB],
                               start=(dc == 0), stop=(dc == 15))
                ins.then_inc(c_sqr.sem, 1)
                if dc == 15:
                    t.wait_ge(c_sqr.sem, sqr_base + 16)
                    t.nop().then_inc(pbk[5].rdy.sem, 1)
            planA.add("PE", pe_stats)
            c_sqr.val += 1
        pbk[4].fre.val += 1
        pbk[5].rdy.val += 1
        pbk[5].fre.val += 1
        SRDY = pbk[5].rdy.val

        vs0 = mkvs(3)

        def dve_lnstats(v, SRDY=SRDY, vs0=vs0):
            v.wait_ge(pbk[5].rdy.sem, SRDY)
            v.tensor_scalar(lnrow[0:1, 0:TB], pbk[4].t[0:1, :], RN, None,
                            ALU.mult).then_inc(pbk[4].fre.sem, 1)
            v.tensor_scalar(lnrow[0:1, 5 * TB:6 * TB], pbk[5].t[0:1, :], RN,
                            None, ALU.mult).then_inc(pbk[5].fre.sem, 1)
            VS(v, vs0, 0)
            v.tensor_tensor(lnrow[0:1, TB:2 * TB], lnrow[0:1, 0:TB],
                            lnrow[0:1, 0:TB], ALU.mult)
            VS(v, vs0, 1)
            v.tensor_tensor(lnrow[0:1, 2 * TB:3 * TB], lnrow[0:1, 5 * TB:6 * TB],
                            lnrow[0:1, TB:2 * TB], ALU.subtract)
            VS(v, vs0, 2)
            v.tensor_scalar(lnrow[0:1, 2 * TB:3 * TB],
                            lnrow[0:1, 2 * TB:3 * TB], EPS, None,
                            ALU.add).then_inc(c_ln.sem, 1)
        planA.add("DVE", dve_lnstats)
        c_ln.val += 1
        LN1 = c_ln.val

        def act_lnsqrt(s, LN1=LN1):
            s.wait_ge(c_ln.sem, LN1)
            s.activation(lnrow[0:1, 3 * TB:4 * TB], lnrow[0:1, 2 * TB:3 * TB],
                         AF.Sqrt).then_inc(c_ln.sem, 1)
        planA.add("ACT", act_lnsqrt)
        c_ln.val += 1
        LNS = c_ln.val

        vs0 = mkvs(2)

        def dve_lnrecip(v, LNS=LNS, vs0=vs0):
            v.wait_ge(c_ln.sem, LNS)
            v.reciprocal(lnrow[0:1, 4 * TB:5 * TB], lnrow[0:1, 3 * TB:4 * TB])
            VS(v, vs0, 0)
            v.tensor_tensor(lnrow[0:1, 6 * TB:7 * TB], lnrow[0:1, 0:TB],
                            lnrow[0:1, 4 * TB:5 * TB], ALU.mult)
            VS(v, vs0, 1)
            v.tensor_copy(lnbfr[0:1, 0:TB], lnrow[0:1, 4 * TB:5 * TB])
            v.tensor_copy(lnbfr[0:1, TB:2 * TB], lnrow[0:1, 6 * TB:7 * TB])
            v.engine_nop().then_inc(c_ln.sem, 1)
        planA.add("DVE", dve_lnrecip)
        c_ln.val += 1
        LN3 = c_ln.val

        for j in range(2):
            bank = pbk[2 + j]
            gen0 = bank.fre.val

            def pe_bc(t, j=j, bank=bank, gen0=gen0, LN3=LN3):
                t.wait_ge(c_ln.sem, LN3)
                if gen0 > 0:
                    t.wait_ge(bank.fre.sem, gen0)
                t.matmul(bank.t[:, :], onesr[:, :],
                         lnbfr[0:1, j * TB:(j + 1) * TB],
                         start=True, stop=True).then_inc(bank.rdy.sem, 1)
            planA.add("PE", pe_bc)
            bank.rdy.val += 1
            bank.fre.val += 1
            RDY = bank.rdy.val

            FREV = bank.fre.val

            def act_bc(s, j=j, bank=bank, RDY=RDY, FREV=FREV):
                s.wait_ge(bank.rdy.sem, RDY)
                s.copy(rbb[:, j * TB:(j + 1) * TB], bank.t[:, :]).then_inc(
                    bank.fre.sem, 1)
                s.wait_ge(bank.fre.sem, FREV)
                s.nop().then_inc(c_ln.sem, 1)
            planA.add("ACT", act_bc)
            c_ln.val += 1
        LN4 = c_ln.val

        X1ST0 = c_x1st.val
        for dc in range(DC):
            vs0 = mkvs(1)

            X1P = c_x1a.val

            def dve_x1(v, dc=dc, LN4=LN4, X1ST0=X1ST0, vs0=vs0, X1P=X1P):
                if dc == 0:
                    v.wait_ge(c_ln.sem, LN4)
                    if X1ST0 > 0:
                        v.wait_ge(c_x1st.sem, X1ST0)
                else:
                    v.wait_ge(c_x1a.sem, X1P)
                v.tensor_tensor(rs2[:, :], pre[:, dc * TB:(dc + 1) * TB],
                                rbb[:, 0:TB], ALU.mult)
                VS(v, vs0, 0)
                v.tensor_tensor(x1b[:, dc * TB:(dc + 1) * TB], rs2[:, :],
                                rbb[:, TB:2 * TB], ALU.subtract).then_inc(
                    c_x1a.sem, 1)
            planA.add("DVE", dve_x1)
            c_x1a.val += 1
            X1A = c_x1a.val

            def sy_x1st(sy, dc=dc, blk=blk, X1A=X1A):
                sy.wait_ge(c_x1a.sem, X1A)
                sy.dma_start(x1d.ap()[dc, :, blk * TB:(blk + 1) * TB],
                             x1b[:, dc * TB:(dc + 1) * TB]).then_inc(
                    c_x1st.sem, 16)
            planA.add("SYNC", sy_x1st)
            c_x1st.val += 16

        rh = ring.load(rtr.ap()[:, :], w=128)
        bank = pbk[6]
        gen0 = bank.fre.val
        X1A_ALL = c_x1a.val

        RRDY = bank.rdy.val + 1

        def pe_router(t, rh=rh, bank=bank, gen0=gen0, X1A_ALL=X1A_ALL,
                      RRDY=RRDY):
            rh.wait(t)
            t.wait_ge(c_x1a.sem, X1A_ALL)
            if gen0 > 0:
                t.wait_ge(bank.fre.sem, gen0)
            ins = None
            for dc in range(DC):
                ins = t.matmul(bank.t[0:8, :], rh.ap[:, dc * 8:(dc + 1) * 8],
                               x1b[:, dc * TB:(dc + 1) * TB],
                               start=(dc == 0), stop=(dc == 15))
            ins.then_inc(bank.rdy.sem, 1)
            t.wait_ge(bank.rdy.sem, RRDY)
            rh.free_on(t.nop())
        planA.add("PE", pe_router)
        bank.rdy.val += 1
        bank.fre.val += 1

        def dve_logit(v, bank=bank, blk=blk, RRDY=RRDY):
            v.wait_ge(bank.rdy.sem, RRDY)
            v.tensor_scalar(logit_sb[:, blk * TB:(blk + 1) * TB],
                            bank.t[0:8, :], 0.0, None, ALU.add).then_inc(
                bank.fre.sem, 1)
        planA.add("DVE", dve_logit)

    X1ST_ALL = c_x1st.val

    def sy_enda(sy):
        sy.wait_ge(c_x1st.sem, X1ST_ALL)
    planA.add("SYNC", sy_enda)

    if DEBUG2:
        dbg_d = {
            "dbg_q": nc.dram_tensor("dq", [128, HQ * TB], BF16, kind="ExternalOutput"),
            "dbg_k": nc.dram_tensor("dk", [128, HKV * TB], BF16, kind="ExternalOutput"),
            "dbg_qh": nc.dram_tensor("dqh", [128, HQ * TB], BF16, kind="ExternalOutput"),
            "dbg_o": nc.dram_tensor("do", [128, HQ * TB], BF16, kind="ExternalOutput"),
            "dbg_kv": nc.dram_tensor("dkv", [128, HQ * DH], BF16, kind="ExternalOutput"),
            "dbg_pre": nc.dram_tensor("dpre", [128, DC * TB], BF16, kind="ExternalOutput"),
            "dbg_vtm": nc.dram_tensor("dvt", [64, CPB * 512], BF16, kind="ExternalOutput"),
            "dbg_lg": nc.dram_tensor("dlg", [8, T], F32, kind="ExternalOutput"),
        }

        def sy_dbg(sy):
            sy.dma_start(dbg_d["dbg_q"].ap(), q_fm[:, :]).then_inc(c_x1st.sem, 16)
            sy.dma_start(dbg_d["dbg_k"].ap(), k_fm[:, :]).then_inc(c_x1st.sem, 16)
            sy.dma_start(dbg_d["dbg_qh"].ap(), qhat[:, :]).then_inc(c_x1st.sem, 16)
            sy.dma_start(dbg_d["dbg_o"].ap(), o_fm[:, :]).then_inc(c_x1st.sem, 16)
            sy.dma_start(dbg_d["dbg_kv"].ap(), kv_bf[:, :]).then_inc(c_x1st.sem, 16)
            sy.dma_start(dbg_d["dbg_pre"].ap(), pre[:, :]).then_inc(c_x1st.sem, 16)
            sy.dma_start(dbg_d["dbg_vtm"].ap(), v_tm[:, :]).then_inc(c_x1st.sem, 16)
            sy.dma_start(dbg_d["dbg_lg"].ap(), logit_sb[:, :]).then_inc(c_x1st.sem, 16)
            sy.wait_ge(c_x1st.sem, X1ST_ALL + 16 * 8)
        planA.add("SYNC", sy_dbg)
        c_x1st.val += 16 * 8
    scopeA.close()

    if SKIP_B:
        with nc.Block("phaseA") as blk:
            @blk.tensor
            def _(t):
                planA.emit("PE", t)

            @blk.vector
            def _(v):
                planA.emit("DVE", v)

            @blk.scalar
            def _(s):
                planA.emit("ACT", s)

            @blk.sync
            def _(sy):
                planA.emit("SYNC", sy)

            @blk.gpsimd
            def _(g):
                planA.emit("POOL", g)
        nc.ctx.close()
        return nc

    scopeB = contextlib.ExitStack()
    sbB = lambda n, s, d=BF16: scopeB.enter_context(nc.sbuf_tensor(n, s, d))
    ringB = Ring(nc, planB, scopeB, "wringB", 6, 2048)
    x1th = sbB("x1th", [128, DC * THW])
    y_acc = sbB("y_acc", [128, DC * THW])
    h_buf = [sbB(f"h_buf{i}", [128, FC * TSW]) for i in range(TS)]
    msb = sbB("msbb", [128, 2 * THW])
    tmpA = [sbB(f"tmpA{i}", [128, TSW]) for i in range(2)]
    tmpSA = [sbB(f"tmpSA{i}", [128, TSW]) for i in range(2)]
    tmpB = [sbB(f"tmpB{i}", [128, TSW]) for i in range(2)]
    l2x = [sbB(f"l2x{i}", [128, THW]) for i in range(2)]
    sq2 = [sbB(f"sq2_{i}", [128, THW]) for i in range(2)]
    rowsc = sbB("rowsc", [1, 16 * 512], F32)
    eqs = sbB("eqs", [1, 8 * 512])
    rb2 = sbB("rb2", [128, 2 * THW])
    ostg = [sbB(f"ostg{i}", [128, THW], F32) for i in range(2)]

    def rsl(i, n=1):
        return rowsc[0:1, i * 512:(i + n) * 512]

    for gq in range(0 if NOMASK else 4):
        for e in range(NE):
            bank = pbk[5 + (gq * 8 + e) % 2]
            gen0 = bank.fre.val

            def pe_mrow(t, e=e, gq=gq, bank=bank, gen0=gen0):
                if gen0 > 0:
                    t.wait_ge(bank.fre.sem, gen0)
                t.matmul(bank.t[0:1, :], oneh[:, e:e + 1],
                         logit_sb[:, gq * 512:(gq + 1) * 512],
                         start=True, stop=True).then_inc(bank.rdy.sem, 1)
            planB.add("PE", pe_mrow)
            bank.rdy.val += 1
            bank.fre.val += 1
            RDY = bank.rdy.val

            FREV = bank.fre.val

            def act_mrow(s, e=e, bank=bank, RDY=RDY, FREV=FREV):
                s.wait_ge(bank.rdy.sem, RDY)
                s.copy(rsl(e), bank.t[0:1, :]).then_inc(bank.fre.sem, 1)
                s.wait_ge(bank.fre.sem, FREV)
                s.nop().then_inc(c_msk.sem, 1)
            planB.add("ACT", act_mrow)
            c_msk.val += 1
        MSK1 = c_msk.val

        vs0 = mkvs(2)

        def dve_mtree(v, MSK1=MSK1, vs0=vs0):
            v.wait_ge(c_msk.sem, MSK1)
            for j in range(4):
                v.tensor_tensor(rsl(8 + j), rsl(2 * j), rsl(2 * j + 1), ALU.max)
            VS(v, vs0, 0)
            v.tensor_tensor(rsl(14), rsl(8), rsl(9), ALU.max)
            v.tensor_tensor(rsl(15), rsl(10), rsl(11), ALU.max)
            VS(v, vs0, 1)
            v.tensor_tensor(rsl(12), rsl(14), rsl(15), ALU.max)
            v.engine_nop().then_inc(c_msk.sem, 1)
        planB.add("DVE", dve_mtree)
        c_msk.val += 1
        MSK2 = c_msk.val

        def act_score(s, MSK2=MSK2):
            s.wait_ge(c_msk.sem, MSK2)
            s.activation(rsl(13), rsl(12), AF.Sigmoid).then_inc(c_msk.sem, 1)
        planB.add("ACT", act_score)
        c_msk.val += 1
        MSK3 = c_msk.val

        vs0 = mkvs(NE)

        def dve_msc(v, MSK3=MSK3, vs0=vs0):
            v.wait_ge(c_msk.sem, MSK3)
            for e in range(NE):
                v.tensor_tensor(rsl(8 + (e % 2)), rsl(e), rsl(12), ALU.is_ge)
                VS(v, vs0, e)
                v.tensor_tensor(eqs[0:1, e * 512:(e + 1) * 512],
                                rsl(8 + (e % 2)), rsl(13), ALU.mult)
            v.engine_nop().then_inc(c_msk.sem, 1)
        planB.add("DVE", dve_msc)
        c_msk.val += 1
        MSK4 = c_msk.val

        bank = pbk[5 + gq % 2]
        gen0 = bank.fre.val

        def pe_mscp(t, gq=gq, bank=bank, gen0=gen0, MSK4=MSK4):
            t.wait_ge(c_msk.sem, MSK4)
            if gen0 > 0:
                t.wait_ge(bank.fre.sem, gen0)
            ins = None
            for e in range(NE):
                ins = t.matmul(bank.t[0:8, :],
                               onehr[0:1, e * 8:(e + 1) * 8],
                               eqs[0:1, e * 512:(e + 1) * 512],
                               start=(e == 0), stop=(e == NE - 1))
            ins.then_inc(bank.rdy.sem, 1)
        planB.add("PE", pe_mscp)
        bank.rdy.val += 1
        bank.fre.val += 1
        RDY = bank.rdy.val

        FREV = bank.fre.val

        def act_mscp(s, gq=gq, bank=bank, RDY=RDY, FREV=FREV):
            s.wait_ge(bank.rdy.sem, RDY)
            s.copy(mscP[:, gq * 512:(gq + 1) * 512],
                   bank.t[0:8, :]).then_inc(bank.fre.sem, 1)
            s.wait_ge(bank.fre.sem, FREV)
            s.nop().then_inc(c_msk.sem, 1)
        planB.add("ACT", act_mscp)
        c_msk.val += 1
    MSKF = c_msk.val

    sav_hist = {}
    y_hist = {}
    hb_hist = {}
    for th in range(0 if NOTAIL else MKTH):
        OUT_prev = c_out.val

        def sy_x1ld(sy, th=th, OUT_prev=OUT_prev):
            if th > 0:
                sy.wait_ge(c_out.sem, OUT_prev)
            for dq in range(4):
                src = bass.AP(x1d, dq * 4 * 128 * T + th * THW,
                              [[T, 128], [128 * T, 4], [1, THW]])
                dst = bass.AP(x1th, dq * 4 * THW,
                              [[DC * THW, 128], [THW, 4], [1, THW]])
                sy.dma_start(dst, src).then_inc(c_x1ld.sem, 16)
        planB.add("SYNC", sy_x1ld)
        c_x1ld.val += 64
        X1LD = c_x1ld.val

        for e in range(MKNE):
            if e < NE:
                for ts in range(TS):
                    bank = pbk[6]
                    gen0 = bank.fre.val

                    def pe_msb(t, e=e, th=th, ts=ts, bank=bank, gen0=gen0,
                               MSKF=MSKF):
                        t.wait_ge(c_msk.sem, MSKF)
                        if gen0 > 0:
                            t.wait_ge(bank.fre.sem, gen0)
                        t.matmul(bank.t[:, :],
                                 selb[:, e * 128:(e + 1) * 128],
                                 mscP[:, th * THW + ts * TSW:
                                      th * THW + (ts + 1) * TSW],
                                 start=True, stop=True).then_inc(bank.rdy.sem, 1)
                    planB.add("PE", pe_msb)
                    bank.rdy.val += 1
                    bank.fre.val += 1
                    RDY = bank.rdy.val
                    MSBW = max(0, c_msbf.val - 1)

                    FREV = bank.fre.val

                    def act_msb(s, e=e, ts=ts, bank=bank, RDY=RDY, MSBW=MSBW,
                                FREV=FREV):
                        s.wait_ge(bank.rdy.sem, RDY)
                        if MSBW > 0:
                            s.wait_ge(c_msbf.sem, MSBW)
                        s.copy(msb[:, (e % 2) * THW + ts * TSW:
                                   (e % 2) * THW + (ts + 1) * TSW],
                               bank.t[:, :]).then_inc(bank.fre.sem, 1)
                        s.wait_ge(bank.fre.sem, FREV)
                        s.nop().then_inc(c_msb.sem, 1)
                    planB.add("ACT", act_msb)
                    c_msb.val += 1
            MSB = c_msb.val

            for fc in range(FC):
                hG = ringB.load(gW.ap()[e, fc, :, :])
                hU = ringB.load(uW.ap()[e, fc, :, :])
                for ts in range(TS):
                    bankG = pbk[ts]
                    bankU = pbk[2 + ts]
                    genG = bankG.fre.val
                    genU = bankU.fre.val

                    RG = bankG.rdy.val + 1
                    RU = bankU.rdy.val + 1

                    def pe_gu(t, e=e, fc=fc, ts=ts, hG=hG, hU=hU,
                              bankG=bankG, bankU=bankU, genG=genG, genU=genU,
                              X1LD=X1LD, RG=RG, RU=RU):
                        if fc == 0 and ts == 0:
                            t.wait_ge(c_x1ld.sem, X1LD)
                        if ts == 0:
                            hG.wait(t)
                            hU.wait(t)
                        if genG > 0:
                            t.wait_ge(bankG.fre.sem, genG)
                        ins = None
                        for dc in range(DC):
                            ins = t.matmul(
                                bankG.t[:, :],
                                hG.ap[:, dc * 128:(dc + 1) * 128],
                                x1th[:, dc * THW + ts * TSW:
                                     dc * THW + (ts + 1) * TSW],
                                start=(dc == 0), stop=(dc == DC - 1))
                        ins.then_inc(bankG.rdy.sem, 1)
                        if ts == TS - 1:
                            t.wait_ge(bankG.rdy.sem, RG)
                            hG.free_on(t.nop())
                        if genU > 0:
                            t.wait_ge(bankU.fre.sem, genU)
                        for dc in range(DC):
                            ins = t.matmul(
                                bankU.t[:, :],
                                hU.ap[:, dc * 128:(dc + 1) * 128],
                                x1th[:, dc * THW + ts * TSW:
                                     dc * THW + (ts + 1) * TSW],
                                start=(dc == 0), stop=(dc == DC - 1))
                        ins.then_inc(bankU.rdy.sem, 1)
                        if ts == TS - 1:
                            t.wait_ge(bankU.rdy.sem, RU)
                            hU.free_on(t.nop())
                    planB.add("PE", pe_gu)
                    bankG.rdy.val += 1
                    bankG.fre.val += 1
                    bankU.rdy.val += 1
                    bankU.fre.val += 1
                    SAVW = sav_hist.get(ts, 0)
                    HBW = hb_hist.get(ts, 0)

                    if e < NE:
                        def dve_ab(v, e=e, ts=ts, bankG=bankG, bankU=bankU,
                                   RG=RG, RU=RU, MSB=MSB, SAVW=SAVW, fc=fc,
                                   HBW=HBW):
                            v.wait_ge(bankG.rdy.sem, RG)
                            if fc == 0 and ts == 0:
                                v.wait_ge(c_msb.sem, MSB)
                            if SAVW > 0:
                                v.wait_ge(c_silu.sem, SAVW)
                            if HBW > 0:
                                v.wait_ge(c_hb.sem, HBW)
                            v.tensor_tensor(
                                tmpA[ts][:, :], bankG.t[:, :],
                                msb[:, (e % 2) * THW + ts * TSW:
                                    (e % 2) * THW + (ts + 1) * TSW],
                                ALU.mult).then_inc(bankG.fre.sem, 1)
                            v.engine_nop().then_inc(c_silu.sem, 1)
                            v.wait_ge(bankU.rdy.sem, RU)
                            v.tensor_tensor(
                                tmpB[ts][:, :], bankU.t[:, :],
                                msb[:, (e % 2) * THW + ts * TSW:
                                    (e % 2) * THW + (ts + 1) * TSW],
                                ALU.mult).then_inc(bankU.fre.sem, 1)
                            v.engine_nop().then_inc(c_silu.sem, 1)
                        planB.add("DVE", dve_ab)
                        c_silu.val += 2
                    else:
                        FG = bankG.fre.val
                        FU = bankU.fre.val

                        def act_ab(s, bankG=bankG, bankU=bankU, RG=RG, RU=RU,
                                   ts=ts, SAVW=SAVW, FG=FG, FU=FU, HBW=HBW):
                            s.wait_ge(bankG.rdy.sem, RG)
                            if SAVW > 0:
                                s.wait_ge(c_silu.sem, SAVW)
                            if HBW > 0:
                                s.wait_ge(c_hb.sem, HBW)
                            s.copy(tmpA[ts][:, :], bankG.t[:, :]).then_inc(
                                bankG.fre.sem, 1)
                            s.wait_ge(bankG.fre.sem, FG)
                            s.nop().then_inc(c_silu.sem, 1)
                            s.wait_ge(bankU.rdy.sem, RU)
                            s.copy(tmpB[ts][:, :], bankU.t[:, :]).then_inc(
                                bankU.fre.sem, 1)
                            s.wait_ge(bankU.fre.sem, FU)
                            s.nop().then_inc(c_silu.sem, 1)
                        planB.add("ACT", act_ab)
                        c_silu.val += 2
                    AB = c_silu.val

                    def act_silu(s, ts=ts, AB=AB):
                        s.wait_ge(c_silu.sem, AB)
                        s.activation(tmpSA[ts][:, :], tmpA[ts][:, :],
                                     AF.Silu).then_inc(c_silu.sem, 1)
                    planB.add("ACT", act_silu)
                    c_silu.val += 1
                    SAV = c_silu.val
                    sav_hist[ts] = SAV

                    def dve_h(v, fc=fc, ts=ts, SAV=SAV):
                        v.wait_ge(c_silu.sem, SAV)
                        v.engine_nop()
                        v.tensor_tensor(h_buf[ts][:, fc * TSW:(fc + 1) * TSW],
                                        tmpSA[ts][:, :], tmpB[ts][:, :],
                                        ALU.mult).then_inc(c_hb.sem, 1)
                    planB.add("DVE", dve_h)
                    c_hb.val += 1
                    hb_hist[ts] = c_hb.val
            HB = c_hb.val

            if e < NE:
                def dve_msbf(v):
                    v.engine_nop().then_inc(c_msbf.sem, 1)
                planB.add("DVE", dve_msbf)
                c_msbf.val += 1

            for dc in range(DC):
                hD = ringB.load(dW.ap()[e, dc, :, :])
                for ts in range(TS):
                    bank = pbk[4 + ts]
                    gen0 = bank.fre.val

                    RDY = bank.rdy.val + 1

                    def pe_down(t, e=e, dc=dc, ts=ts, hD=hD, bank=bank,
                                gen0=gen0, HB=HB, RDY=RDY):
                        if ts == 0:
                            hD.wait(t)
                        if dc == 0 and ts == 0:
                            t.wait_ge(c_hb.sem, HB)
                        if gen0 > 0:
                            t.wait_ge(bank.fre.sem, gen0)
                        ins = None
                        for fc in range(FC):
                            ins = t.matmul(
                                bank.t[:, :],
                                hD.ap[:, fc * 128:(fc + 1) * 128],
                                h_buf[ts][:, fc * TSW:(fc + 1) * TSW],
                                start=(fc == 0), stop=(fc == FC - 1))
                        ins.then_inc(bank.rdy.sem, 1)
                        if ts == TS - 1:
                            t.wait_ge(bank.rdy.sem, RDY)
                            hD.free_on(t.nop())
                    planB.add("PE", pe_down)
                    bank.rdy.val += 1
                    bank.fre.val += 1

                    YPREV = y_hist.get((dc, ts), 0)
                    y_hist[(dc, ts)] = c_yw.val + 1

                    def dve_y(v, e=e, dc=dc, ts=ts, bank=bank, RDY=RDY,
                              YPREV=YPREV):
                        v.wait_ge(bank.rdy.sem, RDY)
                        if YPREV > 0:
                            v.wait_ge(c_yw.sem, YPREV)
                        dst = y_acc[:, dc * THW + ts * TSW:
                                    dc * THW + (ts + 1) * TSW]
                        if e == 0:
                            v.tensor_scalar(dst, bank.t[:, :], 0.0, None,
                                            ALU.add)
                        else:
                            v.tensor_tensor(dst, dst, bank.t[:, :], ALU.add)
                        v.engine_nop().then_inc(bank.fre.sem, 1)
                        v.engine_nop().then_inc(c_yw.sem, 1)
                    planB.add("DVE", dve_y)
                    c_yw.val += 1
        YW = c_yw.val

        gf = [pbk[i].fre.val for i in range(7)]
        l2w_base = c_l2w.val
        sq2w_base = c_sq2w.val
        st2r_base = c_st2r.val
        for dc in range(DC if TAILCUT >= 2 else 0):
            def dve_l2(v, dc=dc, YW=YW, st2r_base=st2r_base):
                if dc == 0:
                    v.wait_ge(c_yw.sem, YW)
                if dc >= 2:
                    v.wait_ge(c_st2r.sem, st2r_base + dc - 1)
                v.tensor_tensor(l2x[dc % 2][:, :],
                                x1th[:, dc * THW:(dc + 1) * THW],
                                y_acc[:, dc * THW:(dc + 1) * THW],
                                ALU.add).then_inc(c_l2w.sem, 1)
            planB.add("DVE", dve_l2)
            c_l2w.val += 1
            L2 = c_l2w.val

            def act_sq2(s, dc=dc, L2=L2, st2r_base=st2r_base):
                s.wait_ge(c_l2w.sem, L2)
                if dc >= 2:
                    s.wait_ge(c_st2r.sem, st2r_base + dc - 1)
                s.square(sq2[dc % 2][:, :], l2x[dc % 2][:, :]).then_inc(
                    c_sq2w.sem, 1)
            planB.add("ACT", act_sq2)
            c_sq2w.val += 1
            L3 = c_sq2w.val

            def pe_st2(t, dc=dc, L2=L2, L3=L3, gf=gf, st2r_base=st2r_base):
                t.wait_ge(c_l2w.sem, L2)
                if dc == 0:
                    for i in (2, 3, 4, 5):
                        if gf[i] > 0:
                            t.wait_ge(pbk[i].fre.sem, gf[i])
                for half in range(2):
                    t.matmul(pbk[2 + half].t[0:1, :], onesc[:, :],
                             l2x[dc % 2][:, half * 512:(half + 1) * 512],
                             start=(dc == 0), stop=(dc == 15))
                t.wait_ge(c_sq2w.sem, L3)
                ins = None
                for half in range(2):
                    ins = t.matmul(pbk[4 + half].t[0:1, :], onesc[:, :],
                                   sq2[dc % 2][:, half * 512:(half + 1) * 512],
                                   start=(dc == 0), stop=(dc == 15))
                ins.then_inc(c_st2r.sem, 1)
                if dc == 15:
                    t.wait_ge(c_st2r.sem, st2r_base + 16)
                    t.nop().then_inc(pbk[5].rdy.sem, 1)
            planB.add("PE", pe_st2)
            c_st2r.val += 1
        for i in (2, 3, 4):
            pbk[i].fre.val += 1
        pbk[5].rdy.val += 1
        pbk[5].fre.val += 1
        S2 = pbk[5].rdy.val

        if TAILCUT < 3:
            continue

        vs0 = mkvs(3)

        def dve_ln2s(v, S2=S2, vs0=vs0):
            v.wait_ge(pbk[5].rdy.sem, S2)
            v.tensor_scalar(rsl(0), pbk[2].t[0:1, :], RN, None,
                            ALU.mult).then_inc(pbk[2].fre.sem, 1)
            v.tensor_scalar(rsl(1), pbk[3].t[0:1, :], RN, None,
                            ALU.mult).then_inc(pbk[3].fre.sem, 1)
            v.tensor_scalar(rsl(2), pbk[4].t[0:1, :], RN, None,
                            ALU.mult).then_inc(pbk[4].fre.sem, 1)
            v.tensor_scalar(rsl(3), pbk[5].t[0:1, :], RN, None,
                            ALU.mult).then_inc(pbk[5].fre.sem, 1)
            VS(v, vs0, 0)
            v.tensor_tensor(rsl(4, 2), rsl(0, 2), rsl(0, 2), ALU.mult)
            VS(v, vs0, 1)
            v.tensor_tensor(rsl(6, 2), rsl(2, 2), rsl(4, 2), ALU.subtract)
            VS(v, vs0, 2)
            v.tensor_scalar(rsl(6, 2), rsl(6, 2), EPS, None,
                            ALU.add).then_inc(c_ln2.sem, 1)
        planB.add("DVE", dve_ln2s)
        c_ln2.val += 1
        L4 = c_ln2.val

        def act_ln2q(s, L4=L4):
            s.wait_ge(c_ln2.sem, L4)
            s.activation(rsl(8, 2), rsl(6, 2), AF.Sqrt).then_inc(c_ln2.sem, 1)
        planB.add("ACT", act_ln2q)
        c_ln2.val += 1
        L5 = c_ln2.val

        vs0 = mkvs(2)

        def dve_ln2r(v, L5=L5, vs0=vs0):
            v.wait_ge(c_ln2.sem, L5)
            v.reciprocal(rsl(10, 2), rsl(8, 2))
            VS(v, vs0, 0)
            v.tensor_tensor(rsl(12, 2), rsl(0, 2), rsl(10, 2), ALU.mult)
            VS(v, vs0, 1)
            v.tensor_copy(eqs[0:1, 0:THW], rsl(10, 2))
            v.tensor_copy(eqs[0:1, THW:2 * THW], rsl(12, 2))
            v.engine_nop().then_inc(c_ln2.sem, 1)
        planB.add("DVE", dve_ln2r)
        c_ln2.val += 1
        L6 = c_ln2.val

        if TAILCUT < 4:
            continue
        for j in range(2):
            for half in range(2):
                bank = pbk[2 * j + half]
                gen0 = bank.fre.val

                def pe_bc2(t, j=j, half=half, bank=bank, gen0=gen0, L6=L6):
                    t.wait_ge(c_ln2.sem, L6)
                    if gen0 > 0:
                        t.wait_ge(bank.fre.sem, gen0)
                    t.matmul(bank.t[:, :], onesr[:, :],
                             eqs[0:1, j * THW + half * 512:
                                 j * THW + (half + 1) * 512],
                             start=True, stop=True).then_inc(bank.rdy.sem, 1)
                planB.add("PE", pe_bc2)
                bank.rdy.val += 1
                bank.fre.val += 1
                RDY = bank.rdy.val

                FREV = bank.fre.val

                def act_bc2(s, j=j, half=half, bank=bank, RDY=RDY, FREV=FREV):
                    s.wait_ge(bank.rdy.sem, RDY)
                    s.copy(rb2[:, j * THW + half * 512:
                               j * THW + (half + 1) * 512],
                           bank.t[:, :]).then_inc(bank.fre.sem, 1)
                    s.wait_ge(bank.fre.sem, FREV)
                    s.nop().then_inc(c_ln2.sem, 1)
                planB.add("ACT", act_bc2)
                c_ln2.val += 1
        L7 = c_ln2.val

        if TAILCUT < 5:
            continue
        ost_base = c_ost.val
        for dc in range(DC):
            vs0 = mkvs(2)
            OSTP = c_ostp[dc % 2].val

            def dve_out(v, dc=dc, th=th, L7=L7, vs0=vs0, OSTP=OSTP):
                if dc == 0:
                    v.wait_ge(c_ln2.sem, L7)
                if OSTP > 0:
                    v.wait_ge(c_ostp[dc % 2].sem, OSTP)
                v.tensor_tensor(l2x[dc % 2][:, :],
                                x1th[:, dc * THW:(dc + 1) * THW],
                                y_acc[:, dc * THW:(dc + 1) * THW], ALU.add)
                VS(v, vs0, 0)
                v.tensor_tensor(sq2[dc % 2][:, :], l2x[dc % 2][:, :],
                                rb2[:, 0:THW], ALU.mult)
                VS(v, vs0, 1)
                v.tensor_tensor(ostg[dc % 2][:, :], sq2[dc % 2][:, :],
                                rb2[:, THW:2 * THW],
                                ALU.subtract).then_inc(c_out.sem, 1)
            planB.add("DVE", dve_out)
            c_out.val += 1
            OUTV = c_out.val

            def sy_out(sy, dc=dc, th=th, OUTV=OUTV):
                sy.wait_ge(c_out.sem, OUTV)
                sy.dma_start(
                    out_d.ap()[:, dc * T + th * THW: dc * T + (th + 1) * THW],
                    ostg[dc % 2][:, :]).then_inc(c_ostp[dc % 2].sem, 16)
            planB.add("SYNC", sy_out)
            c_ostp[dc % 2].val += 16

    OST0 = c_ostp[0].val
    OST1 = c_ostp[1].val
    MSKF2 = c_msk.val

    def sy_final(sy):
        sy.wait_ge(c_ostp[0].sem, OST0)
        sy.wait_ge(c_ostp[1].sem, OST1)
        sy.wait_ge(c_msk.sem, MSKF2)
    planB.add("SYNC", sy_final)

    with nc.Block("phaseA") as blk:
        @blk.tensor
        def _(t):
            planA.emit("PE", t)

        @blk.vector
        def _(v):
            planA.emit("DVE", v)

        @blk.scalar
        def _(s):
            planA.emit("ACT", s)

        @blk.sync
        def _(sy):
            planA.emit("SYNC", sy)

        @blk.gpsimd
        def _(g):
            planA.emit("POOL", g)

    with nc.Block("phaseB") as blk:
        @blk.tensor
        def _(t):
            planB.emit("PE", t)

        @blk.vector
        def _(v):
            planB.emit("DVE", v)

        @blk.scalar
        def _(s):
            planB.emit("ACT", s)

        @blk.sync
        def _(sy):
            planB.emit("SYNC", sy)

        @blk.gpsimd
        def _(g):
            planB.emit("POOL", g)

    scopeB.close()
    nc.ctx.close()
    return nc



def _get_nc():
    if "nc" not in _CACHE:
        _CACHE["nc"] = _build()
    return _CACHE["nc"]


def kernel(L0_attn_x, L0_attn_wq, L0_attn_wk, L0_attn_wv, L0_attn_wo,
           L0_ffn_router_w, L0_ffn_gate_w, L0_ffn_up_w, L0_ffn_down_w,
           L0_ffn_shared_gate_w, L0_ffn_shared_up_w, L0_ffn_shared_down_w):
    x = np.ascontiguousarray(np.asarray(L0_attn_x, np.float32))
    wq = np.asarray(L0_attn_wq, np.float32)
    wk = np.asarray(L0_attn_wk, np.float32)
    wv = np.asarray(L0_attn_wv, np.float32)
    wo = np.asarray(L0_attn_wo, np.float32)
    rw = np.asarray(L0_ffn_router_w, np.float32)
    gw = np.asarray(L0_ffn_gate_w, np.float32)
    uw = np.asarray(L0_ffn_up_w, np.float32)
    dw = np.asarray(L0_ffn_down_w, np.float32)
    sgw = np.asarray(L0_ffn_shared_gate_w, np.float32)
    suw = np.asarray(L0_ffn_shared_up_w, np.float32)
    sdw = np.asarray(L0_ffn_shared_down_w, np.float32)

    tabs = _host_tables()
    shared = {
        "wqT": _swz_lhsT(wq.T), "wkT": _swz_lhsT(wk.T), "woT": _swz_lhsT(wo.T),
        "wvR": np.ascontiguousarray(
            np.ascontiguousarray(wv.T).reshape(4, 4, 128, 512).transpose(
                0, 2, 1, 3).reshape(4, 128, 2048)),
        "rtr": np.ascontiguousarray(
            rw.reshape(16, 128, 8).transpose(1, 0, 2).reshape(128, 128)),
        "gW": np.stack([_swz_lhsT(gw[e]) for e in range(NE)] + [_swz_lhsT(sgw)]),
        "uW": np.stack([_swz_lhsT(uw[e]) for e in range(NE)] + [_swz_lhsT(suw)]),
        "dW": np.stack([_swz_lhsT(dw[e]) for e in range(NE)] + [_swz_lhsT(sdw)]),
        "diagTd": tabs["diagT"], "qdecbd": tabs["qdecb"], "kdecd": tabs["kdec"],
        "identd": tabs["ident"], "onescd": tabs["onesc"],
        "onesrd": tabs["onesr"], "onehd": tabs["oneh"],
        "onehrd": tabs["onehr"], "selbd": tabs["selb"],
    }

    in_maps = []
    zero_tab = np.zeros((128, T), BF)
    zero_x = np.zeros((NBLK, 128, DC * TB), np.float32)
    for c in range(8):
        b, half = c // 2, c % 2
        xt = np.ascontiguousarray(x[b, half * T:(half + 1) * T, :].T)
        m = dict(shared)
        m["xb"] = _swz_x(xt)
        cos_o_t, sin_o_t = _rope_tables(half * T + np.arange(T))
        m["cosb"], m["sinb"] = cos_o_t, sin_o_t
        if half == 1:
            xtp = np.ascontiguousarray(x[b, 0:T, :].T)
            m["xp"] = _swz_x(xtp)
            cos_p_t, sin_p_t = _rope_tables(np.arange(T))
            m["cosp"], m["sinp"] = cos_p_t, sin_p_t
        else:
            m["xp"] = zero_x
            m["cosp"], m["sinp"] = zero_tab, zero_tab
        in_maps.append(m)

    nc = _get_nc()
    res = run_bass_kernel_spmd(nc, in_maps, core_ids=list(range(8)))

    out = np.empty((B, S, D), np.float32)
    for c in range(8):
        r = np.asarray(res.results[c]["out"], np.float32)
        blkm = r.reshape(128, DC, T).transpose(2, 1, 0).reshape(T, D)
        out[c // 2, (c % 2) * T:(c % 2 + 1) * T, :] = blkm
    return out


# version 03
# speedup vs baseline: 1.0043x; 1.0043x over previous
import os
import contextlib
import numpy as np
import ml_dtypes
import concourse.bass as bass
import concourse.mybir as mybir
from concourse.bass_utils import run_bass_kernel_spmd

F32 = mybir.dt.float32
BF16 = mybir.dt.bfloat16
AF = mybir.ActivationFunctionType
ALU = mybir.AluOpType
BF = ml_dtypes.bfloat16

B, S, D = 4, 4096, 2048
T = 2048
NBLK = 4
TB = 512
DC = 16
HQ, HKV, DH = 16, 4, 128
CH = 64
CPB = TB // CH
NE = 8
FC = 16
TH = 2
THW = 1024
TS = 2
TSW = 512
EPS = 1e-5
RN = 1.0 / 2048.0

DEBUG_X1 = bool(os.environ.get("MK_DEBUG_X1"))
DEBUG2 = bool(os.environ.get("MK_DEBUG2"))
SKIP_B = bool(os.environ.get("MK_SKIP_B"))
NO_QHAT = bool(os.environ.get("MK_NO_QHAT"))
NO_LGT = bool(os.environ.get("MK_NO_LGT"))
MKNE = int(os.environ.get("MK_NE", str(NE + 1)))
NOMASK = bool(os.environ.get("MK_NOMASK"))
NOTAIL = bool(os.environ.get("MK_NOTAIL"))
TAILCUT = int(os.environ.get("MK_TAILCUT", "5"))
MKTH = int(os.environ.get("MK_TH", str(TH)))
NBI = int(os.environ.get("MK_NBI", "8"))



def _swz_lhsT(w):
    w = np.ascontiguousarray(w, np.float32)
    K, M = w.shape
    KC, MC = K // 128, M // 128
    return np.ascontiguousarray(
        w.reshape(KC, 128, MC, 128).transpose(2, 1, 0, 3).reshape(MC, 128, KC * 128))


def _swz_x(xt):
    return np.ascontiguousarray(
        xt.reshape(DC, 128, NBLK, TB).transpose(2, 1, 0, 3).reshape(NBLK, 128, DC * TB))


def _lam():
    return np.exp2(-0.5 * (np.arange(HQ, dtype=np.float64) + 1))


def _host_tables():
    t = {}
    lam = _lam()
    r = np.arange(CH, dtype=np.float64)
    diagT = np.zeros((64, HQ * 64), np.float32)
    for h in range(HQ):
        dm = np.exp(-lam[h] * (r[None, :] - r[:, None]))
        diagT[:, h * 64:(h + 1) * 64] = np.where(
            r[None, :] >= r[:, None], dm, 0.0).astype(np.float32)
    t["diagT"] = diagT
    qdecb = np.zeros((128, HQ * 64), np.float32)
    for h in range(HQ):
        qdecb[:, h * 64:(h + 1) * 64] = np.exp(-lam[h] * (r + 1.0))[None, :]
    t["qdecb"] = qdecb
    kdec = np.zeros((64, 16), np.float32)
    for h in range(HQ):
        kdec[:, h] = np.exp(-lam[h] * (CH - 1.0 - r))
    t["kdec"] = kdec
    t["ident"] = np.eye(128, dtype=BF)
    t["onesc"] = np.ones((128, 1), BF)
    t["onesr"] = np.ones((1, 128), BF)
    t["oneh"] = np.eye(8, dtype=np.float32)
    onehr = np.zeros((1, 64), BF)
    for e in range(8):
        onehr[0, e * 8 + e] = 1.0
    t["onehr"] = onehr
    selb = np.zeros((8, 8 * 128), BF)
    for e in range(8):
        selb[e, e * 128:(e + 1) * 128] = 1.0
    t["selb"] = selb
    return t


_BLKD = [float(np.exp(-_lam()[h] * CH)) for h in range(HQ)]


def _rope_tables(pos):
    inv = 10000.0 ** (-(np.arange(64, dtype=np.float64)) / 64.0)
    f = pos[None, :].astype(np.float64) * inv[:, None]
    cos = np.cos(f).astype(np.float32)
    sin = np.sin(f).astype(np.float32)
    return (np.concatenate([cos, cos], 0).astype(BF),
            np.concatenate([-sin, sin], 0).astype(BF))



class Plan:
    def __init__(self):
        self.items = []

    def add(self, eng, fn):
        self.items.append((eng, fn))

    def emit(self, name, eng):
        for e, fn in self.items:
            if e == name:
                fn(eng)


class Ctr:
    def __init__(self, nc, name):
        self.sem = nc.ctx.enter_context(nc.semaphore(name))
        self.val = 0


class Bank:
    def __init__(self, nc, name, dtype=F32, width=512):
        self.t = nc.ctx.enter_context(nc.psum_tensor(name, [128, width], dtype))
        self.rdy = Ctr(nc, name + "_r")
        self.fre = Ctr(nc, name + "_f")


class Ring:

    def __init__(self, nc, plan, stack, name, nslots, width, dtype=BF16):
        self.plan = plan
        self.nslots = nslots
        self.width = width
        self.t = stack.enter_context(nc.sbuf_tensor(name, [128, nslots * width], dtype))
        self.wld = [stack.enter_context(nc.semaphore(f"{name}_l{i}")) for i in range(nslots)]
        self.wfr = [stack.enter_context(nc.semaphore(f"{name}_f{i}")) for i in range(nslots)]
        self.n = 0
        self.fr_val = [0] * nslots

    def load(self, src_ap, w=None):
        slot = self.n % self.nslots
        gen = self.n // self.nslots
        self.n += 1
        need_free = self.fr_val[slot]
        wld, wfr, t, width = self.wld[slot], self.wfr[slot], self.t, self.width
        w = width if w is None else w

        def pool_emit(g, slot=slot, gen=gen, src_ap=src_ap, need_free=need_free,
                      wld=wld, wfr=wfr, t=t, width=width, w=w):
            if gen > 0:
                g.wait_ge(wfr, need_free)
            g.dma_start(t[:, slot * width:slot * width + w], src_ap).then_inc(wld, 16)
        self.plan.add("POOL", pool_emit)
        self.fr_val[slot] += 1
        return RingHandle(self, slot, gen, w)


class RingHandle:
    def __init__(self, ring, slot, gen, w):
        self.ring, self.slot, self.gen, self.w = ring, slot, gen, w

    @property
    def ap(self):
        r = self.ring
        return r.t[:, self.slot * r.width:self.slot * r.width + self.w]

    def wait(self, eng):
        eng.wait_ge(self.ring.wld[self.slot], 16 * (self.gen + 1))

    def free_on(self, instr):
        instr.then_inc(self.ring.wfr[self.slot], 1)



_CACHE = {}


def _build():
    nc = bass.Bass("TRN2", target_bir_lowering=False)
    nc.ctx = contextlib.ExitStack()

    xb = nc.dram_tensor("xb", [NBLK, 128, DC * TB], F32, kind="ExternalInput")
    xp = nc.dram_tensor("xp", [NBLK, 128, DC * TB], F32, kind="ExternalInput")
    wqT = nc.dram_tensor("wqT", [16, 128, 2048], F32, kind="ExternalInput")
    wkT = nc.dram_tensor("wkT", [4, 128, 2048], F32, kind="ExternalInput")
    wvR = nc.dram_tensor("wvR", [4, 128, 2048], F32, kind="ExternalInput")
    woT = nc.dram_tensor("woT", [16, 128, 2048], F32, kind="ExternalInput")
    rtr = nc.dram_tensor("rtr", [128, 128], F32, kind="ExternalInput")
    gW = nc.dram_tensor("gW", [NE + 1, 16, 128, 2048], F32, kind="ExternalInput")
    uW = nc.dram_tensor("uW", [NE + 1, 16, 128, 2048], F32, kind="ExternalInput")
    dW = nc.dram_tensor("dW", [NE + 1, 16, 128, 2048], F32, kind="ExternalInput")
    cosb_d = nc.dram_tensor("cosb", [128, T], BF16, kind="ExternalInput")
    sinb_d = nc.dram_tensor("sinb", [128, T], BF16, kind="ExternalInput")
    cosp_d = nc.dram_tensor("cosp", [128, T], BF16, kind="ExternalInput")
    sinp_d = nc.dram_tensor("sinp", [128, T], BF16, kind="ExternalInput")
    diagT_d = nc.dram_tensor("diagTd", [64, HQ * 64], F32, kind="ExternalInput")
    qdecb_d = nc.dram_tensor("qdecbd", [128, HQ * 64], F32, kind="ExternalInput")
    kdec_d = nc.dram_tensor("kdecd", [64, 16], F32, kind="ExternalInput")
    ident_d = nc.dram_tensor("identd", [128, 128], BF16, kind="ExternalInput")
    onesc_d = nc.dram_tensor("onescd", [128, 1], BF16, kind="ExternalInput")
    onesr_d = nc.dram_tensor("onesrd", [1, 128], BF16, kind="ExternalInput")
    oneh_d = nc.dram_tensor("onehd", [8, 8], F32, kind="ExternalInput")
    onehr_d = nc.dram_tensor("onehrd", [1, 64], BF16, kind="ExternalInput")
    selb_d = nc.dram_tensor("selbd", [8, 8 * 128], BF16, kind="ExternalInput")

    out_d = nc.dram_tensor("out", [128, DC * T], F32, kind="ExternalOutput")
    x1d = nc.dram_tensor("x1d", [DC, 128, T], BF16,
                         kind="ExternalOutput" if DEBUG_X1 else "Internal")

    planA = Plan()
    planB = Plan()

    ident = nc.ctx.enter_context(nc.sbuf_tensor("identt", [128, 128], BF16))
    onesc = nc.ctx.enter_context(nc.sbuf_tensor("onescc", [128, 1], BF16))
    onesr = nc.ctx.enter_context(nc.sbuf_tensor("onesrr", [1, 128], BF16))
    oneh = nc.ctx.enter_context(nc.sbuf_tensor("onehh", [8, 8], F32))
    onehr = nc.ctx.enter_context(nc.sbuf_tensor("onehr", [1, 64], BF16))
    selb = nc.ctx.enter_context(nc.sbuf_tensor("selb", [8, 8 * 128], BF16))
    logit_sb = nc.ctx.enter_context(nc.sbuf_tensor("logit_sb", [8, T], F32))
    mscP = nc.ctx.enter_context(nc.sbuf_tensor("mscP", [8, T], BF16))

    pbk = [Bank(nc, f"pb{i}") for i in range(7)]
    pbt = Bank(nc, "pbt", dtype=BF16, width=1024)

    c_kvs = Ctr(nc, "c_kvs")
    c_khat = Ctr(nc, "c_khat")
    c_ssc = Ctr(nc, "c_ssc")
    c_rope = Ctr(nc, "c_rope")
    c_krope = Ctr(nc, "c_krope")
    c_vtm = Ctr(nc, "c_vtm")
    c_oev = Ctr(nc, "c_oev")
    c_och = Ctr(nc, "c_och")
    c_dch = Ctr(nc, "c_dch")
    c_pre = Ctr(nc, "c_pre")
    c_sq = Ctr(nc, "c_sq")
    c_ln = Ctr(nc, "c_ln")
    c_x1a = Ctr(nc, "c_x1a")
    c_x1st = Ctr(nc, "c_x1st")
    c_msk = Ctr(nc, "c_msk")
    c_hb = Ctr(nc, "c_hb")
    c_silu = Ctr(nc, "c_silu")
    c_msb = Ctr(nc, "c_msb")
    c_msbf = Ctr(nc, "c_msbf")
    c_yw = Ctr(nc, "c_yw")
    c_ln2 = Ctr(nc, "c_ln2")
    c_out = Ctr(nc, "c_out")
    c_ost = Ctr(nc, "c_ost")
    c_x1ld = Ctr(nc, "c_x1ld")
    c_cst = Ctr(nc, "c_cst")
    c_vsy = Ctr(nc, "c_vsy")
    c_sqr = Ctr(nc, "c_sqr")
    c_ostp = [Ctr(nc, "c_ostp0"), Ctr(nc, "c_ostp1")]
    c_l2w = Ctr(nc, "c_l2w")
    c_sq2w = Ctr(nc, "c_sq2w")
    c_st2r = Ctr(nc, "c_st2r")

    def mkvs(n):
        base = c_vsy.val
        c_vsy.val += n
        return base

    def VS(v, base, i):
        v.engine_nop().then_inc(c_vsy.sem, 1)
        v.wait_ge(c_vsy.sem, base + i + 1)

    scopeA = contextlib.ExitStack()
    sbA = lambda n, s, d=BF16: scopeA.enter_context(nc.sbuf_tensor(n, s, d))
    x_sl = [sbA(f"x_sl{i}", [128, DC * TB]) for i in range(2)]
    x_ld = [Ctr(nc, f"x_ld{i}") for i in range(2)]
    x_fr = [Ctr(nc, f"x_fr{i}") for i in range(2)]
    ring = Ring(nc, planA, scopeA, "wring", 4, 2048)
    q_fm = sbA("q_fm", [128, HQ * TB])
    qhat = sbA("qhat", [128, HQ * TB])
    k_fm = sbA("k_fm", [128, HKV * TB])
    v_tm = sbA("v_tm", [64, CPB * 512])
    khat = sbA("khat", [64, 2 * HQ * 128])
    ssc = sbA("sscb", [64, 2 * HQ * 64])
    o_fm = sbA("o_fm", [128, HQ * TB])
    kv_bf = sbA("kv_bf", [128, HQ * DH])
    pre = sbA("preb", [128, DC * TB])
    sqb = sbA("sqb", [128, 2 * TB])
    rs1 = sbA("rs1", [128, TB], F32)
    rs2 = sbA("rs2", [128, TB], F32)
    rs3 = sbA("rs3", [128, TB], F32)
    lnrow = sbA("lnrow", [1, 7 * TB], F32)
    lnbfr = sbA("lnbfr", [1, 2 * TB])
    rbb = sbA("rbb", [128, 2 * TB])
    cos_o = sbA("cos_o", [128, T])
    sin_o = sbA("sin_o", [128, T])
    cos_p = sbA("cos_p", [128, T])
    sin_p = sbA("sin_p", [128, T])
    diagT = sbA("diagTs", [64, HQ * 64], F32)
    qdecb = sbA("qdecbs", [128, HQ * 64], F32)
    kdec = sbA("kdecs", [64, 16], F32)
    x1b = pre

    def _bc8(ec):
        return bass.AP(qdecb, ec * 64, [[HQ * 64, 128], [0, 8], [1, 64]])

    def emit_const(sy):
        for dst, src in [(ident, ident_d), (onesc, onesc_d), (onesr, onesr_d),
                         (oneh, oneh_d), (onehr, onehr_d), (selb, selb_d),
                         (cos_o, cosb_d), (sin_o, sinb_d),
                         (cos_p, cosp_d), (sin_p, sinp_d), (diagT, diagT_d),
                         (qdecb, qdecb_d), (kdec, kdec_d)]:
            sy.dma_start(dst[:, :], src.ap()).then_inc(c_cst.sem, 16)
    planA.add("SYNC", emit_const)
    c_cst.val += 16 * 13
    CST = c_cst.val

    def emit_kv0(v):
        v.wait_ge(c_cst.sem, CST)
        v.memset(kv_bf[:, :], 0.0)
        v.engine_nop().then_inc(c_kvs.sem, 1)
    planA.add("DVE", emit_kv0)
    c_kvs.val += 1

    n_own_chunk = 0

    for bi in range(min(NBI, 2 * NBLK)):
        warm = bi < NBLK
        blk = bi % NBLK
        xslot = bi % 2
        xsrc = xp if warm else xb
        xs = x_sl[xslot]
        cos_t, sin_t = (cos_p, sin_p) if warm else (cos_o, sin_o)

        need_xfree = x_fr[xslot].val

        def pool_x(g, bi=bi, xsrc=xsrc, blk=blk, need_xfree=need_xfree,
                   xslot=xslot):
            if bi > 1:
                g.wait_ge(x_fr[xslot].sem, need_xfree)
            g.wait_ge(c_cst.sem, CST)
            g.dma_start(x_sl[xslot][:, :], xsrc.ap()[blk, :, :]).then_inc(
                x_ld[xslot].sem, 16)
        planA.add("POOL", pool_x)
        x_ld[xslot].val += 16
        XNEED = x_ld[xslot].val

        if not warm:
            for ec in range(HQ):
                h = ring.load(wqT.ap()[ec, :, :])
                bank = pbk[ec % 2]
                gen0 = bank.fre.val

                RDY = bank.rdy.val + 1

                def pe_q(t, h=h, bank=bank, gen0=gen0, xs=xs, XNEED=XNEED,
                         xslot=xslot, RDY=RDY):
                    h.wait(t)
                    t.wait_ge(x_ld[xslot].sem, XNEED)
                    if gen0 > 0:
                        t.wait_ge(bank.fre.sem, gen0)
                    ins = None
                    for dc in range(DC):
                        ins = t.matmul(bank.t[:, :],
                                       h.ap[:, dc * 128:(dc + 1) * 128],
                                       xs[:, dc * TB:(dc + 1) * TB],
                                       start=(dc == 0), stop=(dc == DC - 1))
                    ins.then_inc(bank.rdy.sem, 1)
                    t.wait_ge(bank.rdy.sem, RDY)
                    h.free_on(t.nop())
                planA.add("PE", pe_q)
                bank.rdy.val += 1
                bank.fre.val += 1

                vs0 = mkvs(4)

                def dve_rope(v, bank=bank, ec=ec, blk=blk, RDY=RDY,
                             cos_t=cos_t, sin_t=sin_t, vs0=vs0):
                    v.wait_ge(bank.rdy.sem, RDY)
                    VS(v, vs0, 0)
                    ps = bank.t
                    v.tensor_copy(rs1[64:128, :], ps[0:64, :])
                    v.tensor_copy(rs1[0:64, :], ps[64:128, :])
                    v.tensor_tensor(rs2[:, :], ps[:, :],
                                    cos_t[:, blk * TB:(blk + 1) * TB],
                                    ALU.mult).then_inc(bank.fre.sem, 1)
                    VS(v, vs0, 1)
                    v.tensor_tensor(rs3[:, :], rs1[:, :],
                                    sin_t[:, blk * TB:(blk + 1) * TB], ALU.mult)
                    VS(v, vs0, 2)
                    v.tensor_tensor(q_fm[:, ec * TB:(ec + 1) * TB],
                                    rs2[:, :], rs3[:, :], ALU.add)
                    VS(v, vs0, 3)
                    if NO_QHAT:
                        v.tensor_copy(qhat[:, ec * TB:(ec + 1) * TB],
                                      q_fm[:, ec * TB:(ec + 1) * TB])
                        v.engine_nop().then_inc(c_rope.sem, 1)
                    else:
                        v.tensor_tensor(qhat[:, ec * TB:(ec + 1) * TB],
                                        q_fm[:, ec * TB:(ec + 1) * TB],
                                        _bc8(ec), ALU.mult).then_inc(c_rope.sem, 1)
                planA.add("DVE", dve_rope)
                c_rope.val += 1

        for ec in range(HKV):
            h = ring.load(wkT.ap()[ec, :, :])
            bank = pbk[ec % 2]
            gen0 = bank.fre.val

            RDY = bank.rdy.val + 1

            def pe_k(t, h=h, bank=bank, gen0=gen0, xs=xs, XNEED=XNEED,
                     xslot=xslot, RDY=RDY):
                h.wait(t)
                t.wait_ge(x_ld[xslot].sem, XNEED)
                if gen0 > 0:
                    t.wait_ge(bank.fre.sem, gen0)
                ins = None
                for dc in range(DC):
                    ins = t.matmul(bank.t[:, :],
                                   h.ap[:, dc * 128:(dc + 1) * 128],
                                   xs[:, dc * TB:(dc + 1) * TB],
                                   start=(dc == 0), stop=(dc == DC - 1))
                ins.then_inc(bank.rdy.sem, 1)
                t.wait_ge(bank.rdy.sem, RDY)
                h.free_on(t.nop())
            planA.add("PE", pe_k)
            bank.rdy.val += 1
            bank.fre.val += 1

            vs0 = mkvs(3)

            def dve_krope(v, bank=bank, ec=ec, blk=blk, RDY=RDY,
                          cos_t=cos_t, sin_t=sin_t, vs0=vs0):
                v.wait_ge(bank.rdy.sem, RDY)
                VS(v, vs0, 0)
                ps = bank.t
                v.tensor_copy(rs1[64:128, :], ps[0:64, :])
                v.tensor_copy(rs1[0:64, :], ps[64:128, :])
                v.tensor_tensor(rs2[:, :], ps[:, :],
                                cos_t[:, blk * TB:(blk + 1) * TB],
                                ALU.mult).then_inc(bank.fre.sem, 1)
                VS(v, vs0, 1)
                v.tensor_tensor(rs3[:, :], rs1[:, :],
                                sin_t[:, blk * TB:(blk + 1) * TB], ALU.mult)
                VS(v, vs0, 2)
                v.tensor_tensor(k_fm[:, ec * TB:(ec + 1) * TB],
                                rs2[:, :], rs3[:, :], ALU.add).then_inc(
                    c_krope.sem, 1)
            planA.add("DVE", dve_krope)
            c_krope.val += 1

        vh = [ring.load(wvR.ap()[j, :, :]) for j in range(4)]
        for tc in range(4):
            bank = pbk[2 + tc % 2]
            gen0 = bank.fre.val

            RDY = bank.rdy.val + 1

            def pe_v(t, vh=vh, bank=bank, gen0=gen0, tc=tc, xs=xs, XNEED=XNEED,
                     xslot=xslot, RDY=RDY):
                if tc == 0:
                    for hh in vh:
                        hh.wait(t)
                    t.wait_ge(x_ld[xslot].sem, XNEED)
                if gen0 > 0:
                    t.wait_ge(bank.fre.sem, gen0)
                ins = None
                for dc in range(DC):
                    ins = t.matmul(
                        bank.t[:, :],
                        xs[:, dc * TB + tc * 128: dc * TB + (tc + 1) * 128],
                        vh[dc // 4].ap[:, (dc % 4) * 512:(dc % 4 + 1) * 512],
                        start=(dc == 0), stop=(dc == DC - 1))
                ins.then_inc(bank.rdy.sem, 1)
                if tc == 3:
                    t.wait_ge(bank.rdy.sem, RDY)
                    for hh in vh:
                        hh.free_on(t.nop())
            planA.add("PE", pe_v)
            bank.rdy.val += 1
            bank.fre.val += 1

            FREV = bank.fre.val

            def act_vev(s, bank=bank, tc=tc, RDY=RDY, FREV=FREV):
                s.wait_ge(bank.rdy.sem, RDY)
                s.copy(v_tm[:, (2 * tc) * 512:(2 * tc + 1) * 512],
                       bank.t[0:64, :])
                s.copy(v_tm[:, (2 * tc + 1) * 512:(2 * tc + 2) * 512],
                       bank.t[64:128, :]).then_inc(bank.fre.sem, 1)
                s.wait_ge(bank.fre.sem, FREV)
                s.nop().then_inc(c_vtm.sem, 1)
            planA.add("ACT", act_vev)
            c_vtm.val += 1

        if warm:
            vr_last = pbk[3].rdy.val

            def pe_xfree(t, xslot=xslot, vr_last=vr_last):
                t.wait_ge(pbk[3].rdy.sem, vr_last)
                t.nop().then_inc(x_fr[xslot].sem, 1)
            planA.add("PE", pe_xfree)
            x_fr[xslot].val += 1

        KR = c_krope.val
        VR = c_vtm.val
        for cc in range(0 if NO_LGT else CPB):
            gchunk = bi * CPB + cc
            kv_base = c_kvs.val

            genv = pbt.fre.val

            def pe_tr(t, cc=cc, genv=genv, KR=KR):
                t.wait_ge(c_krope.sem, KR)
                if genv > 0:
                    t.wait_ge(pbt.fre.sem, genv)
                ins = None
                for g in range(4):
                    ins = t.transpose(
                        pbt.t[0:64, g * 128:(g + 1) * 128],
                        k_fm[:, g * TB + cc * 64: g * TB + (cc + 1) * 64],
                        ident[:, :])
                ins.then_inc(pbt.rdy.sem, 1)
            planA.add("PE", pe_tr)
            pbt.rdy.val += 1
            pbt.fre.val += 1
            TRDY = pbt.rdy.val
            DCHW = max(0, gchunk - 1)

            def dve_khat(v, cc=cc, TRDY=TRDY, DCHW=DCHW):
                v.wait_ge(pbt.rdy.sem, TRDY)
                if DCHW > 0:
                    v.wait_ge(c_dch.sem, DCHW)
                ins = None
                for h in range(HQ):
                    g = h // 4
                    ins = v.tensor_scalar(
                        khat[0:64, ((cc % 2) * HQ + h) * 128:
                             ((cc % 2) * HQ + h + 1) * 128],
                        pbt.t[0:64, g * 128:(g + 1) * 128],
                        kdec[:, h:h + 1], None, ALU.mult)
                ins.then_inc(pbt.fre.sem, 1)
                v.engine_nop().then_inc(c_khat.sem, 1)
            planA.add("DVE", dve_khat)
            c_khat.val += 1
            KHAT_CC = c_khat.val

            if not warm:
                QR = c_rope.val
                for hg in range(2):
                    bank = pbk[hg]
                    gen0 = bank.fre.val

                    def pe_st(t, bank=bank, hg=hg, cc=cc, gen0=gen0, QR=QR):
                        t.wait_ge(c_rope.sem, QR)
                        if gen0 > 0:
                            t.wait_ge(bank.fre.sem, gen0)
                        ins = None
                        for hh in range(8):
                            h = hg * 8 + hh
                            g = h // 4
                            ins = t.matmul(
                                bank.t[0:64, hh * 64:(hh + 1) * 64],
                                k_fm[:, g * TB + cc * 64: g * TB + (cc + 1) * 64],
                                q_fm[:, h * TB + cc * 64: h * TB + (cc + 1) * 64],
                                start=True, stop=True)
                        ins.then_inc(bank.rdy.sem, 1)
                    planA.add("PE", pe_st)
                    bank.rdy.val += 1
                    bank.fre.val += 1
                    RDY = bank.rdy.val
                    OCH = max(0, n_own_chunk - 1)

                    def dve_diag(v, bank=bank, hg=hg, cc=cc, RDY=RDY, OCH=OCH):
                        v.wait_ge(bank.rdy.sem, RDY)
                        if OCH > 0:
                            v.wait_ge(c_och.sem, OCH)
                        ins = None
                        for hh in range(8):
                            h = hg * 8 + hh
                            ins = v.tensor_tensor(
                                ssc[0:64, ((cc % 2) * HQ + h) * 64:
                                    ((cc % 2) * HQ + h + 1) * 64],
                                bank.t[0:64, hh * 64:(hh + 1) * 64],
                                diagT[:, h * 64:(h + 1) * 64], ALU.mult)
                        ins.then_inc(bank.fre.sem, 1)
                        v.engine_nop().then_inc(c_ssc.sem, 1)
                    planA.add("DVE", dve_diag)
                    c_ssc.val += 1

                SSC = c_ssc.val
                for hg in range(2):
                    bank = pbk[2 + hg]
                    gen0 = bank.fre.val

                    def pe_o(t, bank=bank, hg=hg, cc=cc,
                             gen0=gen0, SSC=SSC, kv_base=kv_base):
                        t.wait_ge(c_ssc.sem, SSC)
                        t.wait_ge(c_kvs.sem, kv_base)
                        if gen0 > 0:
                            t.wait_ge(bank.fre.sem, gen0)
                        ins = None
                        for hh in range(8):
                            h = hg * 8 + hh
                            g = h // 4
                            t.matmul(
                                bank.t[:, hh * 64:(hh + 1) * 64],
                                v_tm[:, cc * 512 + g * 128:
                                     cc * 512 + (g + 1) * 128],
                                ssc[0:64, ((cc % 2) * HQ + h) * 64:
                                    ((cc % 2) * HQ + h + 1) * 64],
                                start=True, stop=False)
                            ins = t.matmul(
                                bank.t[:, hh * 64:(hh + 1) * 64],
                                kv_bf[:, h * DH:(h + 1) * DH],
                                qhat[:, h * TB + cc * 64: h * TB + (cc + 1) * 64],
                                start=False, stop=True)
                        ins.then_inc(bank.rdy.sem, 1)
                    planA.add("PE", pe_o)
                    bank.rdy.val += 1
                    bank.fre.val += 1
                    RDY = bank.rdy.val

                    FREV = bank.fre.val

                    def act_oev(s, bank=bank, hg=hg, cc=cc, RDY=RDY, FREV=FREV):
                        s.wait_ge(bank.rdy.sem, RDY)
                        dst = bass.AP(o_fm, hg * 8 * TB + cc * 64,
                                      [[HQ * TB, 128], [TB, 8], [1, 64]])
                        s.copy(dst, bank.t[:, :]).then_inc(bank.fre.sem, 1)
                        s.wait_ge(bank.fre.sem, FREV)
                        s.nop().then_inc(c_oev.sem, 1)
                    planA.add("ACT", act_oev)
                    c_oev.val += 1

                o_rdy_last = pbk[3].rdy.val

                def pe_och(t, o_rdy_last=o_rdy_last):
                    t.wait_ge(pbk[3].rdy.sem, o_rdy_last)
                    t.nop().then_inc(c_och.sem, 1)
                planA.add("PE", pe_och)
                c_och.val += 1
                n_own_chunk += 1

            KH = KHAT_CC
            for quad in range(4):
                bank = pbk[6]
                gen0 = bank.fre.val

                def pe_delta(t, bank=bank, quad=quad, cc=cc,
                             gen0=gen0, KH=KH, VR=VR):
                    t.wait_ge(c_khat.sem, KH)
                    t.wait_ge(c_vtm.sem, VR)
                    if gen0 > 0:
                        t.wait_ge(bank.fre.sem, gen0)
                    ins = None
                    for r in range(4):
                        h = quad * 4 + r
                        g = h // 4
                        ins = t.matmul(
                            bank.t[:, r * 128:(r + 1) * 128],
                            khat[0:64, ((cc % 2) * HQ + h) * 128:
                                 ((cc % 2) * HQ + h + 1) * 128],
                            v_tm[:, cc * 512 + g * 128: cc * 512 + (g + 1) * 128],
                            start=True, stop=True)
                    ins.then_inc(bank.rdy.sem, 1)
                planA.add("PE", pe_delta)
                bank.rdy.val += 1
                bank.fre.val += 1
                RDY = bank.rdy.val

                def dve_kv(v, bank=bank, quad=quad, RDY=RDY,
                           kv_base=kv_base):
                    v.wait_ge(bank.rdy.sem, RDY)
                    if kv_base > 1:
                        v.wait_ge(c_kvs.sem, kv_base)
                    ins = None
                    for r in range(4):
                        h = quad * 4 + r
                        ins = v.scalar_tensor_tensor(
                            kv_bf[:, h * DH:(h + 1) * DH],
                            kv_bf[:, h * DH:(h + 1) * DH],
                            _BLKD[h],
                            bank.t[:, r * 128:(r + 1) * 128],
                            ALU.mult, ALU.add)
                        ins.then_inc(c_kvs.sem, 1)
                    v.engine_nop().then_inc(bank.fre.sem, 1)
                planA.add("DVE", dve_kv)
                c_kvs.val += 4

            d_rdy_last = pbk[6].rdy.val

            def pe_dch(t, d_rdy_last=d_rdy_last):
                t.wait_ge(pbk[6].rdy.sem, d_rdy_last)
                t.nop().then_inc(c_dch.sem, 1)
            planA.add("PE", pe_dch)
            c_dch.val += 1

        if warm:
            continue

        OEV = c_oev.val
        X1ST_prev = c_x1st.val
        woh = [ring.load(woT.ap()[dc, :, :]) for dc in range(DC)]
        for dc in range(DC):
            h = woh[dc]
            bank = pbk[dc % 2]
            gen0 = bank.fre.val

            RDY = bank.rdy.val + 1

            def pe_oproj(t, h=h, bank=bank, dc=dc, gen0=gen0, OEV=OEV, RDY=RDY):
                h.wait(t)
                if dc == 0:
                    t.wait_ge(c_oev.sem, OEV)
                if gen0 > 0:
                    t.wait_ge(bank.fre.sem, gen0)
                ins = None
                for ec in range(16):
                    ins = t.matmul(bank.t[:, :],
                                   h.ap[:, ec * 128:(ec + 1) * 128],
                                   o_fm[:, ec * TB:(ec + 1) * TB],
                                   start=(ec == 0), stop=(ec == 15))
                ins.then_inc(bank.rdy.sem, 1)
                t.wait_ge(bank.rdy.sem, RDY)
                h.free_on(t.nop())
            planA.add("PE", pe_oproj)
            bank.rdy.val += 1
            bank.fre.val += 1

            def dve_pre(v, bank=bank, dc=dc, xs=xs, RDY=RDY, X1ST_prev=X1ST_prev):
                v.wait_ge(bank.rdy.sem, RDY)
                if dc == 0 and X1ST_prev > 0:
                    v.wait_ge(c_x1st.sem, X1ST_prev)
                v.tensor_tensor(pre[:, dc * TB:(dc + 1) * TB],
                                xs[:, dc * TB:(dc + 1) * TB],
                                bank.t[:, :], ALU.add).then_inc(
                    bank.fre.sem, 1)
                v.engine_nop().then_inc(c_pre.sem, 1)
            planA.add("DVE", dve_pre)
            c_pre.val += 1

        def dve_xfree(v, xslot=xslot):
            v.engine_nop().then_inc(x_fr[xslot].sem, 1)
        planA.add("DVE", dve_xfree)
        x_fr[xslot].val += 1

        PRE_ALL = c_pre.val
        g4 = pbk[4].fre.val
        g5 = pbk[5].fre.val
        sq_base = c_sq.val
        sqr_base = c_sqr.val
        for dc in range(DC):
            def act_sq(s, dc=dc, PRE_ALL=PRE_ALL, sqr_base=sqr_base):
                s.wait_ge(c_pre.sem, PRE_ALL - (DC - 1) + dc)
                if dc >= 2:
                    s.wait_ge(c_sqr.sem, sqr_base + dc - 1)
                s.square(sqb[:, (dc % 2) * TB:(dc % 2 + 1) * TB],
                         pre[:, dc * TB:(dc + 1) * TB]).then_inc(c_sq.sem, 1)
            planA.add("ACT", act_sq)
            c_sq.val += 1
            SQ = c_sq.val

            def pe_stats(t, dc=dc, SQ=SQ, PRE_ALL=PRE_ALL, g4=g4, g5=g5,
                         sqr_base=sqr_base):
                if dc == 0:
                    t.wait_ge(c_pre.sem, PRE_ALL)
                    if g4 > 0:
                        t.wait_ge(pbk[4].fre.sem, g4)
                    if g5 > 0:
                        t.wait_ge(pbk[5].fre.sem, g5)
                t.matmul(pbk[4].t[0:1, :], onesc[:, :],
                         pre[:, dc * TB:(dc + 1) * TB],
                         start=(dc == 0), stop=(dc == 15))
                t.wait_ge(c_sq.sem, SQ)
                ins = t.matmul(pbk[5].t[0:1, :], onesc[:, :],
                               sqb[:, (dc % 2) * TB:(dc % 2 + 1) * TB],
                               start=(dc == 0), stop=(dc == 15))
                ins.then_inc(c_sqr.sem, 1)
                if dc == 15:
                    t.wait_ge(c_sqr.sem, sqr_base + 16)
                    t.nop().then_inc(pbk[5].rdy.sem, 1)
            planA.add("PE", pe_stats)
            c_sqr.val += 1
        pbk[4].fre.val += 1
        pbk[5].rdy.val += 1
        pbk[5].fre.val += 1
        SRDY = pbk[5].rdy.val

        vs0 = mkvs(3)

        def dve_lnstats(v, SRDY=SRDY, vs0=vs0):
            v.wait_ge(pbk[5].rdy.sem, SRDY)
            v.tensor_scalar(lnrow[0:1, 0:TB], pbk[4].t[0:1, :], RN, None,
                            ALU.mult).then_inc(pbk[4].fre.sem, 1)
            v.tensor_scalar(lnrow[0:1, 5 * TB:6 * TB], pbk[5].t[0:1, :], RN,
                            None, ALU.mult).then_inc(pbk[5].fre.sem, 1)
            VS(v, vs0, 0)
            v.tensor_tensor(lnrow[0:1, TB:2 * TB], lnrow[0:1, 0:TB],
                            lnrow[0:1, 0:TB], ALU.mult)
            VS(v, vs0, 1)
            v.tensor_tensor(lnrow[0:1, 2 * TB:3 * TB], lnrow[0:1, 5 * TB:6 * TB],
                            lnrow[0:1, TB:2 * TB], ALU.subtract)
            VS(v, vs0, 2)
            v.tensor_scalar(lnrow[0:1, 2 * TB:3 * TB],
                            lnrow[0:1, 2 * TB:3 * TB], EPS, None,
                            ALU.add).then_inc(c_ln.sem, 1)
        planA.add("DVE", dve_lnstats)
        c_ln.val += 1
        LN1 = c_ln.val

        def act_lnsqrt(s, LN1=LN1):
            s.wait_ge(c_ln.sem, LN1)
            s.activation(lnrow[0:1, 3 * TB:4 * TB], lnrow[0:1, 2 * TB:3 * TB],
                         AF.Sqrt).then_inc(c_ln.sem, 1)
        planA.add("ACT", act_lnsqrt)
        c_ln.val += 1
        LNS = c_ln.val

        vs0 = mkvs(2)

        def dve_lnrecip(v, LNS=LNS, vs0=vs0):
            v.wait_ge(c_ln.sem, LNS)
            v.reciprocal(lnrow[0:1, 4 * TB:5 * TB], lnrow[0:1, 3 * TB:4 * TB])
            VS(v, vs0, 0)
            v.tensor_tensor(lnrow[0:1, 6 * TB:7 * TB], lnrow[0:1, 0:TB],
                            lnrow[0:1, 4 * TB:5 * TB], ALU.mult)
            VS(v, vs0, 1)
            v.tensor_copy(lnbfr[0:1, 0:TB], lnrow[0:1, 4 * TB:5 * TB])
            v.tensor_copy(lnbfr[0:1, TB:2 * TB], lnrow[0:1, 6 * TB:7 * TB])
            v.engine_nop().then_inc(c_ln.sem, 1)
        planA.add("DVE", dve_lnrecip)
        c_ln.val += 1
        LN3 = c_ln.val

        for j in range(2):
            bank = pbk[2 + j]
            gen0 = bank.fre.val

            def pe_bc(t, j=j, bank=bank, gen0=gen0, LN3=LN3):
                t.wait_ge(c_ln.sem, LN3)
                if gen0 > 0:
                    t.wait_ge(bank.fre.sem, gen0)
                t.matmul(bank.t[:, :], onesr[:, :],
                         lnbfr[0:1, j * TB:(j + 1) * TB],
                         start=True, stop=True).then_inc(bank.rdy.sem, 1)
            planA.add("PE", pe_bc)
            bank.rdy.val += 1
            bank.fre.val += 1
            RDY = bank.rdy.val

            FREV = bank.fre.val

            def act_bc(s, j=j, bank=bank, RDY=RDY, FREV=FREV):
                s.wait_ge(bank.rdy.sem, RDY)
                s.copy(rbb[:, j * TB:(j + 1) * TB], bank.t[:, :]).then_inc(
                    bank.fre.sem, 1)
                s.wait_ge(bank.fre.sem, FREV)
                s.nop().then_inc(c_ln.sem, 1)
            planA.add("ACT", act_bc)
            c_ln.val += 1
        LN4 = c_ln.val

        X1ST0 = c_x1st.val
        for dc in range(DC):
            vs0 = mkvs(1)

            X1P = c_x1a.val

            def dve_x1(v, dc=dc, LN4=LN4, X1ST0=X1ST0, vs0=vs0, X1P=X1P):
                if dc == 0:
                    v.wait_ge(c_ln.sem, LN4)
                    if X1ST0 > 0:
                        v.wait_ge(c_x1st.sem, X1ST0)
                else:
                    v.wait_ge(c_x1a.sem, X1P)
                v.tensor_tensor(rs2[:, :], pre[:, dc * TB:(dc + 1) * TB],
                                rbb[:, 0:TB], ALU.mult)
                VS(v, vs0, 0)
                v.tensor_tensor(x1b[:, dc * TB:(dc + 1) * TB], rs2[:, :],
                                rbb[:, TB:2 * TB], ALU.subtract).then_inc(
                    c_x1a.sem, 1)
            planA.add("DVE", dve_x1)
            c_x1a.val += 1
            X1A = c_x1a.val

            def sy_x1st(sy, dc=dc, blk=blk, X1A=X1A):
                sy.wait_ge(c_x1a.sem, X1A)
                sy.dma_start(x1d.ap()[dc, :, blk * TB:(blk + 1) * TB],
                             x1b[:, dc * TB:(dc + 1) * TB]).then_inc(
                    c_x1st.sem, 16)
            planA.add("SYNC", sy_x1st)
            c_x1st.val += 16

        rh = ring.load(rtr.ap()[:, :], w=128)
        bank = pbk[6]
        gen0 = bank.fre.val
        X1A_ALL = c_x1a.val

        RRDY = bank.rdy.val + 1

        def pe_router(t, rh=rh, bank=bank, gen0=gen0, X1A_ALL=X1A_ALL,
                      RRDY=RRDY):
            rh.wait(t)
            t.wait_ge(c_x1a.sem, X1A_ALL)
            if gen0 > 0:
                t.wait_ge(bank.fre.sem, gen0)
            ins = None
            for dc in range(DC):
                ins = t.matmul(bank.t[0:8, :], rh.ap[:, dc * 8:(dc + 1) * 8],
                               x1b[:, dc * TB:(dc + 1) * TB],
                               start=(dc == 0), stop=(dc == 15))
            ins.then_inc(bank.rdy.sem, 1)
            t.wait_ge(bank.rdy.sem, RRDY)
            rh.free_on(t.nop())
        planA.add("PE", pe_router)
        bank.rdy.val += 1
        bank.fre.val += 1

        def dve_logit(v, bank=bank, blk=blk, RRDY=RRDY):
            v.wait_ge(bank.rdy.sem, RRDY)
            v.tensor_scalar(logit_sb[:, blk * TB:(blk + 1) * TB],
                            bank.t[0:8, :], 0.0, None, ALU.add).then_inc(
                bank.fre.sem, 1)
        planA.add("DVE", dve_logit)

    X1ST_ALL = c_x1st.val

    def sy_enda(sy):
        sy.wait_ge(c_x1st.sem, X1ST_ALL)
    planA.add("SYNC", sy_enda)

    if DEBUG2:
        dbg_d = {
            "dbg_q": nc.dram_tensor("dq", [128, HQ * TB], BF16, kind="ExternalOutput"),
            "dbg_k": nc.dram_tensor("dk", [128, HKV * TB], BF16, kind="ExternalOutput"),
            "dbg_qh": nc.dram_tensor("dqh", [128, HQ * TB], BF16, kind="ExternalOutput"),
            "dbg_o": nc.dram_tensor("do", [128, HQ * TB], BF16, kind="ExternalOutput"),
            "dbg_kv": nc.dram_tensor("dkv", [128, HQ * DH], BF16, kind="ExternalOutput"),
            "dbg_pre": nc.dram_tensor("dpre", [128, DC * TB], BF16, kind="ExternalOutput"),
            "dbg_vtm": nc.dram_tensor("dvt", [64, CPB * 512], BF16, kind="ExternalOutput"),
            "dbg_lg": nc.dram_tensor("dlg", [8, T], F32, kind="ExternalOutput"),
        }

        def sy_dbg(sy):
            sy.dma_start(dbg_d["dbg_q"].ap(), q_fm[:, :]).then_inc(c_x1st.sem, 16)
            sy.dma_start(dbg_d["dbg_k"].ap(), k_fm[:, :]).then_inc(c_x1st.sem, 16)
            sy.dma_start(dbg_d["dbg_qh"].ap(), qhat[:, :]).then_inc(c_x1st.sem, 16)
            sy.dma_start(dbg_d["dbg_o"].ap(), o_fm[:, :]).then_inc(c_x1st.sem, 16)
            sy.dma_start(dbg_d["dbg_kv"].ap(), kv_bf[:, :]).then_inc(c_x1st.sem, 16)
            sy.dma_start(dbg_d["dbg_pre"].ap(), pre[:, :]).then_inc(c_x1st.sem, 16)
            sy.dma_start(dbg_d["dbg_vtm"].ap(), v_tm[:, :]).then_inc(c_x1st.sem, 16)
            sy.dma_start(dbg_d["dbg_lg"].ap(), logit_sb[:, :]).then_inc(c_x1st.sem, 16)
            sy.wait_ge(c_x1st.sem, X1ST_ALL + 16 * 8)
        planA.add("SYNC", sy_dbg)
        c_x1st.val += 16 * 8
    scopeA.close()

    if SKIP_B:
        with nc.Block("phaseA") as blk:
            @blk.tensor
            def _(t):
                planA.emit("PE", t)

            @blk.vector
            def _(v):
                planA.emit("DVE", v)

            @blk.scalar
            def _(s):
                planA.emit("ACT", s)

            @blk.sync
            def _(sy):
                planA.emit("SYNC", sy)

            @blk.gpsimd
            def _(g):
                planA.emit("POOL", g)
        nc.ctx.close()
        return nc

    scopeB = contextlib.ExitStack()
    sbB = lambda n, s, d=BF16: scopeB.enter_context(nc.sbuf_tensor(n, s, d))
    ringB = Ring(nc, planB, scopeB, "wringB", 5, 2048)
    x1th = sbB("x1th", [128, DC * THW])
    y_acc = sbB("y_acc", [128, DC * THW])
    h_buf = [sbB(f"h_buf{i}", [128, FC * TSW]) for i in range(TS)]
    msb = sbB("msbb", [128, 2 * THW])
    tmpA = [sbB(f"tmpA{i}", [128, 2 * TSW]) for i in range(2)]
    tmpSA = [sbB(f"tmpSA{i}", [128, 2 * TSW]) for i in range(2)]
    tmpB = [sbB(f"tmpB{i}", [128, 2 * TSW]) for i in range(2)]
    l2x = [sbB(f"l2x{i}", [128, THW]) for i in range(2)]
    sq2 = [sbB(f"sq2_{i}", [128, THW]) for i in range(2)]
    rowsc = sbB("rowsc", [1, 16 * 512], F32)
    eqs = sbB("eqs", [1, 8 * 512])
    rb2 = sbB("rb2", [128, 2 * THW])
    ostg = [sbB(f"ostg{i}", [128, THW], F32) for i in range(2)]

    def rsl(i, n=1):
        return rowsc[0:1, i * 512:(i + n) * 512]

    for gq in range(0 if NOMASK else 4):
        for e in range(NE):
            bank = pbk[5 + (gq * 8 + e) % 2]
            gen0 = bank.fre.val

            def pe_mrow(t, e=e, gq=gq, bank=bank, gen0=gen0):
                if gen0 > 0:
                    t.wait_ge(bank.fre.sem, gen0)
                t.matmul(bank.t[0:1, :], oneh[:, e:e + 1],
                         logit_sb[:, gq * 512:(gq + 1) * 512],
                         start=True, stop=True).then_inc(bank.rdy.sem, 1)
            planB.add("PE", pe_mrow)
            bank.rdy.val += 1
            bank.fre.val += 1
            RDY = bank.rdy.val

            FREV = bank.fre.val

            def act_mrow(s, e=e, bank=bank, RDY=RDY, FREV=FREV):
                s.wait_ge(bank.rdy.sem, RDY)
                s.copy(rsl(e), bank.t[0:1, :]).then_inc(bank.fre.sem, 1)
                s.wait_ge(bank.fre.sem, FREV)
                s.nop().then_inc(c_msk.sem, 1)
            planB.add("ACT", act_mrow)
            c_msk.val += 1
        MSK1 = c_msk.val

        vs0 = mkvs(2)

        def dve_mtree(v, MSK1=MSK1, vs0=vs0):
            v.wait_ge(c_msk.sem, MSK1)
            for j in range(4):
                v.tensor_tensor(rsl(8 + j), rsl(2 * j), rsl(2 * j + 1), ALU.max)
            VS(v, vs0, 0)
            v.tensor_tensor(rsl(14), rsl(8), rsl(9), ALU.max)
            v.tensor_tensor(rsl(15), rsl(10), rsl(11), ALU.max)
            VS(v, vs0, 1)
            v.tensor_tensor(rsl(12), rsl(14), rsl(15), ALU.max)
            v.engine_nop().then_inc(c_msk.sem, 1)
        planB.add("DVE", dve_mtree)
        c_msk.val += 1
        MSK2 = c_msk.val

        def act_score(s, MSK2=MSK2):
            s.wait_ge(c_msk.sem, MSK2)
            s.activation(rsl(13), rsl(12), AF.Sigmoid).then_inc(c_msk.sem, 1)
        planB.add("ACT", act_score)
        c_msk.val += 1
        MSK3 = c_msk.val

        vs0 = mkvs(NE)

        def dve_msc(v, MSK3=MSK3, vs0=vs0):
            v.wait_ge(c_msk.sem, MSK3)
            for e in range(NE):
                v.tensor_tensor(rsl(8 + (e % 2)), rsl(e), rsl(12), ALU.is_ge)
                VS(v, vs0, e)
                v.tensor_tensor(eqs[0:1, e * 512:(e + 1) * 512],
                                rsl(8 + (e % 2)), rsl(13), ALU.mult)
            v.engine_nop().then_inc(c_msk.sem, 1)
        planB.add("DVE", dve_msc)
        c_msk.val += 1
        MSK4 = c_msk.val

        bank = pbk[5 + gq % 2]
        gen0 = bank.fre.val

        def pe_mscp(t, gq=gq, bank=bank, gen0=gen0, MSK4=MSK4):
            t.wait_ge(c_msk.sem, MSK4)
            if gen0 > 0:
                t.wait_ge(bank.fre.sem, gen0)
            ins = None
            for e in range(NE):
                ins = t.matmul(bank.t[0:8, :],
                               onehr[0:1, e * 8:(e + 1) * 8],
                               eqs[0:1, e * 512:(e + 1) * 512],
                               start=(e == 0), stop=(e == NE - 1))
            ins.then_inc(bank.rdy.sem, 1)
        planB.add("PE", pe_mscp)
        bank.rdy.val += 1
        bank.fre.val += 1
        RDY = bank.rdy.val

        FREV = bank.fre.val

        def act_mscp(s, gq=gq, bank=bank, RDY=RDY, FREV=FREV):
            s.wait_ge(bank.rdy.sem, RDY)
            s.copy(mscP[:, gq * 512:(gq + 1) * 512],
                   bank.t[0:8, :]).then_inc(bank.fre.sem, 1)
            s.wait_ge(bank.fre.sem, FREV)
            s.nop().then_inc(c_msk.sem, 1)
        planB.add("ACT", act_mscp)
        c_msk.val += 1
    MSKF = c_msk.val

    sav_hist = {}
    y_hist = {}
    hb_hist = {}
    for th in range(0 if NOTAIL else MKTH):
        OUT_prev = c_out.val

        def sy_x1ld(sy, th=th, OUT_prev=OUT_prev):
            if th > 0:
                sy.wait_ge(c_out.sem, OUT_prev)
            for dq in range(4):
                src = bass.AP(x1d, dq * 4 * 128 * T + th * THW,
                              [[T, 128], [128 * T, 4], [1, THW]])
                dst = bass.AP(x1th, dq * 4 * THW,
                              [[DC * THW, 128], [THW, 4], [1, THW]])
                sy.dma_start(dst, src).then_inc(c_x1ld.sem, 16)
        planB.add("SYNC", sy_x1ld)
        c_x1ld.val += 64
        X1LD = c_x1ld.val

        for e in range(MKNE):
            if e < NE:
                for ts in range(TS):
                    bank = pbk[6]
                    gen0 = bank.fre.val

                    def pe_msb(t, e=e, th=th, ts=ts, bank=bank, gen0=gen0,
                               MSKF=MSKF):
                        t.wait_ge(c_msk.sem, MSKF)
                        if gen0 > 0:
                            t.wait_ge(bank.fre.sem, gen0)
                        t.matmul(bank.t[:, :],
                                 selb[:, e * 128:(e + 1) * 128],
                                 mscP[:, th * THW + ts * TSW:
                                      th * THW + (ts + 1) * TSW],
                                 start=True, stop=True).then_inc(bank.rdy.sem, 1)
                    planB.add("PE", pe_msb)
                    bank.rdy.val += 1
                    bank.fre.val += 1
                    RDY = bank.rdy.val
                    MSBW = max(0, c_msbf.val - 1)

                    FREV = bank.fre.val

                    def act_msb(s, e=e, ts=ts, bank=bank, RDY=RDY, MSBW=MSBW,
                                FREV=FREV):
                        s.wait_ge(bank.rdy.sem, RDY)
                        if MSBW > 0:
                            s.wait_ge(c_msbf.sem, MSBW)
                        s.copy(msb[:, (e % 2) * THW + ts * TSW:
                                   (e % 2) * THW + (ts + 1) * TSW],
                               bank.t[:, :]).then_inc(bank.fre.sem, 1)
                        s.wait_ge(bank.fre.sem, FREV)
                        s.nop().then_inc(c_msb.sem, 1)
                    planB.add("ACT", act_msb)
                    c_msb.val += 1
            MSB = c_msb.val

            for fc in range(FC):
                hG = ringB.load(gW.ap()[e, fc, :, :])
                hU = ringB.load(uW.ap()[e, fc, :, :])
                for ts in range(TS):
                    bankG = pbk[ts]
                    bankU = pbk[2 + ts]
                    genG = bankG.fre.val
                    genU = bankU.fre.val

                    RG = bankG.rdy.val + 1
                    RU = bankU.rdy.val + 1

                    def pe_gu(t, e=e, fc=fc, ts=ts, hG=hG, hU=hU,
                              bankG=bankG, bankU=bankU, genG=genG, genU=genU,
                              X1LD=X1LD, RG=RG, RU=RU):
                        if fc == 0 and ts == 0:
                            t.wait_ge(c_x1ld.sem, X1LD)
                        if ts == 0:
                            hG.wait(t)
                            hU.wait(t)
                        if genG > 0:
                            t.wait_ge(bankG.fre.sem, genG)
                        ins = None
                        for dc in range(DC):
                            ins = t.matmul(
                                bankG.t[:, :],
                                hG.ap[:, dc * 128:(dc + 1) * 128],
                                x1th[:, dc * THW + ts * TSW:
                                     dc * THW + (ts + 1) * TSW],
                                start=(dc == 0), stop=(dc == DC - 1))
                        ins.then_inc(bankG.rdy.sem, 1)
                        if ts == TS - 1:
                            t.wait_ge(bankG.rdy.sem, RG)
                            hG.free_on(t.nop())
                        if genU > 0:
                            t.wait_ge(bankU.fre.sem, genU)
                        for dc in range(DC):
                            ins = t.matmul(
                                bankU.t[:, :],
                                hU.ap[:, dc * 128:(dc + 1) * 128],
                                x1th[:, dc * THW + ts * TSW:
                                     dc * THW + (ts + 1) * TSW],
                                start=(dc == 0), stop=(dc == DC - 1))
                        ins.then_inc(bankU.rdy.sem, 1)
                        if ts == TS - 1:
                            t.wait_ge(bankU.rdy.sem, RU)
                            hU.free_on(t.nop())
                    planB.add("PE", pe_gu)
                    bankG.rdy.val += 1
                    bankG.fre.val += 1
                    bankU.rdy.val += 1
                    bankU.fre.val += 1
                    par = fc % 2
                    SAVW = sav_hist.get((ts, par), 0)
                    HBW = hb_hist.get((ts, par), 0)

                    if e < NE:
                        def dve_ab(v, e=e, ts=ts, bankG=bankG, bankU=bankU,
                                   RG=RG, RU=RU, MSB=MSB, SAVW=SAVW, fc=fc,
                                   HBW=HBW, par=par):
                            v.wait_ge(bankG.rdy.sem, RG)
                            if fc == 0 and ts == 0:
                                v.wait_ge(c_msb.sem, MSB)
                            if SAVW > 0:
                                v.wait_ge(c_silu.sem, SAVW)
                            if HBW > 0:
                                v.wait_ge(c_hb.sem, HBW)
                            v.tensor_tensor(
                                tmpA[ts][:, par * TSW:(par + 1) * TSW], bankG.t[:, :],
                                msb[:, (e % 2) * THW + ts * TSW:
                                    (e % 2) * THW + (ts + 1) * TSW],
                                ALU.mult).then_inc(bankG.fre.sem, 1)
                            v.engine_nop().then_inc(c_silu.sem, 1)
                            v.wait_ge(bankU.rdy.sem, RU)
                            v.tensor_tensor(
                                tmpB[ts][:, par * TSW:(par + 1) * TSW], bankU.t[:, :],
                                msb[:, (e % 2) * THW + ts * TSW:
                                    (e % 2) * THW + (ts + 1) * TSW],
                                ALU.mult).then_inc(bankU.fre.sem, 1)
                            v.engine_nop().then_inc(c_silu.sem, 1)
                        planB.add("DVE", dve_ab)
                        c_silu.val += 2
                    else:
                        FG = bankG.fre.val
                        FU = bankU.fre.val

                        def act_ab(s, bankG=bankG, bankU=bankU, RG=RG, RU=RU,
                                   ts=ts, SAVW=SAVW, FG=FG, FU=FU, HBW=HBW, par=par):
                            s.wait_ge(bankG.rdy.sem, RG)
                            if SAVW > 0:
                                s.wait_ge(c_silu.sem, SAVW)
                            if HBW > 0:
                                s.wait_ge(c_hb.sem, HBW)
                            s.copy(tmpA[ts][:, par * TSW:(par + 1) * TSW], bankG.t[:, :]).then_inc(
                                bankG.fre.sem, 1)
                            s.wait_ge(bankG.fre.sem, FG)
                            s.nop().then_inc(c_silu.sem, 1)
                            s.wait_ge(bankU.rdy.sem, RU)
                            s.copy(tmpB[ts][:, par * TSW:(par + 1) * TSW], bankU.t[:, :]).then_inc(
                                bankU.fre.sem, 1)
                            s.wait_ge(bankU.fre.sem, FU)
                            s.nop().then_inc(c_silu.sem, 1)
                        planB.add("ACT", act_ab)
                        c_silu.val += 2
                    AB = c_silu.val

                    def act_silu(s, ts=ts, AB=AB, par=par):
                        s.wait_ge(c_silu.sem, AB)
                        s.activation(tmpSA[ts][:, par * TSW:(par + 1) * TSW], tmpA[ts][:, par * TSW:(par + 1) * TSW],
                                     AF.Silu).then_inc(c_silu.sem, 1)
                    planB.add("ACT", act_silu)
                    c_silu.val += 1
                    SAV = c_silu.val
                    sav_hist[(ts, par)] = SAV

                    def dve_h(v, fc=fc, ts=ts, SAV=SAV, par=par):
                        v.wait_ge(c_silu.sem, SAV)
                        v.engine_nop()
                        v.tensor_tensor(h_buf[ts][:, fc * TSW:(fc + 1) * TSW],
                                        tmpSA[ts][:, par * TSW:(par + 1) * TSW], tmpB[ts][:, par * TSW:(par + 1) * TSW],
                                        ALU.mult).then_inc(c_hb.sem, 1)
                    planB.add("DVE", dve_h)
                    c_hb.val += 1
                    hb_hist[(ts, par)] = c_hb.val
            HB = c_hb.val

            if e < NE:
                def dve_msbf(v):
                    v.engine_nop().then_inc(c_msbf.sem, 1)
                planB.add("DVE", dve_msbf)
                c_msbf.val += 1

            for dc in range(DC):
                hD = ringB.load(dW.ap()[e, dc, :, :])
                for ts in range(TS):
                    bank = pbk[4 + ts]
                    gen0 = bank.fre.val

                    RDY = bank.rdy.val + 1

                    def pe_down(t, e=e, dc=dc, ts=ts, hD=hD, bank=bank,
                                gen0=gen0, HB=HB, RDY=RDY):
                        if ts == 0:
                            hD.wait(t)
                        if dc == 0 and ts == 0:
                            t.wait_ge(c_hb.sem, HB)
                        if gen0 > 0:
                            t.wait_ge(bank.fre.sem, gen0)
                        ins = None
                        for fc in range(FC):
                            ins = t.matmul(
                                bank.t[:, :],
                                hD.ap[:, fc * 128:(fc + 1) * 128],
                                h_buf[ts][:, fc * TSW:(fc + 1) * TSW],
                                start=(fc == 0), stop=(fc == FC - 1))
                        ins.then_inc(bank.rdy.sem, 1)
                        if ts == TS - 1:
                            t.wait_ge(bank.rdy.sem, RDY)
                            hD.free_on(t.nop())
                    planB.add("PE", pe_down)
                    bank.rdy.val += 1
                    bank.fre.val += 1

                    YPREV = y_hist.get((dc, ts), 0)
                    y_hist[(dc, ts)] = c_yw.val + 1

                    def dve_y(v, e=e, dc=dc, ts=ts, bank=bank, RDY=RDY,
                              YPREV=YPREV):
                        v.wait_ge(bank.rdy.sem, RDY)
                        if YPREV > 0:
                            v.wait_ge(c_yw.sem, YPREV)
                        dst = y_acc[:, dc * THW + ts * TSW:
                                    dc * THW + (ts + 1) * TSW]
                        if e == 0:
                            v.tensor_scalar(dst, bank.t[:, :], 0.0, None,
                                            ALU.add)
                        else:
                            v.tensor_tensor(dst, dst, bank.t[:, :], ALU.add)
                        v.engine_nop().then_inc(bank.fre.sem, 1)
                        v.engine_nop().then_inc(c_yw.sem, 1)
                    planB.add("DVE", dve_y)
                    c_yw.val += 1
        YW = c_yw.val

        gf = [pbk[i].fre.val for i in range(7)]
        l2w_base = c_l2w.val
        sq2w_base = c_sq2w.val
        st2r_base = c_st2r.val
        for dc in range(DC if TAILCUT >= 2 else 0):
            def dve_l2(v, dc=dc, YW=YW, st2r_base=st2r_base):
                if dc == 0:
                    v.wait_ge(c_yw.sem, YW)
                if dc >= 2:
                    v.wait_ge(c_st2r.sem, st2r_base + dc - 1)
                v.tensor_tensor(l2x[dc % 2][:, :],
                                x1th[:, dc * THW:(dc + 1) * THW],
                                y_acc[:, dc * THW:(dc + 1) * THW],
                                ALU.add).then_inc(c_l2w.sem, 1)
            planB.add("DVE", dve_l2)
            c_l2w.val += 1
            L2 = c_l2w.val

            def act_sq2(s, dc=dc, L2=L2, st2r_base=st2r_base):
                s.wait_ge(c_l2w.sem, L2)
                if dc >= 2:
                    s.wait_ge(c_st2r.sem, st2r_base + dc - 1)
                s.square(sq2[dc % 2][:, :], l2x[dc % 2][:, :]).then_inc(
                    c_sq2w.sem, 1)
            planB.add("ACT", act_sq2)
            c_sq2w.val += 1
            L3 = c_sq2w.val

            def pe_st2(t, dc=dc, L2=L2, L3=L3, gf=gf, st2r_base=st2r_base):
                t.wait_ge(c_l2w.sem, L2)
                if dc == 0:
                    for i in (2, 3, 4, 5):
                        if gf[i] > 0:
                            t.wait_ge(pbk[i].fre.sem, gf[i])
                for half in range(2):
                    t.matmul(pbk[2 + half].t[0:1, :], onesc[:, :],
                             l2x[dc % 2][:, half * 512:(half + 1) * 512],
                             start=(dc == 0), stop=(dc == 15))
                t.wait_ge(c_sq2w.sem, L3)
                ins = None
                for half in range(2):
                    ins = t.matmul(pbk[4 + half].t[0:1, :], onesc[:, :],
                                   sq2[dc % 2][:, half * 512:(half + 1) * 512],
                                   start=(dc == 0), stop=(dc == 15))
                ins.then_inc(c_st2r.sem, 1)
                if dc == 15:
                    t.wait_ge(c_st2r.sem, st2r_base + 16)
                    t.nop().then_inc(pbk[5].rdy.sem, 1)
            planB.add("PE", pe_st2)
            c_st2r.val += 1
        for i in (2, 3, 4):
            pbk[i].fre.val += 1
        pbk[5].rdy.val += 1
        pbk[5].fre.val += 1
        S2 = pbk[5].rdy.val

        if TAILCUT < 3:
            continue

        vs0 = mkvs(3)

        def dve_ln2s(v, S2=S2, vs0=vs0):
            v.wait_ge(pbk[5].rdy.sem, S2)
            v.tensor_scalar(rsl(0), pbk[2].t[0:1, :], RN, None,
                            ALU.mult).then_inc(pbk[2].fre.sem, 1)
            v.tensor_scalar(rsl(1), pbk[3].t[0:1, :], RN, None,
                            ALU.mult).then_inc(pbk[3].fre.sem, 1)
            v.tensor_scalar(rsl(2), pbk[4].t[0:1, :], RN, None,
                            ALU.mult).then_inc(pbk[4].fre.sem, 1)
            v.tensor_scalar(rsl(3), pbk[5].t[0:1, :], RN, None,
                            ALU.mult).then_inc(pbk[5].fre.sem, 1)
            VS(v, vs0, 0)
            v.tensor_tensor(rsl(4, 2), rsl(0, 2), rsl(0, 2), ALU.mult)
            VS(v, vs0, 1)
            v.tensor_tensor(rsl(6, 2), rsl(2, 2), rsl(4, 2), ALU.subtract)
            VS(v, vs0, 2)
            v.tensor_scalar(rsl(6, 2), rsl(6, 2), EPS, None,
                            ALU.add).then_inc(c_ln2.sem, 1)
        planB.add("DVE", dve_ln2s)
        c_ln2.val += 1
        L4 = c_ln2.val

        def act_ln2q(s, L4=L4):
            s.wait_ge(c_ln2.sem, L4)
            s.activation(rsl(8, 2), rsl(6, 2), AF.Sqrt).then_inc(c_ln2.sem, 1)
        planB.add("ACT", act_ln2q)
        c_ln2.val += 1
        L5 = c_ln2.val

        vs0 = mkvs(2)

        def dve_ln2r(v, L5=L5, vs0=vs0):
            v.wait_ge(c_ln2.sem, L5)
            v.reciprocal(rsl(10, 2), rsl(8, 2))
            VS(v, vs0, 0)
            v.tensor_tensor(rsl(12, 2), rsl(0, 2), rsl(10, 2), ALU.mult)
            VS(v, vs0, 1)
            v.tensor_copy(eqs[0:1, 0:THW], rsl(10, 2))
            v.tensor_copy(eqs[0:1, THW:2 * THW], rsl(12, 2))
            v.engine_nop().then_inc(c_ln2.sem, 1)
        planB.add("DVE", dve_ln2r)
        c_ln2.val += 1
        L6 = c_ln2.val

        if TAILCUT < 4:
            continue
        for j in range(2):
            for half in range(2):
                bank = pbk[2 * j + half]
                gen0 = bank.fre.val

                def pe_bc2(t, j=j, half=half, bank=bank, gen0=gen0, L6=L6):
                    t.wait_ge(c_ln2.sem, L6)
                    if gen0 > 0:
                        t.wait_ge(bank.fre.sem, gen0)
                    t.matmul(bank.t[:, :], onesr[:, :],
                             eqs[0:1, j * THW + half * 512:
                                 j * THW + (half + 1) * 512],
                             start=True, stop=True).then_inc(bank.rdy.sem, 1)
                planB.add("PE", pe_bc2)
                bank.rdy.val += 1
                bank.fre.val += 1
                RDY = bank.rdy.val

                FREV = bank.fre.val

                def act_bc2(s, j=j, half=half, bank=bank, RDY=RDY, FREV=FREV):
                    s.wait_ge(bank.rdy.sem, RDY)
                    s.copy(rb2[:, j * THW + half * 512:
                               j * THW + (half + 1) * 512],
                           bank.t[:, :]).then_inc(bank.fre.sem, 1)
                    s.wait_ge(bank.fre.sem, FREV)
                    s.nop().then_inc(c_ln2.sem, 1)
                planB.add("ACT", act_bc2)
                c_ln2.val += 1
        L7 = c_ln2.val

        if TAILCUT < 5:
            continue
        ost_base = c_ost.val
        for dc in range(DC):
            vs0 = mkvs(2)
            OSTP = c_ostp[dc % 2].val

            def dve_out(v, dc=dc, th=th, L7=L7, vs0=vs0, OSTP=OSTP):
                if dc == 0:
                    v.wait_ge(c_ln2.sem, L7)
                if OSTP > 0:
                    v.wait_ge(c_ostp[dc % 2].sem, OSTP)
                v.tensor_tensor(l2x[dc % 2][:, :],
                                x1th[:, dc * THW:(dc + 1) * THW],
                                y_acc[:, dc * THW:(dc + 1) * THW], ALU.add)
                VS(v, vs0, 0)
                v.tensor_tensor(sq2[dc % 2][:, :], l2x[dc % 2][:, :],
                                rb2[:, 0:THW], ALU.mult)
                VS(v, vs0, 1)
                v.tensor_tensor(ostg[dc % 2][:, :], sq2[dc % 2][:, :],
                                rb2[:, THW:2 * THW],
                                ALU.subtract).then_inc(c_out.sem, 1)
            planB.add("DVE", dve_out)
            c_out.val += 1
            OUTV = c_out.val

            def sy_out(sy, dc=dc, th=th, OUTV=OUTV):
                sy.wait_ge(c_out.sem, OUTV)
                sy.dma_start(
                    out_d.ap()[:, dc * T + th * THW: dc * T + (th + 1) * THW],
                    ostg[dc % 2][:, :]).then_inc(c_ostp[dc % 2].sem, 16)
            planB.add("SYNC", sy_out)
            c_ostp[dc % 2].val += 16

    OST0 = c_ostp[0].val
    OST1 = c_ostp[1].val
    MSKF2 = c_msk.val

    def sy_final(sy):
        sy.wait_ge(c_ostp[0].sem, OST0)
        sy.wait_ge(c_ostp[1].sem, OST1)
        sy.wait_ge(c_msk.sem, MSKF2)
    planB.add("SYNC", sy_final)

    with nc.Block("phaseA") as blk:
        @blk.tensor
        def _(t):
            planA.emit("PE", t)

        @blk.vector
        def _(v):
            planA.emit("DVE", v)

        @blk.scalar
        def _(s):
            planA.emit("ACT", s)

        @blk.sync
        def _(sy):
            planA.emit("SYNC", sy)

        @blk.gpsimd
        def _(g):
            planA.emit("POOL", g)

    with nc.Block("phaseB") as blk:
        @blk.tensor
        def _(t):
            planB.emit("PE", t)

        @blk.vector
        def _(v):
            planB.emit("DVE", v)

        @blk.scalar
        def _(s):
            planB.emit("ACT", s)

        @blk.sync
        def _(sy):
            planB.emit("SYNC", sy)

        @blk.gpsimd
        def _(g):
            planB.emit("POOL", g)

    scopeB.close()
    nc.ctx.close()
    return nc



def _get_nc():
    if "nc" not in _CACHE:
        _CACHE["nc"] = _build()
    return _CACHE["nc"]


def kernel(L0_attn_x, L0_attn_wq, L0_attn_wk, L0_attn_wv, L0_attn_wo,
           L0_ffn_router_w, L0_ffn_gate_w, L0_ffn_up_w, L0_ffn_down_w,
           L0_ffn_shared_gate_w, L0_ffn_shared_up_w, L0_ffn_shared_down_w):
    x = np.ascontiguousarray(np.asarray(L0_attn_x, np.float32))
    wq = np.asarray(L0_attn_wq, np.float32)
    wk = np.asarray(L0_attn_wk, np.float32)
    wv = np.asarray(L0_attn_wv, np.float32)
    wo = np.asarray(L0_attn_wo, np.float32)
    rw = np.asarray(L0_ffn_router_w, np.float32)
    gw = np.asarray(L0_ffn_gate_w, np.float32)
    uw = np.asarray(L0_ffn_up_w, np.float32)
    dw = np.asarray(L0_ffn_down_w, np.float32)
    sgw = np.asarray(L0_ffn_shared_gate_w, np.float32)
    suw = np.asarray(L0_ffn_shared_up_w, np.float32)
    sdw = np.asarray(L0_ffn_shared_down_w, np.float32)

    tabs = _host_tables()
    shared = {
        "wqT": _swz_lhsT(wq.T), "wkT": _swz_lhsT(wk.T), "woT": _swz_lhsT(wo.T),
        "wvR": np.ascontiguousarray(
            np.ascontiguousarray(wv.T).reshape(4, 4, 128, 512).transpose(
                0, 2, 1, 3).reshape(4, 128, 2048)),
        "rtr": np.ascontiguousarray(
            rw.reshape(16, 128, 8).transpose(1, 0, 2).reshape(128, 128)),
        "gW": np.stack([_swz_lhsT(gw[e]) for e in range(NE)] + [_swz_lhsT(sgw)]),
        "uW": np.stack([_swz_lhsT(uw[e]) for e in range(NE)] + [_swz_lhsT(suw)]),
        "dW": np.stack([_swz_lhsT(dw[e]) for e in range(NE)] + [_swz_lhsT(sdw)]),
        "diagTd": tabs["diagT"], "qdecbd": tabs["qdecb"], "kdecd": tabs["kdec"],
        "identd": tabs["ident"], "onescd": tabs["onesc"],
        "onesrd": tabs["onesr"], "onehd": tabs["oneh"],
        "onehrd": tabs["onehr"], "selbd": tabs["selb"],
    }

    in_maps = []
    zero_tab = np.zeros((128, T), BF)
    zero_x = np.zeros((NBLK, 128, DC * TB), np.float32)
    for c in range(8):
        b, half = c // 2, c % 2
        xt = np.ascontiguousarray(x[b, half * T:(half + 1) * T, :].T)
        m = dict(shared)
        m["xb"] = _swz_x(xt)
        cos_o_t, sin_o_t = _rope_tables(half * T + np.arange(T))
        m["cosb"], m["sinb"] = cos_o_t, sin_o_t
        if half == 1:
            xtp = np.ascontiguousarray(x[b, 0:T, :].T)
            m["xp"] = _swz_x(xtp)
            cos_p_t, sin_p_t = _rope_tables(np.arange(T))
            m["cosp"], m["sinp"] = cos_p_t, sin_p_t
        else:
            m["xp"] = zero_x
            m["cosp"], m["sinp"] = zero_tab, zero_tab
        in_maps.append(m)

    nc = _get_nc()
    res = run_bass_kernel_spmd(nc, in_maps, core_ids=list(range(8)))

    out = np.empty((B, S, D), np.float32)
    for c in range(8):
        r = np.asarray(res.results[c]["out"], np.float32)
        blkm = r.reshape(128, DC, T).transpose(2, 1, 0).reshape(T, D)
        out[c // 2, (c % 2) * T:(c % 2 + 1) * T, :] = blkm
    return out


# version 04
# speedup vs baseline: 1.0942x; 1.0896x over previous
import os
import contextlib
import numpy as np
import ml_dtypes
import concourse.bass as bass
import concourse.mybir as mybir
from concourse.bass_utils import run_bass_kernel_spmd

F32 = mybir.dt.float32
BF16 = mybir.dt.bfloat16
AF = mybir.ActivationFunctionType
ALU = mybir.AluOpType
BF = ml_dtypes.bfloat16

B, S, D = 4, 4096, 2048
T = 2048
NBLK = 4
TB = 512
DC = 16
HQ, HKV, DH = 16, 4, 128
CH = 64
CPB = TB // CH
NE = 8
FC = 16
TH = 2
THW = 1024
TS = 2
TSW = 512
EPS = 1e-5
RN = 1.0 / 2048.0

DEBUG_X1 = bool(os.environ.get("MK_DEBUG_X1"))
DEBUG2 = bool(os.environ.get("MK_DEBUG2"))
SKIP_B = bool(os.environ.get("MK_SKIP_B"))
NO_QHAT = bool(os.environ.get("MK_NO_QHAT"))
NO_LGT = bool(os.environ.get("MK_NO_LGT"))
MKNE = int(os.environ.get("MK_NE", str(NE + 1)))
NOMASK = bool(os.environ.get("MK_NOMASK"))
NOTAIL = bool(os.environ.get("MK_NOTAIL"))
TAILCUT = int(os.environ.get("MK_TAILCUT", "5"))
MKTH = int(os.environ.get("MK_TH", str(TH)))
NBI = int(os.environ.get("MK_NBI", "8"))



def _swz_lhsT(w):
    w = np.ascontiguousarray(w, np.float32)
    K, M = w.shape
    KC, MC = K // 128, M // 128
    return np.ascontiguousarray(
        w.reshape(KC, 128, MC, 128).transpose(2, 1, 0, 3).reshape(MC, 128, KC * 128))


def _swz_x(xt):
    return np.ascontiguousarray(
        xt.reshape(DC, 128, NBLK, TB).transpose(2, 1, 0, 3).reshape(NBLK, 128, DC * TB))


def _lam():
    return np.exp2(-0.5 * (np.arange(HQ, dtype=np.float64) + 1))


def _host_tables():
    t = {}
    lam = _lam()
    r = np.arange(CH, dtype=np.float64)
    diagT = np.zeros((64, HQ * 64), np.float32)
    for h in range(HQ):
        dm = np.exp(-lam[h] * (r[None, :] - r[:, None]))
        diagT[:, h * 64:(h + 1) * 64] = np.where(
            r[None, :] >= r[:, None], dm, 0.0).astype(np.float32)
    t["diagT"] = diagT
    qdecb = np.zeros((128, HQ * 64), np.float32)
    for h in range(HQ):
        qdecb[:, h * 64:(h + 1) * 64] = np.exp(-lam[h] * (r + 1.0))[None, :]
    t["qdecb"] = qdecb
    kdec = np.zeros((64, 16), np.float32)
    for h in range(HQ):
        kdec[:, h] = np.exp(-lam[h] * (CH - 1.0 - r))
    t["kdec"] = kdec
    t["ident"] = np.eye(128, dtype=BF)
    t["onesc"] = np.ones((128, 1), BF)
    t["onesr"] = np.ones((1, 128), BF)
    t["oneh"] = np.eye(8, dtype=np.float32)
    onehr = np.zeros((1, 64), BF)
    for e in range(8):
        onehr[0, e * 8 + e] = 1.0
    t["onehr"] = onehr
    selb = np.zeros((8, 8 * 128), BF)
    for e in range(8):
        selb[e, e * 128:(e + 1) * 128] = 1.0
    t["selb"] = selb
    return t


_BLKD = [float(np.exp(-_lam()[h] * CH)) for h in range(HQ)]


def _rope_tables(pos):
    inv = 10000.0 ** (-(np.arange(64, dtype=np.float64)) / 64.0)
    f = pos[None, :].astype(np.float64) * inv[:, None]
    cos = np.cos(f).astype(np.float32)
    sin = np.sin(f).astype(np.float32)
    return (np.concatenate([cos, cos], 0).astype(BF),
            np.concatenate([-sin, sin], 0).astype(BF))



class Plan:
    def __init__(self):
        self.items = []

    def add(self, eng, fn):
        self.items.append((eng, fn))

    def emit(self, name, eng):
        for e, fn in self.items:
            if e == name:
                fn(eng)


class Ctr:
    def __init__(self, nc, name):
        self.sem = nc.ctx.enter_context(nc.semaphore(name))
        self.val = 0


class Bank:
    def __init__(self, nc, name, dtype=F32, width=512):
        self.t = nc.ctx.enter_context(nc.psum_tensor(name, [128, width], dtype))
        self.rdy = Ctr(nc, name + "_r")
        self.fre = Ctr(nc, name + "_f")


class Ring:

    def __init__(self, nc, plan, stack, name, nslots, width, dtype=BF16):
        self.plan = plan
        self.nslots = nslots
        self.width = width
        self.t = stack.enter_context(nc.sbuf_tensor(name, [128, nslots * width], dtype))
        self.wld = [stack.enter_context(nc.semaphore(f"{name}_l{i}")) for i in range(nslots)]
        self.wfr = [stack.enter_context(nc.semaphore(f"{name}_f{i}")) for i in range(nslots)]
        self.n = 0
        self.fr_val = [0] * nslots

    def load(self, src_ap, w=None):
        slot = self.n % self.nslots
        gen = self.n // self.nslots
        self.n += 1
        need_free = self.fr_val[slot]
        wld, wfr, t, width = self.wld[slot], self.wfr[slot], self.t, self.width
        w = width if w is None else w

        def pool_emit(g, slot=slot, gen=gen, src_ap=src_ap, need_free=need_free,
                      wld=wld, wfr=wfr, t=t, width=width, w=w):
            if gen > 0:
                g.wait_ge(wfr, need_free)
            g.dma_start(t[:, slot * width:slot * width + w], src_ap).then_inc(wld, 16)
        self.plan.add("POOL", pool_emit)
        self.fr_val[slot] += 1
        return RingHandle(self, slot, gen, w)


class RingHandle:
    def __init__(self, ring, slot, gen, w):
        self.ring, self.slot, self.gen, self.w = ring, slot, gen, w

    @property
    def ap(self):
        r = self.ring
        return r.t[:, self.slot * r.width:self.slot * r.width + self.w]

    def wait(self, eng):
        eng.wait_ge(self.ring.wld[self.slot], 16 * (self.gen + 1))

    def free_on(self, instr):
        instr.then_inc(self.ring.wfr[self.slot], 1)



_CACHE = {}


def _build():
    nc = bass.Bass("TRN2", target_bir_lowering=False)
    nc.ctx = contextlib.ExitStack()

    xb = nc.dram_tensor("xb", [NBLK, 128, DC * TB], F32, kind="ExternalInput")
    xp = nc.dram_tensor("xp", [NBLK, 128, DC * TB], F32, kind="ExternalInput")
    wqT = nc.dram_tensor("wqT", [16, 128, 2048], F32, kind="ExternalInput")
    wkT = nc.dram_tensor("wkT", [4, 128, 2048], F32, kind="ExternalInput")
    wvR = nc.dram_tensor("wvR", [4, 128, 2048], F32, kind="ExternalInput")
    woT = nc.dram_tensor("woT", [16, 128, 2048], F32, kind="ExternalInput")
    rtr = nc.dram_tensor("rtr", [128, 128], F32, kind="ExternalInput")
    gW = nc.dram_tensor("gW", [NE + 1, 16, 128, 2048], F32, kind="ExternalInput")
    uW = nc.dram_tensor("uW", [NE + 1, 16, 128, 2048], F32, kind="ExternalInput")
    dW = nc.dram_tensor("dW", [NE + 1, 16, 128, 2048], F32, kind="ExternalInput")
    cosb_d = nc.dram_tensor("cosb", [128, T], BF16, kind="ExternalInput")
    sinb_d = nc.dram_tensor("sinb", [128, T], BF16, kind="ExternalInput")
    cosp_d = nc.dram_tensor("cosp", [128, T], BF16, kind="ExternalInput")
    sinp_d = nc.dram_tensor("sinp", [128, T], BF16, kind="ExternalInput")
    diagT_d = nc.dram_tensor("diagTd", [64, HQ * 64], F32, kind="ExternalInput")
    qdecb_d = nc.dram_tensor("qdecbd", [128, HQ * 64], F32, kind="ExternalInput")
    kdec_d = nc.dram_tensor("kdecd", [64, 16], F32, kind="ExternalInput")
    ident_d = nc.dram_tensor("identd", [128, 128], BF16, kind="ExternalInput")
    onesc_d = nc.dram_tensor("onescd", [128, 1], BF16, kind="ExternalInput")
    onesr_d = nc.dram_tensor("onesrd", [1, 128], BF16, kind="ExternalInput")
    oneh_d = nc.dram_tensor("onehd", [8, 8], F32, kind="ExternalInput")
    onehr_d = nc.dram_tensor("onehrd", [1, 64], BF16, kind="ExternalInput")
    selb_d = nc.dram_tensor("selbd", [8, 8 * 128], BF16, kind="ExternalInput")

    out_d = nc.dram_tensor("out", [128, DC * T], F32, kind="ExternalOutput")
    x1d = nc.dram_tensor("x1d", [DC, 128, T], BF16,
                         kind="ExternalOutput" if DEBUG_X1 else "Internal")

    planA = Plan()
    planB = Plan()

    ident = nc.ctx.enter_context(nc.sbuf_tensor("identt", [128, 128], BF16))
    onesc = nc.ctx.enter_context(nc.sbuf_tensor("onescc", [128, 1], BF16))
    onesr = nc.ctx.enter_context(nc.sbuf_tensor("onesrr", [1, 128], BF16))
    oneh = nc.ctx.enter_context(nc.sbuf_tensor("onehh", [8, 8], F32))
    onehr = nc.ctx.enter_context(nc.sbuf_tensor("onehr", [1, 64], BF16))
    selb = nc.ctx.enter_context(nc.sbuf_tensor("selb", [8, 8 * 128], BF16))
    logit_sb = nc.ctx.enter_context(nc.sbuf_tensor("logit_sb", [8, T], F32))
    mscP = nc.ctx.enter_context(nc.sbuf_tensor("mscP", [8, T], BF16))

    pbk = [Bank(nc, f"pb{i}") for i in range(7)]
    pbt = Bank(nc, "pbt", dtype=BF16, width=1024)

    c_kvs = Ctr(nc, "c_kvs")
    c_khat = Ctr(nc, "c_khat")
    c_ssc = Ctr(nc, "c_ssc")
    c_rope = Ctr(nc, "c_rope")
    c_krope = Ctr(nc, "c_krope")
    c_vtm = Ctr(nc, "c_vtm")
    c_oev = Ctr(nc, "c_oev")
    c_och = Ctr(nc, "c_och")
    c_dch = Ctr(nc, "c_dch")
    c_pre = Ctr(nc, "c_pre")
    c_sq = Ctr(nc, "c_sq")
    c_ln = Ctr(nc, "c_ln")
    c_x1a = Ctr(nc, "c_x1a")
    c_x1st = Ctr(nc, "c_x1st")
    c_msk = Ctr(nc, "c_msk")
    c_hb = Ctr(nc, "c_hb")
    c_silu = Ctr(nc, "c_silu")
    c_msb = Ctr(nc, "c_msb")
    c_msbf = Ctr(nc, "c_msbf")
    c_yw = Ctr(nc, "c_yw")
    c_ln2 = Ctr(nc, "c_ln2")
    c_out = Ctr(nc, "c_out")
    c_ost = Ctr(nc, "c_ost")
    c_x1ld = Ctr(nc, "c_x1ld")
    c_cst = Ctr(nc, "c_cst")
    c_vsy = Ctr(nc, "c_vsy")
    c_sqr = Ctr(nc, "c_sqr")
    c_ostp = [Ctr(nc, "c_ostp0"), Ctr(nc, "c_ostp1")]
    c_l2w = Ctr(nc, "c_l2w")
    c_sq2w = Ctr(nc, "c_sq2w")
    c_st2r = Ctr(nc, "c_st2r")

    def mkvs(n):
        base = c_vsy.val
        c_vsy.val += n
        return base

    def VS(v, base, i):
        v.engine_nop().then_inc(c_vsy.sem, 1)
        v.wait_ge(c_vsy.sem, base + i + 1)

    scopeA = contextlib.ExitStack()
    sbA = lambda n, s, d=BF16: scopeA.enter_context(nc.sbuf_tensor(n, s, d))
    x_sl = [sbA(f"x_sl{i}", [128, DC * TB]) for i in range(2)]
    x_ld = [Ctr(nc, f"x_ld{i}") for i in range(2)]
    x_fr = [Ctr(nc, f"x_fr{i}") for i in range(2)]
    ring = Ring(nc, planA, scopeA, "wring", 4, 2048)
    q_fm = sbA("q_fm", [128, HQ * TB])
    qhat = sbA("qhat", [128, HQ * TB])
    k_fm = sbA("k_fm", [128, HKV * TB])
    v_tm = sbA("v_tm", [64, CPB * 512])
    khat = sbA("khat", [64, 2 * HQ * 128])
    ssc = sbA("sscb", [64, 2 * HQ * 64])
    o_fm = sbA("o_fm", [128, HQ * TB])
    kv_bf = sbA("kv_bf", [128, HQ * DH])
    pre = sbA("preb", [128, DC * TB])
    sqb = sbA("sqb", [128, 2 * TB])
    rs1 = sbA("rs1", [128, TB], F32)
    rs2 = sbA("rs2", [128, TB], F32)
    rs3 = sbA("rs3", [128, TB], F32)
    lnrow = sbA("lnrow", [1, 7 * TB], F32)
    lnbfr = sbA("lnbfr", [1, 2 * TB])
    rbb = sbA("rbb", [128, 2 * TB])
    cos_o = sbA("cos_o", [128, T])
    sin_o = sbA("sin_o", [128, T])
    cos_p = sbA("cos_p", [128, T])
    sin_p = sbA("sin_p", [128, T])
    diagT = sbA("diagTs", [64, HQ * 64], F32)
    qdecb = sbA("qdecbs", [128, HQ * 64], F32)
    kdec = sbA("kdecs", [64, 16], F32)
    x1b = pre

    def _bc8(ec):
        return bass.AP(qdecb, ec * 64, [[HQ * 64, 128], [0, 8], [1, 64]])

    def emit_const(sy):
        for dst, src in [(ident, ident_d), (onesc, onesc_d), (onesr, onesr_d),
                         (oneh, oneh_d), (onehr, onehr_d), (selb, selb_d),
                         (cos_o, cosb_d), (sin_o, sinb_d),
                         (cos_p, cosp_d), (sin_p, sinp_d), (diagT, diagT_d),
                         (qdecb, qdecb_d), (kdec, kdec_d)]:
            sy.dma_start(dst[:, :], src.ap()).then_inc(c_cst.sem, 16)
    planA.add("SYNC", emit_const)
    c_cst.val += 16 * 13
    CST = c_cst.val

    def emit_kv0(v):
        v.wait_ge(c_cst.sem, CST)
        v.memset(kv_bf[:, :], 0.0)
        v.engine_nop().then_inc(c_kvs.sem, 1)
    planA.add("DVE", emit_kv0)
    c_kvs.val += 1

    n_own_chunk = 0

    for bi in range(min(NBI, 2 * NBLK)):
        warm = bi < NBLK
        blk = bi % NBLK
        xslot = bi % 2
        xsrc = xp if warm else xb
        xs = x_sl[xslot]
        cos_t, sin_t = (cos_p, sin_p) if warm else (cos_o, sin_o)

        need_xfree = x_fr[xslot].val

        def pool_x(g, bi=bi, xsrc=xsrc, blk=blk, need_xfree=need_xfree,
                   xslot=xslot):
            if bi > 1:
                g.wait_ge(x_fr[xslot].sem, need_xfree)
            g.wait_ge(c_cst.sem, CST)
            g.dma_start(x_sl[xslot][:, :], xsrc.ap()[blk, :, :]).then_inc(
                x_ld[xslot].sem, 16)
        planA.add("POOL", pool_x)
        x_ld[xslot].val += 16
        XNEED = x_ld[xslot].val

        if not warm:
            for ec in range(HQ):
                h = ring.load(wqT.ap()[ec, :, :])
                bank = pbk[ec % 2]
                gen0 = bank.fre.val

                RDY = bank.rdy.val + 1

                def pe_q(t, h=h, bank=bank, gen0=gen0, xs=xs, XNEED=XNEED,
                         xslot=xslot, RDY=RDY):
                    h.wait(t)
                    t.wait_ge(x_ld[xslot].sem, XNEED)
                    if gen0 > 0:
                        t.wait_ge(bank.fre.sem, gen0)
                    ins = None
                    for dc in range(DC):
                        ins = t.matmul(bank.t[:, :],
                                       h.ap[:, dc * 128:(dc + 1) * 128],
                                       xs[:, dc * TB:(dc + 1) * TB],
                                       start=(dc == 0), stop=(dc == DC - 1))
                    ins.then_inc(bank.rdy.sem, 1)
                    t.wait_ge(bank.rdy.sem, RDY)
                    h.free_on(t.nop())
                planA.add("PE", pe_q)
                bank.rdy.val += 1
                bank.fre.val += 1

                vs0 = mkvs(4)

                def dve_rope(v, bank=bank, ec=ec, blk=blk, RDY=RDY,
                             cos_t=cos_t, sin_t=sin_t, vs0=vs0):
                    v.wait_ge(bank.rdy.sem, RDY)
                    VS(v, vs0, 0)
                    ps = bank.t
                    v.tensor_copy(rs1[64:128, :], ps[0:64, :])
                    v.tensor_copy(rs1[0:64, :], ps[64:128, :])
                    v.tensor_tensor(rs2[:, :], ps[:, :],
                                    cos_t[:, blk * TB:(blk + 1) * TB],
                                    ALU.mult).then_inc(bank.fre.sem, 1)
                    VS(v, vs0, 1)
                    v.tensor_tensor(rs3[:, :], rs1[:, :],
                                    sin_t[:, blk * TB:(blk + 1) * TB], ALU.mult)
                    VS(v, vs0, 2)
                    v.tensor_tensor(q_fm[:, ec * TB:(ec + 1) * TB],
                                    rs2[:, :], rs3[:, :], ALU.add)
                    VS(v, vs0, 3)
                    if NO_QHAT:
                        v.tensor_copy(qhat[:, ec * TB:(ec + 1) * TB],
                                      q_fm[:, ec * TB:(ec + 1) * TB])
                        v.engine_nop().then_inc(c_rope.sem, 1)
                    else:
                        v.tensor_tensor(qhat[:, ec * TB:(ec + 1) * TB],
                                        q_fm[:, ec * TB:(ec + 1) * TB],
                                        _bc8(ec), ALU.mult).then_inc(c_rope.sem, 1)
                planA.add("DVE", dve_rope)
                c_rope.val += 1

        for ec in range(HKV):
            h = ring.load(wkT.ap()[ec, :, :])
            bank = pbk[ec % 2]
            gen0 = bank.fre.val

            RDY = bank.rdy.val + 1

            def pe_k(t, h=h, bank=bank, gen0=gen0, xs=xs, XNEED=XNEED,
                     xslot=xslot, RDY=RDY):
                h.wait(t)
                t.wait_ge(x_ld[xslot].sem, XNEED)
                if gen0 > 0:
                    t.wait_ge(bank.fre.sem, gen0)
                ins = None
                for dc in range(DC):
                    ins = t.matmul(bank.t[:, :],
                                   h.ap[:, dc * 128:(dc + 1) * 128],
                                   xs[:, dc * TB:(dc + 1) * TB],
                                   start=(dc == 0), stop=(dc == DC - 1))
                ins.then_inc(bank.rdy.sem, 1)
                t.wait_ge(bank.rdy.sem, RDY)
                h.free_on(t.nop())
            planA.add("PE", pe_k)
            bank.rdy.val += 1
            bank.fre.val += 1

            vs0 = mkvs(3)

            def dve_krope(v, bank=bank, ec=ec, blk=blk, RDY=RDY,
                          cos_t=cos_t, sin_t=sin_t, vs0=vs0):
                v.wait_ge(bank.rdy.sem, RDY)
                VS(v, vs0, 0)
                ps = bank.t
                v.tensor_copy(rs1[64:128, :], ps[0:64, :])
                v.tensor_copy(rs1[0:64, :], ps[64:128, :])
                v.tensor_tensor(rs2[:, :], ps[:, :],
                                cos_t[:, blk * TB:(blk + 1) * TB],
                                ALU.mult).then_inc(bank.fre.sem, 1)
                VS(v, vs0, 1)
                v.tensor_tensor(rs3[:, :], rs1[:, :],
                                sin_t[:, blk * TB:(blk + 1) * TB], ALU.mult)
                VS(v, vs0, 2)
                v.tensor_tensor(k_fm[:, ec * TB:(ec + 1) * TB],
                                rs2[:, :], rs3[:, :], ALU.add).then_inc(
                    c_krope.sem, 1)
            planA.add("DVE", dve_krope)
            c_krope.val += 1

        vh = [ring.load(wvR.ap()[j, :, :]) for j in range(4)]
        for tc in range(4):
            bank = pbk[2 + tc % 2]
            gen0 = bank.fre.val

            RDY = bank.rdy.val + 1

            def pe_v(t, vh=vh, bank=bank, gen0=gen0, tc=tc, xs=xs, XNEED=XNEED,
                     xslot=xslot, RDY=RDY):
                if tc == 0:
                    for hh in vh:
                        hh.wait(t)
                    t.wait_ge(x_ld[xslot].sem, XNEED)
                if gen0 > 0:
                    t.wait_ge(bank.fre.sem, gen0)
                ins = None
                for dc in range(DC):
                    ins = t.matmul(
                        bank.t[:, :],
                        xs[:, dc * TB + tc * 128: dc * TB + (tc + 1) * 128],
                        vh[dc // 4].ap[:, (dc % 4) * 512:(dc % 4 + 1) * 512],
                        start=(dc == 0), stop=(dc == DC - 1))
                ins.then_inc(bank.rdy.sem, 1)
                if tc == 3:
                    t.wait_ge(bank.rdy.sem, RDY)
                    for hh in vh:
                        hh.free_on(t.nop())
            planA.add("PE", pe_v)
            bank.rdy.val += 1
            bank.fre.val += 1

            FREV = bank.fre.val

            def act_vev(s, bank=bank, tc=tc, RDY=RDY, FREV=FREV):
                s.wait_ge(bank.rdy.sem, RDY)
                s.copy(v_tm[:, (2 * tc) * 512:(2 * tc + 1) * 512],
                       bank.t[0:64, :])
                s.copy(v_tm[:, (2 * tc + 1) * 512:(2 * tc + 2) * 512],
                       bank.t[64:128, :]).then_inc(bank.fre.sem, 1)
                s.wait_ge(bank.fre.sem, FREV)
                s.nop().then_inc(c_vtm.sem, 1)
            planA.add("ACT", act_vev)
            c_vtm.val += 1

        if warm:
            vr_last = pbk[3].rdy.val

            def pe_xfree(t, xslot=xslot, vr_last=vr_last):
                t.wait_ge(pbk[3].rdy.sem, vr_last)
                t.nop().then_inc(x_fr[xslot].sem, 1)
            planA.add("PE", pe_xfree)
            x_fr[xslot].val += 1

        KR = c_krope.val
        VR = c_vtm.val
        for cc in range(0 if NO_LGT else CPB):
            gchunk = bi * CPB + cc
            kv_base = c_kvs.val

            genv = pbt.fre.val

            def pe_tr(t, cc=cc, genv=genv, KR=KR):
                t.wait_ge(c_krope.sem, KR)
                if genv > 0:
                    t.wait_ge(pbt.fre.sem, genv)
                ins = None
                for g in range(4):
                    ins = t.transpose(
                        pbt.t[0:64, g * 128:(g + 1) * 128],
                        k_fm[:, g * TB + cc * 64: g * TB + (cc + 1) * 64],
                        ident[:, :])
                ins.then_inc(pbt.rdy.sem, 1)
            planA.add("PE", pe_tr)
            pbt.rdy.val += 1
            pbt.fre.val += 1
            TRDY = pbt.rdy.val
            DCHW = max(0, gchunk - 1)

            def dve_khat(v, cc=cc, TRDY=TRDY, DCHW=DCHW):
                v.wait_ge(pbt.rdy.sem, TRDY)
                if DCHW > 0:
                    v.wait_ge(c_dch.sem, DCHW)
                ins = None
                for h in range(HQ):
                    g = h // 4
                    ins = v.tensor_scalar(
                        khat[0:64, ((cc % 2) * HQ + h) * 128:
                             ((cc % 2) * HQ + h + 1) * 128],
                        pbt.t[0:64, g * 128:(g + 1) * 128],
                        kdec[:, h:h + 1], None, ALU.mult)
                ins.then_inc(pbt.fre.sem, 1)
                v.engine_nop().then_inc(c_khat.sem, 1)
            planA.add("DVE", dve_khat)
            c_khat.val += 1
            KHAT_CC = c_khat.val

            if not warm:
                QR = c_rope.val
                for hg in range(2):
                    bank = pbk[hg]
                    gen0 = bank.fre.val

                    def pe_st(t, bank=bank, hg=hg, cc=cc, gen0=gen0, QR=QR):
                        t.wait_ge(c_rope.sem, QR)
                        if gen0 > 0:
                            t.wait_ge(bank.fre.sem, gen0)
                        ins = None
                        for hh in range(8):
                            h = hg * 8 + hh
                            g = h // 4
                            ins = t.matmul(
                                bank.t[0:64, hh * 64:(hh + 1) * 64],
                                k_fm[:, g * TB + cc * 64: g * TB + (cc + 1) * 64],
                                q_fm[:, h * TB + cc * 64: h * TB + (cc + 1) * 64],
                                start=True, stop=True)
                        ins.then_inc(bank.rdy.sem, 1)
                    planA.add("PE", pe_st)
                    bank.rdy.val += 1
                    bank.fre.val += 1
                    RDY = bank.rdy.val
                    OCH = max(0, n_own_chunk - 1)

                    def dve_diag(v, bank=bank, hg=hg, cc=cc, RDY=RDY, OCH=OCH):
                        v.wait_ge(bank.rdy.sem, RDY)
                        if OCH > 0:
                            v.wait_ge(c_och.sem, OCH)
                        ins = None
                        for hh in range(8):
                            h = hg * 8 + hh
                            ins = v.tensor_tensor(
                                ssc[0:64, ((cc % 2) * HQ + h) * 64:
                                    ((cc % 2) * HQ + h + 1) * 64],
                                bank.t[0:64, hh * 64:(hh + 1) * 64],
                                diagT[:, h * 64:(h + 1) * 64], ALU.mult)
                        ins.then_inc(bank.fre.sem, 1)
                        v.engine_nop().then_inc(c_ssc.sem, 1)
                    planA.add("DVE", dve_diag)
                    c_ssc.val += 1

                SSC = c_ssc.val
                for hg in range(2):
                    bank = pbk[2 + hg]
                    gen0 = bank.fre.val

                    def pe_o(t, bank=bank, hg=hg, cc=cc,
                             gen0=gen0, SSC=SSC, kv_base=kv_base):
                        t.wait_ge(c_ssc.sem, SSC)
                        t.wait_ge(c_kvs.sem, kv_base)
                        if gen0 > 0:
                            t.wait_ge(bank.fre.sem, gen0)
                        ins = None
                        for hh in range(8):
                            h = hg * 8 + hh
                            g = h // 4
                            t.matmul(
                                bank.t[:, hh * 64:(hh + 1) * 64],
                                v_tm[:, cc * 512 + g * 128:
                                     cc * 512 + (g + 1) * 128],
                                ssc[0:64, ((cc % 2) * HQ + h) * 64:
                                    ((cc % 2) * HQ + h + 1) * 64],
                                start=True, stop=False)
                            ins = t.matmul(
                                bank.t[:, hh * 64:(hh + 1) * 64],
                                kv_bf[:, h * DH:(h + 1) * DH],
                                qhat[:, h * TB + cc * 64: h * TB + (cc + 1) * 64],
                                start=False, stop=True)
                        ins.then_inc(bank.rdy.sem, 1)
                    planA.add("PE", pe_o)
                    bank.rdy.val += 1
                    bank.fre.val += 1
                    RDY = bank.rdy.val

                    FREV = bank.fre.val

                    def act_oev(s, bank=bank, hg=hg, cc=cc, RDY=RDY, FREV=FREV):
                        s.wait_ge(bank.rdy.sem, RDY)
                        dst = bass.AP(o_fm, hg * 8 * TB + cc * 64,
                                      [[HQ * TB, 128], [TB, 8], [1, 64]])
                        s.copy(dst, bank.t[:, :]).then_inc(bank.fre.sem, 1)
                        s.wait_ge(bank.fre.sem, FREV)
                        s.nop().then_inc(c_oev.sem, 1)
                    planA.add("ACT", act_oev)
                    c_oev.val += 1

                o_rdy_last = pbk[3].rdy.val

                def pe_och(t, o_rdy_last=o_rdy_last):
                    t.wait_ge(pbk[3].rdy.sem, o_rdy_last)
                    t.nop().then_inc(c_och.sem, 1)
                planA.add("PE", pe_och)
                c_och.val += 1
                n_own_chunk += 1

            KH = KHAT_CC
            for quad in range(4):
                bank = pbk[6]
                gen0 = bank.fre.val

                def pe_delta(t, bank=bank, quad=quad, cc=cc,
                             gen0=gen0, KH=KH, VR=VR):
                    t.wait_ge(c_khat.sem, KH)
                    t.wait_ge(c_vtm.sem, VR)
                    if gen0 > 0:
                        t.wait_ge(bank.fre.sem, gen0)
                    ins = None
                    for r in range(4):
                        h = quad * 4 + r
                        g = h // 4
                        ins = t.matmul(
                            bank.t[:, r * 128:(r + 1) * 128],
                            khat[0:64, ((cc % 2) * HQ + h) * 128:
                                 ((cc % 2) * HQ + h + 1) * 128],
                            v_tm[:, cc * 512 + g * 128: cc * 512 + (g + 1) * 128],
                            start=True, stop=True)
                    ins.then_inc(bank.rdy.sem, 1)
                planA.add("PE", pe_delta)
                bank.rdy.val += 1
                bank.fre.val += 1
                RDY = bank.rdy.val

                def dve_kv(v, bank=bank, quad=quad, RDY=RDY,
                           kv_base=kv_base):
                    v.wait_ge(bank.rdy.sem, RDY)
                    if kv_base > 1:
                        v.wait_ge(c_kvs.sem, kv_base)
                    ins = None
                    for r in range(4):
                        h = quad * 4 + r
                        ins = v.scalar_tensor_tensor(
                            kv_bf[:, h * DH:(h + 1) * DH],
                            kv_bf[:, h * DH:(h + 1) * DH],
                            _BLKD[h],
                            bank.t[:, r * 128:(r + 1) * 128],
                            ALU.mult, ALU.add)
                        ins.then_inc(c_kvs.sem, 1)
                    v.engine_nop().then_inc(bank.fre.sem, 1)
                planA.add("DVE", dve_kv)
                c_kvs.val += 4

            d_rdy_last = pbk[6].rdy.val

            def pe_dch(t, d_rdy_last=d_rdy_last):
                t.wait_ge(pbk[6].rdy.sem, d_rdy_last)
                t.nop().then_inc(c_dch.sem, 1)
            planA.add("PE", pe_dch)
            c_dch.val += 1

        if warm:
            continue

        OEV = c_oev.val
        X1ST_prev = c_x1st.val
        woh = [ring.load(woT.ap()[dc, :, :]) for dc in range(DC)]
        for dc in range(DC):
            h = woh[dc]
            bank = pbk[dc % 2]
            gen0 = bank.fre.val

            RDY = bank.rdy.val + 1

            def pe_oproj(t, h=h, bank=bank, dc=dc, gen0=gen0, OEV=OEV, RDY=RDY):
                h.wait(t)
                if dc == 0:
                    t.wait_ge(c_oev.sem, OEV)
                if gen0 > 0:
                    t.wait_ge(bank.fre.sem, gen0)
                ins = None
                for ec in range(16):
                    ins = t.matmul(bank.t[:, :],
                                   h.ap[:, ec * 128:(ec + 1) * 128],
                                   o_fm[:, ec * TB:(ec + 1) * TB],
                                   start=(ec == 0), stop=(ec == 15))
                ins.then_inc(bank.rdy.sem, 1)
                t.wait_ge(bank.rdy.sem, RDY)
                h.free_on(t.nop())
            planA.add("PE", pe_oproj)
            bank.rdy.val += 1
            bank.fre.val += 1

            def dve_pre(v, bank=bank, dc=dc, xs=xs, RDY=RDY, X1ST_prev=X1ST_prev):
                v.wait_ge(bank.rdy.sem, RDY)
                if dc == 0 and X1ST_prev > 0:
                    v.wait_ge(c_x1st.sem, X1ST_prev)
                v.tensor_tensor(pre[:, dc * TB:(dc + 1) * TB],
                                xs[:, dc * TB:(dc + 1) * TB],
                                bank.t[:, :], ALU.add).then_inc(
                    bank.fre.sem, 1)
                v.engine_nop().then_inc(c_pre.sem, 1)
            planA.add("DVE", dve_pre)
            c_pre.val += 1

        def dve_xfree(v, xslot=xslot):
            v.engine_nop().then_inc(x_fr[xslot].sem, 1)
        planA.add("DVE", dve_xfree)
        x_fr[xslot].val += 1

        PRE_ALL = c_pre.val
        g4 = pbk[4].fre.val
        g5 = pbk[5].fre.val
        sq_base = c_sq.val
        sqr_base = c_sqr.val
        for dc in range(DC):
            def act_sq(s, dc=dc, PRE_ALL=PRE_ALL, sqr_base=sqr_base):
                s.wait_ge(c_pre.sem, PRE_ALL - (DC - 1) + dc)
                if dc >= 2:
                    s.wait_ge(c_sqr.sem, sqr_base + dc - 1)
                s.square(sqb[:, (dc % 2) * TB:(dc % 2 + 1) * TB],
                         pre[:, dc * TB:(dc + 1) * TB]).then_inc(c_sq.sem, 1)
            planA.add("ACT", act_sq)
            c_sq.val += 1
            SQ = c_sq.val

            def pe_stats(t, dc=dc, SQ=SQ, PRE_ALL=PRE_ALL, g4=g4, g5=g5,
                         sqr_base=sqr_base):
                if dc == 0:
                    t.wait_ge(c_pre.sem, PRE_ALL)
                    if g4 > 0:
                        t.wait_ge(pbk[4].fre.sem, g4)
                    if g5 > 0:
                        t.wait_ge(pbk[5].fre.sem, g5)
                t.matmul(pbk[4].t[0:1, :], onesc[:, :],
                         pre[:, dc * TB:(dc + 1) * TB],
                         start=(dc == 0), stop=(dc == 15))
                t.wait_ge(c_sq.sem, SQ)
                ins = t.matmul(pbk[5].t[0:1, :], onesc[:, :],
                               sqb[:, (dc % 2) * TB:(dc % 2 + 1) * TB],
                               start=(dc == 0), stop=(dc == 15))
                ins.then_inc(c_sqr.sem, 1)
                if dc == 15:
                    t.wait_ge(c_sqr.sem, sqr_base + 16)
                    t.nop().then_inc(pbk[5].rdy.sem, 1)
            planA.add("PE", pe_stats)
            c_sqr.val += 1
        pbk[4].fre.val += 1
        pbk[5].rdy.val += 1
        pbk[5].fre.val += 1
        SRDY = pbk[5].rdy.val

        vs0 = mkvs(3)

        def dve_lnstats(v, SRDY=SRDY, vs0=vs0):
            v.wait_ge(pbk[5].rdy.sem, SRDY)
            v.tensor_scalar(lnrow[0:1, 0:TB], pbk[4].t[0:1, :], RN, None,
                            ALU.mult).then_inc(pbk[4].fre.sem, 1)
            v.tensor_scalar(lnrow[0:1, 5 * TB:6 * TB], pbk[5].t[0:1, :], RN,
                            None, ALU.mult).then_inc(pbk[5].fre.sem, 1)
            VS(v, vs0, 0)
            v.tensor_tensor(lnrow[0:1, TB:2 * TB], lnrow[0:1, 0:TB],
                            lnrow[0:1, 0:TB], ALU.mult)
            VS(v, vs0, 1)
            v.tensor_tensor(lnrow[0:1, 2 * TB:3 * TB], lnrow[0:1, 5 * TB:6 * TB],
                            lnrow[0:1, TB:2 * TB], ALU.subtract)
            VS(v, vs0, 2)
            v.tensor_scalar(lnrow[0:1, 2 * TB:3 * TB],
                            lnrow[0:1, 2 * TB:3 * TB], EPS, None,
                            ALU.add).then_inc(c_ln.sem, 1)
        planA.add("DVE", dve_lnstats)
        c_ln.val += 1
        LN1 = c_ln.val

        def act_lnsqrt(s, LN1=LN1):
            s.wait_ge(c_ln.sem, LN1)
            s.activation(lnrow[0:1, 3 * TB:4 * TB], lnrow[0:1, 2 * TB:3 * TB],
                         AF.Sqrt).then_inc(c_ln.sem, 1)
        planA.add("ACT", act_lnsqrt)
        c_ln.val += 1
        LNS = c_ln.val

        vs0 = mkvs(2)

        def dve_lnrecip(v, LNS=LNS, vs0=vs0):
            v.wait_ge(c_ln.sem, LNS)
            v.reciprocal(lnrow[0:1, 4 * TB:5 * TB], lnrow[0:1, 3 * TB:4 * TB])
            VS(v, vs0, 0)
            v.tensor_tensor(lnrow[0:1, 6 * TB:7 * TB], lnrow[0:1, 0:TB],
                            lnrow[0:1, 4 * TB:5 * TB], ALU.mult)
            VS(v, vs0, 1)
            v.tensor_copy(lnbfr[0:1, 0:TB], lnrow[0:1, 4 * TB:5 * TB])
            v.tensor_copy(lnbfr[0:1, TB:2 * TB], lnrow[0:1, 6 * TB:7 * TB])
            v.engine_nop().then_inc(c_ln.sem, 1)
        planA.add("DVE", dve_lnrecip)
        c_ln.val += 1
        LN3 = c_ln.val

        for j in range(2):
            bank = pbk[2 + j]
            gen0 = bank.fre.val

            def pe_bc(t, j=j, bank=bank, gen0=gen0, LN3=LN3):
                t.wait_ge(c_ln.sem, LN3)
                if gen0 > 0:
                    t.wait_ge(bank.fre.sem, gen0)
                t.matmul(bank.t[:, :], onesr[:, :],
                         lnbfr[0:1, j * TB:(j + 1) * TB],
                         start=True, stop=True).then_inc(bank.rdy.sem, 1)
            planA.add("PE", pe_bc)
            bank.rdy.val += 1
            bank.fre.val += 1
            RDY = bank.rdy.val

            FREV = bank.fre.val

            def act_bc(s, j=j, bank=bank, RDY=RDY, FREV=FREV):
                s.wait_ge(bank.rdy.sem, RDY)
                s.copy(rbb[:, j * TB:(j + 1) * TB], bank.t[:, :]).then_inc(
                    bank.fre.sem, 1)
                s.wait_ge(bank.fre.sem, FREV)
                s.nop().then_inc(c_ln.sem, 1)
            planA.add("ACT", act_bc)
            c_ln.val += 1
        LN4 = c_ln.val

        X1ST0 = c_x1st.val
        for dc in range(DC):
            vs0 = mkvs(1)

            X1P = c_x1a.val

            def dve_x1(v, dc=dc, LN4=LN4, X1ST0=X1ST0, vs0=vs0, X1P=X1P):
                if dc == 0:
                    v.wait_ge(c_ln.sem, LN4)
                    if X1ST0 > 0:
                        v.wait_ge(c_x1st.sem, X1ST0)
                else:
                    v.wait_ge(c_x1a.sem, X1P)
                v.tensor_tensor(rs2[:, :], pre[:, dc * TB:(dc + 1) * TB],
                                rbb[:, 0:TB], ALU.mult)
                VS(v, vs0, 0)
                v.tensor_tensor(x1b[:, dc * TB:(dc + 1) * TB], rs2[:, :],
                                rbb[:, TB:2 * TB], ALU.subtract).then_inc(
                    c_x1a.sem, 1)
            planA.add("DVE", dve_x1)
            c_x1a.val += 1
            X1A = c_x1a.val

            def sy_x1st(sy, dc=dc, blk=blk, X1A=X1A):
                sy.wait_ge(c_x1a.sem, X1A)
                sy.dma_start(x1d.ap()[dc, :, blk * TB:(blk + 1) * TB],
                             x1b[:, dc * TB:(dc + 1) * TB]).then_inc(
                    c_x1st.sem, 16)
            planA.add("SYNC", sy_x1st)
            c_x1st.val += 16

        rh = ring.load(rtr.ap()[:, :], w=128)
        bank = pbk[6]
        gen0 = bank.fre.val
        X1A_ALL = c_x1a.val

        RRDY = bank.rdy.val + 1

        def pe_router(t, rh=rh, bank=bank, gen0=gen0, X1A_ALL=X1A_ALL,
                      RRDY=RRDY):
            rh.wait(t)
            t.wait_ge(c_x1a.sem, X1A_ALL)
            if gen0 > 0:
                t.wait_ge(bank.fre.sem, gen0)
            ins = None
            for dc in range(DC):
                ins = t.matmul(bank.t[0:8, :], rh.ap[:, dc * 8:(dc + 1) * 8],
                               x1b[:, dc * TB:(dc + 1) * TB],
                               start=(dc == 0), stop=(dc == 15))
            ins.then_inc(bank.rdy.sem, 1)
            t.wait_ge(bank.rdy.sem, RRDY)
            rh.free_on(t.nop())
        planA.add("PE", pe_router)
        bank.rdy.val += 1
        bank.fre.val += 1

        def dve_logit(v, bank=bank, blk=blk, RRDY=RRDY):
            v.wait_ge(bank.rdy.sem, RRDY)
            v.tensor_scalar(logit_sb[:, blk * TB:(blk + 1) * TB],
                            bank.t[0:8, :], 0.0, None, ALU.add).then_inc(
                bank.fre.sem, 1)
        planA.add("DVE", dve_logit)

    X1ST_ALL = c_x1st.val

    def sy_enda(sy):
        sy.wait_ge(c_x1st.sem, X1ST_ALL)
    planA.add("SYNC", sy_enda)

    if DEBUG2:
        dbg_d = {
            "dbg_q": nc.dram_tensor("dq", [128, HQ * TB], BF16, kind="ExternalOutput"),
            "dbg_k": nc.dram_tensor("dk", [128, HKV * TB], BF16, kind="ExternalOutput"),
            "dbg_qh": nc.dram_tensor("dqh", [128, HQ * TB], BF16, kind="ExternalOutput"),
            "dbg_o": nc.dram_tensor("do", [128, HQ * TB], BF16, kind="ExternalOutput"),
            "dbg_kv": nc.dram_tensor("dkv", [128, HQ * DH], BF16, kind="ExternalOutput"),
            "dbg_pre": nc.dram_tensor("dpre", [128, DC * TB], BF16, kind="ExternalOutput"),
            "dbg_vtm": nc.dram_tensor("dvt", [64, CPB * 512], BF16, kind="ExternalOutput"),
            "dbg_lg": nc.dram_tensor("dlg", [8, T], F32, kind="ExternalOutput"),
        }

        def sy_dbg(sy):
            sy.dma_start(dbg_d["dbg_q"].ap(), q_fm[:, :]).then_inc(c_x1st.sem, 16)
            sy.dma_start(dbg_d["dbg_k"].ap(), k_fm[:, :]).then_inc(c_x1st.sem, 16)
            sy.dma_start(dbg_d["dbg_qh"].ap(), qhat[:, :]).then_inc(c_x1st.sem, 16)
            sy.dma_start(dbg_d["dbg_o"].ap(), o_fm[:, :]).then_inc(c_x1st.sem, 16)
            sy.dma_start(dbg_d["dbg_kv"].ap(), kv_bf[:, :]).then_inc(c_x1st.sem, 16)
            sy.dma_start(dbg_d["dbg_pre"].ap(), pre[:, :]).then_inc(c_x1st.sem, 16)
            sy.dma_start(dbg_d["dbg_vtm"].ap(), v_tm[:, :]).then_inc(c_x1st.sem, 16)
            sy.dma_start(dbg_d["dbg_lg"].ap(), logit_sb[:, :]).then_inc(c_x1st.sem, 16)
            sy.wait_ge(c_x1st.sem, X1ST_ALL + 16 * 8)
        planA.add("SYNC", sy_dbg)
        c_x1st.val += 16 * 8
    scopeA.close()

    if SKIP_B:
        with nc.Block("phaseA") as blk:
            @blk.tensor
            def _(t):
                planA.emit("PE", t)

            @blk.vector
            def _(v):
                planA.emit("DVE", v)

            @blk.scalar
            def _(s):
                planA.emit("ACT", s)

            @blk.sync
            def _(sy):
                planA.emit("SYNC", sy)

            @blk.gpsimd
            def _(g):
                planA.emit("POOL", g)
        nc.ctx.close()
        return nc

    scopeB = contextlib.ExitStack()
    sbB = lambda n, s, d=BF16: scopeB.enter_context(nc.sbuf_tensor(n, s, d))
    ringB = Ring(nc, planB, scopeB, "wringB", 5, 2048)
    x1th = sbB("x1th", [128, DC * THW])
    y_acc = sbB("y_acc", [128, DC * THW])
    h_buf = [sbB(f"h_buf{i}", [128, FC * TSW]) for i in range(TS)]
    msb = sbB("msbb", [128, 2 * THW])
    tmpA = [sbB(f"tmpA{i}", [128, 2 * TSW]) for i in range(2)]
    tmpSA = [sbB(f"tmpSA{i}", [128, 2 * TSW]) for i in range(2)]
    tmpB = [sbB(f"tmpB{i}", [128, 2 * TSW]) for i in range(2)]
    l2x = [sbB(f"l2x{i}", [128, THW]) for i in range(2)]
    sq2 = [sbB(f"sq2_{i}", [128, THW]) for i in range(2)]
    rowsc = sbB("rowsc", [1, 16 * 512], F32)
    eqs = sbB("eqs", [1, 8 * 512])
    rb2 = sbB("rb2", [128, 2 * THW])
    ostg = [sbB(f"ostg{i}", [128, THW], F32) for i in range(2)]

    def rsl(i, n=1):
        return rowsc[0:1, i * 512:(i + n) * 512]

    for gq in range(0 if NOMASK else 4):
        for e in range(NE):
            bank = pbk[5 + (gq * 8 + e) % 2]
            gen0 = bank.fre.val

            def pe_mrow(t, e=e, gq=gq, bank=bank, gen0=gen0):
                if gen0 > 0:
                    t.wait_ge(bank.fre.sem, gen0)
                t.matmul(bank.t[0:1, :], oneh[:, e:e + 1],
                         logit_sb[:, gq * 512:(gq + 1) * 512],
                         start=True, stop=True).then_inc(bank.rdy.sem, 1)
            planB.add("PE", pe_mrow)
            bank.rdy.val += 1
            bank.fre.val += 1
            RDY = bank.rdy.val

            FREV = bank.fre.val

            def act_mrow(s, e=e, bank=bank, RDY=RDY, FREV=FREV):
                s.wait_ge(bank.rdy.sem, RDY)
                s.copy(rsl(e), bank.t[0:1, :]).then_inc(bank.fre.sem, 1)
                s.wait_ge(bank.fre.sem, FREV)
                s.nop().then_inc(c_msk.sem, 1)
            planB.add("ACT", act_mrow)
            c_msk.val += 1
        MSK1 = c_msk.val

        vs0 = mkvs(2)

        def dve_mtree(v, MSK1=MSK1, vs0=vs0):
            v.wait_ge(c_msk.sem, MSK1)
            for j in range(4):
                v.tensor_tensor(rsl(8 + j), rsl(2 * j), rsl(2 * j + 1), ALU.max)
            VS(v, vs0, 0)
            v.tensor_tensor(rsl(14), rsl(8), rsl(9), ALU.max)
            v.tensor_tensor(rsl(15), rsl(10), rsl(11), ALU.max)
            VS(v, vs0, 1)
            v.tensor_tensor(rsl(12), rsl(14), rsl(15), ALU.max)
            v.engine_nop().then_inc(c_msk.sem, 1)
        planB.add("DVE", dve_mtree)
        c_msk.val += 1
        MSK2 = c_msk.val

        def act_score(s, MSK2=MSK2):
            s.wait_ge(c_msk.sem, MSK2)
            s.activation(rsl(13), rsl(12), AF.Sigmoid).then_inc(c_msk.sem, 1)
        planB.add("ACT", act_score)
        c_msk.val += 1
        MSK3 = c_msk.val

        vs0 = mkvs(NE)

        def dve_msc(v, MSK3=MSK3, vs0=vs0):
            v.wait_ge(c_msk.sem, MSK3)
            for e in range(NE):
                v.tensor_tensor(rsl(8 + (e % 2)), rsl(e), rsl(12), ALU.is_ge)
                VS(v, vs0, e)
                v.tensor_tensor(eqs[0:1, e * 512:(e + 1) * 512],
                                rsl(8 + (e % 2)), rsl(13), ALU.mult)
            v.engine_nop().then_inc(c_msk.sem, 1)
        planB.add("DVE", dve_msc)
        c_msk.val += 1
        MSK4 = c_msk.val

        bank = pbk[5 + gq % 2]
        gen0 = bank.fre.val

        def pe_mscp(t, gq=gq, bank=bank, gen0=gen0, MSK4=MSK4):
            t.wait_ge(c_msk.sem, MSK4)
            if gen0 > 0:
                t.wait_ge(bank.fre.sem, gen0)
            ins = None
            for e in range(NE):
                ins = t.matmul(bank.t[0:8, :],
                               onehr[0:1, e * 8:(e + 1) * 8],
                               eqs[0:1, e * 512:(e + 1) * 512],
                               start=(e == 0), stop=(e == NE - 1))
            ins.then_inc(bank.rdy.sem, 1)
        planB.add("PE", pe_mscp)
        bank.rdy.val += 1
        bank.fre.val += 1
        RDY = bank.rdy.val

        FREV = bank.fre.val

        def act_mscp(s, gq=gq, bank=bank, RDY=RDY, FREV=FREV):
            s.wait_ge(bank.rdy.sem, RDY)
            s.copy(mscP[:, gq * 512:(gq + 1) * 512],
                   bank.t[0:8, :]).then_inc(bank.fre.sem, 1)
            s.wait_ge(bank.fre.sem, FREV)
            s.nop().then_inc(c_msk.sem, 1)
        planB.add("ACT", act_mscp)
        c_msk.val += 1
    MSKF = c_msk.val

    sav_hist = {}
    y_hist = {}
    hb_hist = {}
    gu_rot = [0]
    y_rot = [0]
    for th in range(0 if NOTAIL else MKTH):
        OUT_prev = c_out.val

        def sy_x1ld(sy, th=th, OUT_prev=OUT_prev):
            if th > 0:
                sy.wait_ge(c_out.sem, OUT_prev)
            for dq in range(4):
                src = bass.AP(x1d, dq * 4 * 128 * T + th * THW,
                              [[T, 128], [128 * T, 4], [1, THW]])
                dst = bass.AP(x1th, dq * 4 * THW,
                              [[DC * THW, 128], [THW, 4], [1, THW]])
                sy.dma_start(dst, src).then_inc(c_x1ld.sem, 16)
        planB.add("SYNC", sy_x1ld)
        c_x1ld.val += 64
        X1LD = c_x1ld.val

        for e in range(MKNE):
            if e < NE:
                for ts in range(TS):
                    bank = pbk[6]
                    gen0 = bank.fre.val

                    def pe_msb(t, e=e, th=th, ts=ts, bank=bank, gen0=gen0,
                               MSKF=MSKF):
                        t.wait_ge(c_msk.sem, MSKF)
                        if gen0 > 0:
                            t.wait_ge(bank.fre.sem, gen0)
                        t.matmul(bank.t[:, :],
                                 selb[:, e * 128:(e + 1) * 128],
                                 mscP[:, th * THW + ts * TSW:
                                      th * THW + (ts + 1) * TSW],
                                 start=True, stop=True).then_inc(bank.rdy.sem, 1)
                    planB.add("PE", pe_msb)
                    bank.rdy.val += 1
                    bank.fre.val += 1
                    RDY = bank.rdy.val
                    MSBW = max(0, c_msbf.val - 1)

                    FREV = bank.fre.val

                    def act_msb(s, e=e, ts=ts, bank=bank, RDY=RDY, MSBW=MSBW,
                                FREV=FREV):
                        s.wait_ge(bank.rdy.sem, RDY)
                        if MSBW > 0:
                            s.wait_ge(c_msbf.sem, MSBW)
                        s.copy(msb[:, (e % 2) * THW + ts * TSW:
                                   (e % 2) * THW + (ts + 1) * TSW],
                               bank.t[:, :]).then_inc(bank.fre.sem, 1)
                        s.wait_ge(bank.fre.sem, FREV)
                        s.nop().then_inc(c_msb.sem, 1)
                    planB.add("ACT", act_msb)
                    c_msb.val += 1
            MSB = c_msb.val

            for fc in range(FC):
                hG = ringB.load(gW.ap()[e, fc, :, :])
                hU = ringB.load(uW.ap()[e, fc, :, :])
                for ts in range(TS):
                    bankG = pbk[gu_rot[0] % 4]
                    gu_rot[0] += 1
                    bankU = pbk[gu_rot[0] % 4]
                    gu_rot[0] += 1
                    genG = bankG.fre.val
                    genU = bankU.fre.val

                    RG = bankG.rdy.val + 1
                    RU = bankU.rdy.val + 1

                    def pe_gu(t, e=e, fc=fc, ts=ts, hG=hG, hU=hU,
                              bankG=bankG, bankU=bankU, genG=genG, genU=genU,
                              X1LD=X1LD, RG=RG, RU=RU):
                        if fc == 0 and ts == 0:
                            t.wait_ge(c_x1ld.sem, X1LD)
                        if ts == 0:
                            hG.wait(t)
                            hU.wait(t)
                        if genG > 0:
                            t.wait_ge(bankG.fre.sem, genG)
                        ins = None
                        for dc in range(DC):
                            ins = t.matmul(
                                bankG.t[:, :],
                                hG.ap[:, dc * 128:(dc + 1) * 128],
                                x1th[:, dc * THW + ts * TSW:
                                     dc * THW + (ts + 1) * TSW],
                                start=(dc == 0), stop=(dc == DC - 1))
                        ins.then_inc(bankG.rdy.sem, 1)
                        if ts == TS - 1:
                            t.wait_ge(bankG.rdy.sem, RG)
                            hG.free_on(t.nop())
                        if genU > 0:
                            t.wait_ge(bankU.fre.sem, genU)
                        for dc in range(DC):
                            ins = t.matmul(
                                bankU.t[:, :],
                                hU.ap[:, dc * 128:(dc + 1) * 128],
                                x1th[:, dc * THW + ts * TSW:
                                     dc * THW + (ts + 1) * TSW],
                                start=(dc == 0), stop=(dc == DC - 1))
                        ins.then_inc(bankU.rdy.sem, 1)
                        if ts == TS - 1:
                            t.wait_ge(bankU.rdy.sem, RU)
                            hU.free_on(t.nop())
                    planB.add("PE", pe_gu)
                    bankG.rdy.val += 1
                    bankG.fre.val += 1
                    bankU.rdy.val += 1
                    bankU.fre.val += 1
                    par = fc % 2
                    SAVW = sav_hist.get((ts, par), 0)
                    HBW = hb_hist.get((ts, par), 0)

                    if e < NE:
                        def dve_ab(v, e=e, ts=ts, bankG=bankG, bankU=bankU,
                                   RG=RG, RU=RU, MSB=MSB, SAVW=SAVW, fc=fc,
                                   HBW=HBW, par=par):
                            v.wait_ge(bankG.rdy.sem, RG)
                            if fc == 0 and ts == 0:
                                v.wait_ge(c_msb.sem, MSB)
                            if SAVW > 0:
                                v.wait_ge(c_silu.sem, SAVW)
                            if HBW > 0:
                                v.wait_ge(c_hb.sem, HBW)
                            v.tensor_tensor(
                                tmpA[ts][:, par * TSW:(par + 1) * TSW], bankG.t[:, :],
                                msb[:, (e % 2) * THW + ts * TSW:
                                    (e % 2) * THW + (ts + 1) * TSW],
                                ALU.mult).then_inc(bankG.fre.sem, 1)
                            v.engine_nop().then_inc(c_silu.sem, 1)
                            v.wait_ge(bankU.rdy.sem, RU)
                            v.tensor_tensor(
                                tmpB[ts][:, par * TSW:(par + 1) * TSW], bankU.t[:, :],
                                msb[:, (e % 2) * THW + ts * TSW:
                                    (e % 2) * THW + (ts + 1) * TSW],
                                ALU.mult).then_inc(bankU.fre.sem, 1)
                            v.engine_nop().then_inc(c_silu.sem, 1)
                        planB.add("DVE", dve_ab)
                        c_silu.val += 2
                    else:
                        FG = bankG.fre.val
                        FU = bankU.fre.val

                        def act_ab(s, bankG=bankG, bankU=bankU, RG=RG, RU=RU,
                                   ts=ts, SAVW=SAVW, FG=FG, FU=FU, HBW=HBW, par=par):
                            s.wait_ge(bankG.rdy.sem, RG)
                            if SAVW > 0:
                                s.wait_ge(c_silu.sem, SAVW)
                            if HBW > 0:
                                s.wait_ge(c_hb.sem, HBW)
                            s.copy(tmpA[ts][:, par * TSW:(par + 1) * TSW], bankG.t[:, :]).then_inc(
                                bankG.fre.sem, 1)
                            s.wait_ge(bankG.fre.sem, FG)
                            s.nop().then_inc(c_silu.sem, 1)
                            s.wait_ge(bankU.rdy.sem, RU)
                            s.copy(tmpB[ts][:, par * TSW:(par + 1) * TSW], bankU.t[:, :]).then_inc(
                                bankU.fre.sem, 1)
                            s.wait_ge(bankU.fre.sem, FU)
                            s.nop().then_inc(c_silu.sem, 1)
                        planB.add("ACT", act_ab)
                        c_silu.val += 2
                    AB = c_silu.val

                    def act_silu(s, ts=ts, AB=AB, par=par):
                        s.wait_ge(c_silu.sem, AB)
                        s.activation(tmpSA[ts][:, par * TSW:(par + 1) * TSW], tmpA[ts][:, par * TSW:(par + 1) * TSW],
                                     AF.Silu).then_inc(c_silu.sem, 1)
                    planB.add("ACT", act_silu)
                    c_silu.val += 1
                    SAV = c_silu.val
                    sav_hist[(ts, par)] = SAV

                    def dve_h(v, fc=fc, ts=ts, SAV=SAV, par=par):
                        v.wait_ge(c_silu.sem, SAV)
                        v.engine_nop()
                        v.tensor_tensor(h_buf[ts][:, fc * TSW:(fc + 1) * TSW],
                                        tmpSA[ts][:, par * TSW:(par + 1) * TSW], tmpB[ts][:, par * TSW:(par + 1) * TSW],
                                        ALU.mult).then_inc(c_hb.sem, 1)
                    planB.add("DVE", dve_h)
                    c_hb.val += 1
                    hb_hist[(ts, par)] = c_hb.val
            HB = c_hb.val

            if e < NE:
                def dve_msbf(v):
                    v.engine_nop().then_inc(c_msbf.sem, 1)
                planB.add("DVE", dve_msbf)
                c_msbf.val += 1

            for dc in range(DC):
                hD = ringB.load(dW.ap()[e, dc, :, :])
                for ts in range(TS):
                    bank = pbk[4 + y_rot[0] % 3]
                    y_rot[0] += 1
                    gen0 = bank.fre.val

                    RDY = bank.rdy.val + 1

                    def pe_down(t, e=e, dc=dc, ts=ts, hD=hD, bank=bank,
                                gen0=gen0, HB=HB, RDY=RDY):
                        if ts == 0:
                            hD.wait(t)
                        if dc == 0 and ts == 0:
                            t.wait_ge(c_hb.sem, HB)
                        if gen0 > 0:
                            t.wait_ge(bank.fre.sem, gen0)
                        ins = None
                        for fc in range(FC):
                            ins = t.matmul(
                                bank.t[:, :],
                                hD.ap[:, fc * 128:(fc + 1) * 128],
                                h_buf[ts][:, fc * TSW:(fc + 1) * TSW],
                                start=(fc == 0), stop=(fc == FC - 1))
                        ins.then_inc(bank.rdy.sem, 1)
                        if ts == TS - 1:
                            t.wait_ge(bank.rdy.sem, RDY)
                            hD.free_on(t.nop())
                    planB.add("PE", pe_down)
                    bank.rdy.val += 1
                    bank.fre.val += 1

                    YPREV = y_hist.get((dc, ts), 0)
                    y_hist[(dc, ts)] = c_yw.val + 1

                    def dve_y(v, e=e, dc=dc, ts=ts, bank=bank, RDY=RDY,
                              YPREV=YPREV):
                        v.wait_ge(bank.rdy.sem, RDY)
                        if YPREV > 0:
                            v.wait_ge(c_yw.sem, YPREV)
                        dst = y_acc[:, dc * THW + ts * TSW:
                                    dc * THW + (ts + 1) * TSW]
                        if e == 0:
                            v.tensor_scalar(dst, bank.t[:, :], 0.0, None,
                                            ALU.add)
                        else:
                            v.tensor_tensor(dst, dst, bank.t[:, :], ALU.add)
                        v.engine_nop().then_inc(bank.fre.sem, 1)
                        v.engine_nop().then_inc(c_yw.sem, 1)
                    planB.add("DVE", dve_y)
                    c_yw.val += 1
        YW = c_yw.val

        gf = [pbk[i].fre.val for i in range(7)]
        l2w_base = c_l2w.val
        sq2w_base = c_sq2w.val
        st2r_base = c_st2r.val
        for dc in range(DC if TAILCUT >= 2 else 0):
            def dve_l2(v, dc=dc, YW=YW, st2r_base=st2r_base):
                if dc == 0:
                    v.wait_ge(c_yw.sem, YW)
                if dc >= 2:
                    v.wait_ge(c_st2r.sem, st2r_base + dc - 1)
                v.tensor_tensor(l2x[dc % 2][:, :],
                                x1th[:, dc * THW:(dc + 1) * THW],
                                y_acc[:, dc * THW:(dc + 1) * THW],
                                ALU.add).then_inc(c_l2w.sem, 1)
            planB.add("DVE", dve_l2)
            c_l2w.val += 1
            L2 = c_l2w.val

            def act_sq2(s, dc=dc, L2=L2, st2r_base=st2r_base):
                s.wait_ge(c_l2w.sem, L2)
                if dc >= 2:
                    s.wait_ge(c_st2r.sem, st2r_base + dc - 1)
                s.square(sq2[dc % 2][:, :], l2x[dc % 2][:, :]).then_inc(
                    c_sq2w.sem, 1)
            planB.add("ACT", act_sq2)
            c_sq2w.val += 1
            L3 = c_sq2w.val

            def pe_st2(t, dc=dc, L2=L2, L3=L3, gf=gf, st2r_base=st2r_base):
                t.wait_ge(c_l2w.sem, L2)
                if dc == 0:
                    for i in (2, 3, 4, 5):
                        if gf[i] > 0:
                            t.wait_ge(pbk[i].fre.sem, gf[i])
                for half in range(2):
                    t.matmul(pbk[2 + half].t[0:1, :], onesc[:, :],
                             l2x[dc % 2][:, half * 512:(half + 1) * 512],
                             start=(dc == 0), stop=(dc == 15))
                t.wait_ge(c_sq2w.sem, L3)
                ins = None
                for half in range(2):
                    ins = t.matmul(pbk[4 + half].t[0:1, :], onesc[:, :],
                                   sq2[dc % 2][:, half * 512:(half + 1) * 512],
                                   start=(dc == 0), stop=(dc == 15))
                ins.then_inc(c_st2r.sem, 1)
                if dc == 15:
                    t.wait_ge(c_st2r.sem, st2r_base + 16)
                    t.nop().then_inc(pbk[5].rdy.sem, 1)
            planB.add("PE", pe_st2)
            c_st2r.val += 1
        for i in (2, 3, 4):
            pbk[i].fre.val += 1
        pbk[5].rdy.val += 1
        pbk[5].fre.val += 1
        S2 = pbk[5].rdy.val

        if TAILCUT < 3:
            continue

        vs0 = mkvs(3)

        def dve_ln2s(v, S2=S2, vs0=vs0):
            v.wait_ge(pbk[5].rdy.sem, S2)
            v.tensor_scalar(rsl(0), pbk[2].t[0:1, :], RN, None,
                            ALU.mult).then_inc(pbk[2].fre.sem, 1)
            v.tensor_scalar(rsl(1), pbk[3].t[0:1, :], RN, None,
                            ALU.mult).then_inc(pbk[3].fre.sem, 1)
            v.tensor_scalar(rsl(2), pbk[4].t[0:1, :], RN, None,
                            ALU.mult).then_inc(pbk[4].fre.sem, 1)
            v.tensor_scalar(rsl(3), pbk[5].t[0:1, :], RN, None,
                            ALU.mult).then_inc(pbk[5].fre.sem, 1)
            VS(v, vs0, 0)
            v.tensor_tensor(rsl(4, 2), rsl(0, 2), rsl(0, 2), ALU.mult)
            VS(v, vs0, 1)
            v.tensor_tensor(rsl(6, 2), rsl(2, 2), rsl(4, 2), ALU.subtract)
            VS(v, vs0, 2)
            v.tensor_scalar(rsl(6, 2), rsl(6, 2), EPS, None,
                            ALU.add).then_inc(c_ln2.sem, 1)
        planB.add("DVE", dve_ln2s)
        c_ln2.val += 1
        L4 = c_ln2.val

        def act_ln2q(s, L4=L4):
            s.wait_ge(c_ln2.sem, L4)
            s.activation(rsl(8, 2), rsl(6, 2), AF.Sqrt).then_inc(c_ln2.sem, 1)
        planB.add("ACT", act_ln2q)
        c_ln2.val += 1
        L5 = c_ln2.val

        vs0 = mkvs(2)

        def dve_ln2r(v, L5=L5, vs0=vs0):
            v.wait_ge(c_ln2.sem, L5)
            v.reciprocal(rsl(10, 2), rsl(8, 2))
            VS(v, vs0, 0)
            v.tensor_tensor(rsl(12, 2), rsl(0, 2), rsl(10, 2), ALU.mult)
            VS(v, vs0, 1)
            v.tensor_copy(eqs[0:1, 0:THW], rsl(10, 2))
            v.tensor_copy(eqs[0:1, THW:2 * THW], rsl(12, 2))
            v.engine_nop().then_inc(c_ln2.sem, 1)
        planB.add("DVE", dve_ln2r)
        c_ln2.val += 1
        L6 = c_ln2.val

        if TAILCUT < 4:
            continue
        for j in range(2):
            for half in range(2):
                bank = pbk[2 * j + half]
                gen0 = bank.fre.val

                def pe_bc2(t, j=j, half=half, bank=bank, gen0=gen0, L6=L6):
                    t.wait_ge(c_ln2.sem, L6)
                    if gen0 > 0:
                        t.wait_ge(bank.fre.sem, gen0)
                    t.matmul(bank.t[:, :], onesr[:, :],
                             eqs[0:1, j * THW + half * 512:
                                 j * THW + (half + 1) * 512],
                             start=True, stop=True).then_inc(bank.rdy.sem, 1)
                planB.add("PE", pe_bc2)
                bank.rdy.val += 1
                bank.fre.val += 1
                RDY = bank.rdy.val

                FREV = bank.fre.val

                def act_bc2(s, j=j, half=half, bank=bank, RDY=RDY, FREV=FREV):
                    s.wait_ge(bank.rdy.sem, RDY)
                    s.copy(rb2[:, j * THW + half * 512:
                               j * THW + (half + 1) * 512],
                           bank.t[:, :]).then_inc(bank.fre.sem, 1)
                    s.wait_ge(bank.fre.sem, FREV)
                    s.nop().then_inc(c_ln2.sem, 1)
                planB.add("ACT", act_bc2)
                c_ln2.val += 1
        L7 = c_ln2.val

        if TAILCUT < 5:
            continue
        ost_base = c_ost.val
        for dc in range(DC):
            vs0 = mkvs(2)
            OSTP = c_ostp[dc % 2].val

            def dve_out(v, dc=dc, th=th, L7=L7, vs0=vs0, OSTP=OSTP):
                if dc == 0:
                    v.wait_ge(c_ln2.sem, L7)
                if OSTP > 0:
                    v.wait_ge(c_ostp[dc % 2].sem, OSTP)
                v.tensor_tensor(l2x[dc % 2][:, :],
                                x1th[:, dc * THW:(dc + 1) * THW],
                                y_acc[:, dc * THW:(dc + 1) * THW], ALU.add)
                VS(v, vs0, 0)
                v.tensor_tensor(sq2[dc % 2][:, :], l2x[dc % 2][:, :],
                                rb2[:, 0:THW], ALU.mult)
                VS(v, vs0, 1)
                v.tensor_tensor(ostg[dc % 2][:, :], sq2[dc % 2][:, :],
                                rb2[:, THW:2 * THW],
                                ALU.subtract).then_inc(c_out.sem, 1)
            planB.add("DVE", dve_out)
            c_out.val += 1
            OUTV = c_out.val

            def sy_out(sy, dc=dc, th=th, OUTV=OUTV):
                sy.wait_ge(c_out.sem, OUTV)
                sy.dma_start(
                    out_d.ap()[:, dc * T + th * THW: dc * T + (th + 1) * THW],
                    ostg[dc % 2][:, :]).then_inc(c_ostp[dc % 2].sem, 16)
            planB.add("SYNC", sy_out)
            c_ostp[dc % 2].val += 16

    OST0 = c_ostp[0].val
    OST1 = c_ostp[1].val
    MSKF2 = c_msk.val

    def sy_final(sy):
        sy.wait_ge(c_ostp[0].sem, OST0)
        sy.wait_ge(c_ostp[1].sem, OST1)
        sy.wait_ge(c_msk.sem, MSKF2)
    planB.add("SYNC", sy_final)

    with nc.Block("phaseA") as blk:
        @blk.tensor
        def _(t):
            planA.emit("PE", t)

        @blk.vector
        def _(v):
            planA.emit("DVE", v)

        @blk.scalar
        def _(s):
            planA.emit("ACT", s)

        @blk.sync
        def _(sy):
            planA.emit("SYNC", sy)

        @blk.gpsimd
        def _(g):
            planA.emit("POOL", g)

    with nc.Block("phaseB") as blk:
        @blk.tensor
        def _(t):
            planB.emit("PE", t)

        @blk.vector
        def _(v):
            planB.emit("DVE", v)

        @blk.scalar
        def _(s):
            planB.emit("ACT", s)

        @blk.sync
        def _(sy):
            planB.emit("SYNC", sy)

        @blk.gpsimd
        def _(g):
            planB.emit("POOL", g)

    scopeB.close()
    nc.ctx.close()
    return nc



def _get_nc():
    if "nc" not in _CACHE:
        _CACHE["nc"] = _build()
    return _CACHE["nc"]


def kernel(L0_attn_x, L0_attn_wq, L0_attn_wk, L0_attn_wv, L0_attn_wo,
           L0_ffn_router_w, L0_ffn_gate_w, L0_ffn_up_w, L0_ffn_down_w,
           L0_ffn_shared_gate_w, L0_ffn_shared_up_w, L0_ffn_shared_down_w):
    x = np.ascontiguousarray(np.asarray(L0_attn_x, np.float32))
    wq = np.asarray(L0_attn_wq, np.float32)
    wk = np.asarray(L0_attn_wk, np.float32)
    wv = np.asarray(L0_attn_wv, np.float32)
    wo = np.asarray(L0_attn_wo, np.float32)
    rw = np.asarray(L0_ffn_router_w, np.float32)
    gw = np.asarray(L0_ffn_gate_w, np.float32)
    uw = np.asarray(L0_ffn_up_w, np.float32)
    dw = np.asarray(L0_ffn_down_w, np.float32)
    sgw = np.asarray(L0_ffn_shared_gate_w, np.float32)
    suw = np.asarray(L0_ffn_shared_up_w, np.float32)
    sdw = np.asarray(L0_ffn_shared_down_w, np.float32)

    tabs = _host_tables()
    shared = {
        "wqT": _swz_lhsT(wq.T), "wkT": _swz_lhsT(wk.T), "woT": _swz_lhsT(wo.T),
        "wvR": np.ascontiguousarray(
            np.ascontiguousarray(wv.T).reshape(4, 4, 128, 512).transpose(
                0, 2, 1, 3).reshape(4, 128, 2048)),
        "rtr": np.ascontiguousarray(
            rw.reshape(16, 128, 8).transpose(1, 0, 2).reshape(128, 128)),
        "gW": np.stack([_swz_lhsT(gw[e]) for e in range(NE)] + [_swz_lhsT(sgw)]),
        "uW": np.stack([_swz_lhsT(uw[e]) for e in range(NE)] + [_swz_lhsT(suw)]),
        "dW": np.stack([_swz_lhsT(dw[e]) for e in range(NE)] + [_swz_lhsT(sdw)]),
        "diagTd": tabs["diagT"], "qdecbd": tabs["qdecb"], "kdecd": tabs["kdec"],
        "identd": tabs["ident"], "onescd": tabs["onesc"],
        "onesrd": tabs["onesr"], "onehd": tabs["oneh"],
        "onehrd": tabs["onehr"], "selbd": tabs["selb"],
    }

    in_maps = []
    zero_tab = np.zeros((128, T), BF)
    zero_x = np.zeros((NBLK, 128, DC * TB), np.float32)
    for c in range(8):
        b, half = c // 2, c % 2
        xt = np.ascontiguousarray(x[b, half * T:(half + 1) * T, :].T)
        m = dict(shared)
        m["xb"] = _swz_x(xt)
        cos_o_t, sin_o_t = _rope_tables(half * T + np.arange(T))
        m["cosb"], m["sinb"] = cos_o_t, sin_o_t
        if half == 1:
            xtp = np.ascontiguousarray(x[b, 0:T, :].T)
            m["xp"] = _swz_x(xtp)
            cos_p_t, sin_p_t = _rope_tables(np.arange(T))
            m["cosp"], m["sinp"] = cos_p_t, sin_p_t
        else:
            m["xp"] = zero_x
            m["cosp"], m["sinp"] = zero_tab, zero_tab
        in_maps.append(m)

    nc = _get_nc()
    res = run_bass_kernel_spmd(nc, in_maps, core_ids=list(range(8)))

    out = np.empty((B, S, D), np.float32)
    for c in range(8):
        r = np.asarray(res.results[c]["out"], np.float32)
        blkm = r.reshape(128, DC, T).transpose(2, 1, 0).reshape(T, D)
        out[c // 2, (c % 2) * T:(c % 2 + 1) * T, :] = blkm
    return out


# version 05
# speedup vs baseline: 1.1106x; 1.0149x over previous
import os
import contextlib
import numpy as np
import ml_dtypes
import concourse.bass as bass
import concourse.mybir as mybir
from concourse.bass_utils import run_bass_kernel_spmd

F32 = mybir.dt.float32
BF16 = mybir.dt.bfloat16
AF = mybir.ActivationFunctionType
ALU = mybir.AluOpType
BF = ml_dtypes.bfloat16

B, S, D = 4, 4096, 2048
T = 2048
NBLK = 4
TB = 512
DC = 16
HQ, HKV, DH = 16, 4, 128
CH = 64
CPB = TB // CH
NE = 8
FC = 16
TH = 2
THW = 1024
TS = 2
TSW = 512
EPS = 1e-5
RN = 1.0 / 2048.0

DEBUG_X1 = bool(os.environ.get("MK_DEBUG_X1"))
DEBUG2 = bool(os.environ.get("MK_DEBUG2"))
SKIP_B = bool(os.environ.get("MK_SKIP_B"))
NO_QHAT = bool(os.environ.get("MK_NO_QHAT"))
NO_LGT = bool(os.environ.get("MK_NO_LGT"))
MKNE = int(os.environ.get("MK_NE", str(NE + 1)))
NOMASK = bool(os.environ.get("MK_NOMASK"))
NOTAIL = bool(os.environ.get("MK_NOTAIL"))
TAILCUT = int(os.environ.get("MK_TAILCUT", "5"))
MKTH = int(os.environ.get("MK_TH", str(TH)))
NBI = int(os.environ.get("MK_NBI", "8"))



def _swz_lhsT(w):
    w = np.ascontiguousarray(w, np.float32)
    K, M = w.shape
    KC, MC = K // 128, M // 128
    return np.ascontiguousarray(
        w.reshape(KC, 128, MC, 128).transpose(2, 1, 0, 3).reshape(MC, 128, KC * 128))


def _swz_x(xt):
    return np.ascontiguousarray(
        xt.reshape(DC, 128, NBLK, TB).transpose(2, 1, 0, 3).reshape(NBLK, 128, DC * TB))


def _lam():
    return np.exp2(-0.5 * (np.arange(HQ, dtype=np.float64) + 1))


def _host_tables():
    t = {}
    lam = _lam()
    r = np.arange(CH, dtype=np.float64)
    diagT = np.zeros((64, HQ * 64), np.float32)
    for h in range(HQ):
        dm = np.exp(-lam[h] * (r[None, :] - r[:, None]))
        diagT[:, h * 64:(h + 1) * 64] = np.where(
            r[None, :] >= r[:, None], dm, 0.0).astype(np.float32)
    t["diagT"] = diagT
    qdecb = np.zeros((128, HQ * 64), np.float32)
    for h in range(HQ):
        qdecb[:, h * 64:(h + 1) * 64] = np.exp(-lam[h] * (r + 1.0))[None, :]
    t["qdecb"] = qdecb
    kdec = np.zeros((64, 16), np.float32)
    for h in range(HQ):
        kdec[:, h] = np.exp(-lam[h] * (CH - 1.0 - r))
    t["kdec"] = kdec
    t["ident"] = np.eye(128, dtype=BF)
    t["onesc"] = np.ones((128, 1), BF)
    t["onesr"] = np.ones((1, 128), BF)
    t["oneh"] = np.eye(8, dtype=np.float32)
    onehr = np.zeros((1, 64), BF)
    for e in range(8):
        onehr[0, e * 8 + e] = 1.0
    t["onehr"] = onehr
    selb = np.zeros((8, 8 * 128), BF)
    for e in range(8):
        selb[e, e * 128:(e + 1) * 128] = 1.0
    t["selb"] = selb
    return t


_BLKD = [float(np.exp(-_lam()[h] * CH)) for h in range(HQ)]


def _rope_tables(pos):
    inv = 10000.0 ** (-(np.arange(64, dtype=np.float64)) / 64.0)
    f = pos[None, :].astype(np.float64) * inv[:, None]
    cos = np.cos(f).astype(np.float32)
    sin = np.sin(f).astype(np.float32)
    return (np.concatenate([cos, cos], 0).astype(BF),
            np.concatenate([-sin, sin], 0).astype(BF))



class Plan:
    def __init__(self):
        self.items = []

    def add(self, eng, fn):
        self.items.append((eng, fn))

    def emit(self, name, eng):
        for e, fn in self.items:
            if e == name:
                fn(eng)


class Ctr:
    def __init__(self, nc, name):
        self.sem = nc.ctx.enter_context(nc.semaphore(name))
        self.val = 0


class Bank:
    def __init__(self, nc, name, dtype=F32, width=512):
        self.t = nc.ctx.enter_context(nc.psum_tensor(name, [128, width], dtype))
        self.rdy = Ctr(nc, name + "_r")
        self.fre = Ctr(nc, name + "_f")


class Ring:

    def __init__(self, nc, plan, stack, name, nslots, width, dtype=BF16):
        self.plan = plan
        self.nslots = nslots
        self.width = width
        self.t = stack.enter_context(nc.sbuf_tensor(name, [128, nslots * width], dtype))
        self.wld = [stack.enter_context(nc.semaphore(f"{name}_l{i}")) for i in range(nslots)]
        self.wfr = [stack.enter_context(nc.semaphore(f"{name}_f{i}")) for i in range(nslots)]
        self.n = 0
        self.fr_val = [0] * nslots

    def load(self, src_ap, w=None):
        slot = self.n % self.nslots
        gen = self.n // self.nslots
        self.n += 1
        need_free = self.fr_val[slot]
        wld, wfr, t, width = self.wld[slot], self.wfr[slot], self.t, self.width
        w = width if w is None else w

        def pool_emit(g, slot=slot, gen=gen, src_ap=src_ap, need_free=need_free,
                      wld=wld, wfr=wfr, t=t, width=width, w=w):
            if gen > 0:
                g.wait_ge(wfr, need_free)
            g.dma_start(t[:, slot * width:slot * width + w], src_ap).then_inc(wld, 16)
        self.plan.add("POOL", pool_emit)
        self.fr_val[slot] += 1
        return RingHandle(self, slot, gen, w)


class RingHandle:
    def __init__(self, ring, slot, gen, w):
        self.ring, self.slot, self.gen, self.w = ring, slot, gen, w

    @property
    def ap(self):
        r = self.ring
        return r.t[:, self.slot * r.width:self.slot * r.width + self.w]

    def wait(self, eng):
        eng.wait_ge(self.ring.wld[self.slot], 16 * (self.gen + 1))

    def free_on(self, instr):
        instr.then_inc(self.ring.wfr[self.slot], 1)



_CACHE = {}


def _build():
    nc = bass.Bass("TRN2", target_bir_lowering=False)
    nc.ctx = contextlib.ExitStack()

    xb = nc.dram_tensor("xb", [NBLK, 128, DC * TB], F32, kind="ExternalInput")
    xp = nc.dram_tensor("xp", [NBLK, 128, DC * TB], F32, kind="ExternalInput")
    wqT = nc.dram_tensor("wqT", [16, 128, 2048], F32, kind="ExternalInput")
    wkT = nc.dram_tensor("wkT", [4, 128, 2048], F32, kind="ExternalInput")
    wvR = nc.dram_tensor("wvR", [4, 128, 2048], F32, kind="ExternalInput")
    woT = nc.dram_tensor("woT", [16, 128, 2048], F32, kind="ExternalInput")
    rtr = nc.dram_tensor("rtr", [128, 128], F32, kind="ExternalInput")
    gW = nc.dram_tensor("gW", [NE + 1, 16, 128, 2048], F32, kind="ExternalInput")
    uW = nc.dram_tensor("uW", [NE + 1, 16, 128, 2048], F32, kind="ExternalInput")
    dW = nc.dram_tensor("dW", [NE + 1, 16, 128, 2048], F32, kind="ExternalInput")
    cosb_d = nc.dram_tensor("cosb", [128, T], BF16, kind="ExternalInput")
    sinb_d = nc.dram_tensor("sinb", [128, T], BF16, kind="ExternalInput")
    cosp_d = nc.dram_tensor("cosp", [128, T], BF16, kind="ExternalInput")
    sinp_d = nc.dram_tensor("sinp", [128, T], BF16, kind="ExternalInput")
    diagT_d = nc.dram_tensor("diagTd", [64, HQ * 64], F32, kind="ExternalInput")
    qdecb_d = nc.dram_tensor("qdecbd", [128, HQ * 64], F32, kind="ExternalInput")
    kdec_d = nc.dram_tensor("kdecd", [64, 16], F32, kind="ExternalInput")
    ident_d = nc.dram_tensor("identd", [128, 128], BF16, kind="ExternalInput")
    onesc_d = nc.dram_tensor("onescd", [128, 1], BF16, kind="ExternalInput")
    onesr_d = nc.dram_tensor("onesrd", [1, 128], BF16, kind="ExternalInput")
    oneh_d = nc.dram_tensor("onehd", [8, 8], F32, kind="ExternalInput")
    onehr_d = nc.dram_tensor("onehrd", [1, 64], BF16, kind="ExternalInput")
    selb_d = nc.dram_tensor("selbd", [8, 8 * 128], BF16, kind="ExternalInput")

    out_d = nc.dram_tensor("out", [128, DC * T], F32, kind="ExternalOutput")
    x1d = nc.dram_tensor("x1d", [DC, 128, T], BF16,
                         kind="ExternalOutput" if DEBUG_X1 else "Internal")

    planA = Plan()
    planB = Plan()

    ident = nc.ctx.enter_context(nc.sbuf_tensor("identt", [128, 128], BF16))
    onesc = nc.ctx.enter_context(nc.sbuf_tensor("onescc", [128, 1], BF16))
    onesr = nc.ctx.enter_context(nc.sbuf_tensor("onesrr", [1, 128], BF16))
    oneh = nc.ctx.enter_context(nc.sbuf_tensor("onehh", [8, 8], F32))
    onehr = nc.ctx.enter_context(nc.sbuf_tensor("onehr", [1, 64], BF16))
    selb = nc.ctx.enter_context(nc.sbuf_tensor("selb", [8, 8 * 128], BF16))
    logit_sb = nc.ctx.enter_context(nc.sbuf_tensor("logit_sb", [8, T], F32))
    mscP = nc.ctx.enter_context(nc.sbuf_tensor("mscP", [8, T], BF16))

    pbk = [Bank(nc, f"pb{i}") for i in range(7)]
    pbt = Bank(nc, "pbt", dtype=BF16, width=1024)

    c_kvs = Ctr(nc, "c_kvs")
    c_khat = Ctr(nc, "c_khat")
    c_ssc = Ctr(nc, "c_ssc")
    c_rope = Ctr(nc, "c_rope")
    c_krope = Ctr(nc, "c_krope")
    c_vtm = Ctr(nc, "c_vtm")
    c_oev = Ctr(nc, "c_oev")
    c_och = Ctr(nc, "c_och")
    c_dch = Ctr(nc, "c_dch")
    c_pre = Ctr(nc, "c_pre")
    c_sq = Ctr(nc, "c_sq")
    c_ln = Ctr(nc, "c_ln")
    c_x1a = Ctr(nc, "c_x1a")
    c_x1st = Ctr(nc, "c_x1st")
    c_msk = Ctr(nc, "c_msk")
    c_hb = Ctr(nc, "c_hb")
    c_silu = Ctr(nc, "c_silu")
    c_msb = Ctr(nc, "c_msb")
    c_msbf = Ctr(nc, "c_msbf")
    c_yw = Ctr(nc, "c_yw")
    c_ln2 = Ctr(nc, "c_ln2")
    c_out = Ctr(nc, "c_out")
    c_ost = Ctr(nc, "c_ost")
    c_x1ld = Ctr(nc, "c_x1ld")
    c_cst = Ctr(nc, "c_cst")
    c_vsy = Ctr(nc, "c_vsy")
    c_sqr = Ctr(nc, "c_sqr")
    c_ostp = [Ctr(nc, "c_ostp0"), Ctr(nc, "c_ostp1")]
    c_l2w = Ctr(nc, "c_l2w")
    c_sq2w = Ctr(nc, "c_sq2w")
    c_st2r = Ctr(nc, "c_st2r")

    def mkvs(n):
        base = c_vsy.val
        c_vsy.val += n
        return base

    def VS(v, base, i):
        v.engine_nop().then_inc(c_vsy.sem, 1)
        v.wait_ge(c_vsy.sem, base + i + 1)

    scopeA = contextlib.ExitStack()
    sbA = lambda n, s, d=BF16: scopeA.enter_context(nc.sbuf_tensor(n, s, d))
    x_sl = [sbA(f"x_sl{i}", [128, DC * TB]) for i in range(2)]
    x_ld = [Ctr(nc, f"x_ld{i}") for i in range(2)]
    x_fr = [Ctr(nc, f"x_fr{i}") for i in range(2)]
    ring = Ring(nc, planA, scopeA, "wring", 4, 2048)
    q_fm = sbA("q_fm", [128, HQ * TB])
    qhat = sbA("qhat", [128, HQ * TB])
    k_fm = sbA("k_fm", [128, HKV * TB])
    v_tm = sbA("v_tm", [64, CPB * 512])
    khat = sbA("khat", [64, 2 * HQ * 128])
    ssc = sbA("sscb", [64, 2 * HQ * 64])
    o_fm = sbA("o_fm", [128, HQ * TB])
    kv_bf = sbA("kv_bf", [128, HQ * DH])
    pre = sbA("preb", [128, DC * TB])
    sqb = sbA("sqb", [128, 2 * TB])
    rs1 = sbA("rs1", [128, TB], F32)
    rs2 = sbA("rs2", [128, TB], F32)
    rs3 = sbA("rs3", [128, TB], F32)
    lnrow = sbA("lnrow", [1, 7 * TB], F32)
    lnbfr = sbA("lnbfr", [1, 2 * TB])
    rbb = sbA("rbb", [128, 2 * TB])
    cos_o = sbA("cos_o", [128, T])
    sin_o = sbA("sin_o", [128, T])
    cos_p = sbA("cos_p", [128, T])
    sin_p = sbA("sin_p", [128, T])
    diagT = sbA("diagTs", [64, HQ * 64], F32)
    qdecb = sbA("qdecbs", [128, HQ * 64], F32)
    kdec = sbA("kdecs", [64, 16], F32)
    x1b = pre

    def _bc8(ec):
        return bass.AP(qdecb, ec * 64, [[HQ * 64, 128], [0, 8], [1, 64]])

    def emit_const(sy):
        for dst, src in [(ident, ident_d), (onesc, onesc_d), (onesr, onesr_d),
                         (oneh, oneh_d), (onehr, onehr_d), (selb, selb_d),
                         (cos_o, cosb_d), (sin_o, sinb_d),
                         (cos_p, cosp_d), (sin_p, sinp_d), (diagT, diagT_d),
                         (qdecb, qdecb_d), (kdec, kdec_d)]:
            sy.dma_start(dst[:, :], src.ap()).then_inc(c_cst.sem, 16)
    planA.add("SYNC", emit_const)
    c_cst.val += 16 * 13
    CST = c_cst.val

    def emit_kv0(v):
        v.wait_ge(c_cst.sem, CST)
        v.memset(kv_bf[:, :], 0.0)
        v.engine_nop().then_inc(c_kvs.sem, 1)
    planA.add("DVE", emit_kv0)
    c_kvs.val += 1

    n_own_chunk = 0

    for bi in range(min(NBI, 2 * NBLK)):
        warm = bi < NBLK
        blk = bi % NBLK
        xslot = bi % 2
        xsrc = xp if warm else xb
        xs = x_sl[xslot]
        cos_t, sin_t = (cos_p, sin_p) if warm else (cos_o, sin_o)

        need_xfree = x_fr[xslot].val

        def pool_x(g, bi=bi, xsrc=xsrc, blk=blk, need_xfree=need_xfree,
                   xslot=xslot):
            if bi > 1:
                g.wait_ge(x_fr[xslot].sem, need_xfree)
            g.wait_ge(c_cst.sem, CST)
            g.dma_start(x_sl[xslot][:, :], xsrc.ap()[blk, :, :]).then_inc(
                x_ld[xslot].sem, 16)
        planA.add("POOL", pool_x)
        x_ld[xslot].val += 16
        XNEED = x_ld[xslot].val

        if not warm:
            for ec in range(HQ):
                h = ring.load(wqT.ap()[ec, :, :])
                bank = pbk[ec % 2]
                gen0 = bank.fre.val

                RDY = bank.rdy.val + 1

                def pe_q(t, h=h, bank=bank, gen0=gen0, xs=xs, XNEED=XNEED,
                         xslot=xslot, RDY=RDY):
                    h.wait(t)
                    t.wait_ge(x_ld[xslot].sem, XNEED)
                    if gen0 > 0:
                        t.wait_ge(bank.fre.sem, gen0)
                    ins = None
                    for dc in range(DC):
                        ins = t.matmul(bank.t[:, :],
                                       h.ap[:, dc * 128:(dc + 1) * 128],
                                       xs[:, dc * TB:(dc + 1) * TB],
                                       start=(dc == 0), stop=(dc == DC - 1))
                    ins.then_inc(bank.rdy.sem, 1)
                    t.wait_ge(bank.rdy.sem, RDY)
                    h.free_on(t.nop())
                planA.add("PE", pe_q)
                bank.rdy.val += 1
                bank.fre.val += 1

                vs0 = mkvs(3)

                def dve_rope(v, bank=bank, ec=ec, blk=blk, RDY=RDY,
                             cos_t=cos_t, sin_t=sin_t, vs0=vs0):
                    v.wait_ge(bank.rdy.sem, RDY)
                    VS(v, vs0, 0)
                    ps = bank.t
                    v.tensor_tensor(rs3[0:64, :], ps[64:128, :],
                                    sin_t[0:64, blk * TB:(blk + 1) * TB],
                                    ALU.mult)
                    v.tensor_tensor(rs3[64:128, :], ps[0:64, :],
                                    sin_t[64:128, blk * TB:(blk + 1) * TB],
                                    ALU.mult)
                    v.tensor_tensor(rs2[:, :], ps[:, :],
                                    cos_t[:, blk * TB:(blk + 1) * TB],
                                    ALU.mult).then_inc(bank.fre.sem, 1)
                    VS(v, vs0, 1)
                    v.tensor_tensor(q_fm[:, ec * TB:(ec + 1) * TB],
                                    rs2[:, :], rs3[:, :], ALU.add)
                    VS(v, vs0, 2)
                    if NO_QHAT:
                        v.tensor_copy(qhat[:, ec * TB:(ec + 1) * TB],
                                      q_fm[:, ec * TB:(ec + 1) * TB])
                        v.engine_nop().then_inc(c_rope.sem, 1)
                    else:
                        v.tensor_tensor(qhat[:, ec * TB:(ec + 1) * TB],
                                        q_fm[:, ec * TB:(ec + 1) * TB],
                                        _bc8(ec), ALU.mult).then_inc(c_rope.sem, 1)
                planA.add("DVE", dve_rope)
                c_rope.val += 1

        for ec in range(HKV):
            h = ring.load(wkT.ap()[ec, :, :])
            bank = pbk[ec % 2]
            gen0 = bank.fre.val

            RDY = bank.rdy.val + 1

            def pe_k(t, h=h, bank=bank, gen0=gen0, xs=xs, XNEED=XNEED,
                     xslot=xslot, RDY=RDY):
                h.wait(t)
                t.wait_ge(x_ld[xslot].sem, XNEED)
                if gen0 > 0:
                    t.wait_ge(bank.fre.sem, gen0)
                ins = None
                for dc in range(DC):
                    ins = t.matmul(bank.t[:, :],
                                   h.ap[:, dc * 128:(dc + 1) * 128],
                                   xs[:, dc * TB:(dc + 1) * TB],
                                   start=(dc == 0), stop=(dc == DC - 1))
                ins.then_inc(bank.rdy.sem, 1)
                t.wait_ge(bank.rdy.sem, RDY)
                h.free_on(t.nop())
            planA.add("PE", pe_k)
            bank.rdy.val += 1
            bank.fre.val += 1

            vs0 = mkvs(2)

            def dve_krope(v, bank=bank, ec=ec, blk=blk, RDY=RDY,
                          cos_t=cos_t, sin_t=sin_t, vs0=vs0):
                v.wait_ge(bank.rdy.sem, RDY)
                VS(v, vs0, 0)
                ps = bank.t
                v.tensor_tensor(rs3[0:64, :], ps[64:128, :],
                                sin_t[0:64, blk * TB:(blk + 1) * TB], ALU.mult)
                v.tensor_tensor(rs3[64:128, :], ps[0:64, :],
                                sin_t[64:128, blk * TB:(blk + 1) * TB], ALU.mult)
                v.tensor_tensor(rs2[:, :], ps[:, :],
                                cos_t[:, blk * TB:(blk + 1) * TB],
                                ALU.mult).then_inc(bank.fre.sem, 1)
                VS(v, vs0, 1)
                v.tensor_tensor(k_fm[:, ec * TB:(ec + 1) * TB],
                                rs2[:, :], rs3[:, :], ALU.add).then_inc(
                    c_krope.sem, 1)
            planA.add("DVE", dve_krope)
            c_krope.val += 1

        vh = [ring.load(wvR.ap()[j, :, :]) for j in range(4)]
        for tc in range(4):
            bank = pbk[2 + tc % 2]
            gen0 = bank.fre.val

            RDY = bank.rdy.val + 1

            def pe_v(t, vh=vh, bank=bank, gen0=gen0, tc=tc, xs=xs, XNEED=XNEED,
                     xslot=xslot, RDY=RDY):
                if tc == 0:
                    for hh in vh:
                        hh.wait(t)
                    t.wait_ge(x_ld[xslot].sem, XNEED)
                if gen0 > 0:
                    t.wait_ge(bank.fre.sem, gen0)
                ins = None
                for dc in range(DC):
                    ins = t.matmul(
                        bank.t[:, :],
                        xs[:, dc * TB + tc * 128: dc * TB + (tc + 1) * 128],
                        vh[dc // 4].ap[:, (dc % 4) * 512:(dc % 4 + 1) * 512],
                        start=(dc == 0), stop=(dc == DC - 1))
                ins.then_inc(bank.rdy.sem, 1)
                if tc == 3:
                    t.wait_ge(bank.rdy.sem, RDY)
                    for hh in vh:
                        hh.free_on(t.nop())
            planA.add("PE", pe_v)
            bank.rdy.val += 1
            bank.fre.val += 1

            FREV = bank.fre.val

            def act_vev(s, bank=bank, tc=tc, RDY=RDY, FREV=FREV):
                s.wait_ge(bank.rdy.sem, RDY)
                s.copy(v_tm[:, (2 * tc) * 512:(2 * tc + 1) * 512],
                       bank.t[0:64, :])
                s.copy(v_tm[:, (2 * tc + 1) * 512:(2 * tc + 2) * 512],
                       bank.t[64:128, :]).then_inc(bank.fre.sem, 1)
                s.wait_ge(bank.fre.sem, FREV)
                s.nop().then_inc(c_vtm.sem, 1)
            planA.add("ACT", act_vev)
            c_vtm.val += 1

        if warm:
            vr_last = pbk[3].rdy.val

            def pe_xfree(t, xslot=xslot, vr_last=vr_last):
                t.wait_ge(pbk[3].rdy.sem, vr_last)
                t.nop().then_inc(x_fr[xslot].sem, 1)
            planA.add("PE", pe_xfree)
            x_fr[xslot].val += 1

        KR = c_krope.val
        VR = c_vtm.val
        for cc in range(0 if NO_LGT else CPB):
            gchunk = bi * CPB + cc
            kv_base = c_kvs.val

            genv = pbt.fre.val

            def pe_tr(t, cc=cc, genv=genv, KR=KR):
                t.wait_ge(c_krope.sem, KR)
                if genv > 0:
                    t.wait_ge(pbt.fre.sem, genv)
                ins = None
                for g in range(4):
                    ins = t.transpose(
                        pbt.t[0:64, g * 128:(g + 1) * 128],
                        k_fm[:, g * TB + cc * 64: g * TB + (cc + 1) * 64],
                        ident[:, :])
                ins.then_inc(pbt.rdy.sem, 1)
            planA.add("PE", pe_tr)
            pbt.rdy.val += 1
            pbt.fre.val += 1
            TRDY = pbt.rdy.val
            DCHW = max(0, gchunk - 1)

            TFREV = pbt.fre.val

            def act_khat(s, cc=cc, TRDY=TRDY, DCHW=DCHW, TFREV=TFREV):
                s.wait_ge(pbt.rdy.sem, TRDY)
                if DCHW > 0:
                    s.wait_ge(c_dch.sem, DCHW)
                ins = None
                for h in range(HQ):
                    g = h // 4
                    ins = s.activation(
                        khat[0:64, ((cc % 2) * HQ + h) * 128:
                             ((cc % 2) * HQ + h + 1) * 128],
                        pbt.t[0:64, g * 128:(g + 1) * 128],
                        AF.Copy, scale=kdec[:, h:h + 1])
                ins.then_inc(pbt.fre.sem, 1)
                s.wait_ge(pbt.fre.sem, TFREV)
                s.nop().then_inc(c_khat.sem, 1)
            planA.add("ACT", act_khat)
            c_khat.val += 1
            KHAT_CC = c_khat.val

            if not warm:
                QR = c_rope.val
                for hg in range(2):
                    bank = pbk[hg]
                    gen0 = bank.fre.val

                    def pe_st(t, bank=bank, hg=hg, cc=cc, gen0=gen0, QR=QR):
                        t.wait_ge(c_rope.sem, QR)
                        if gen0 > 0:
                            t.wait_ge(bank.fre.sem, gen0)
                        ins = None
                        for hh in range(8):
                            h = hg * 8 + hh
                            g = h // 4
                            ins = t.matmul(
                                bank.t[0:64, hh * 64:(hh + 1) * 64],
                                k_fm[:, g * TB + cc * 64: g * TB + (cc + 1) * 64],
                                q_fm[:, h * TB + cc * 64: h * TB + (cc + 1) * 64],
                                start=True, stop=True)
                        ins.then_inc(bank.rdy.sem, 1)
                    planA.add("PE", pe_st)
                    bank.rdy.val += 1
                    bank.fre.val += 1
                    RDY = bank.rdy.val
                    OCH = max(0, n_own_chunk - 1)

                    def dve_diag(v, bank=bank, hg=hg, cc=cc, RDY=RDY, OCH=OCH):
                        v.wait_ge(bank.rdy.sem, RDY)
                        if OCH > 0:
                            v.wait_ge(c_och.sem, OCH)
                        v.tensor_tensor(
                            ssc[0:64, (cc % 2) * HQ * 64 + hg * 512:
                                (cc % 2) * HQ * 64 + (hg + 1) * 512],
                            bank.t[0:64, :],
                            diagT[:, hg * 512:(hg + 1) * 512],
                            ALU.mult).then_inc(bank.fre.sem, 1)
                        v.engine_nop().then_inc(c_ssc.sem, 1)
                    planA.add("DVE", dve_diag)
                    c_ssc.val += 1

                SSC = c_ssc.val
                for hg in range(2):
                    bank = pbk[2 + hg]
                    gen0 = bank.fre.val

                    def pe_o(t, bank=bank, hg=hg, cc=cc,
                             gen0=gen0, SSC=SSC, kv_base=kv_base):
                        t.wait_ge(c_ssc.sem, SSC)
                        t.wait_ge(c_kvs.sem, kv_base)
                        if gen0 > 0:
                            t.wait_ge(bank.fre.sem, gen0)
                        ins = None
                        for hh in range(8):
                            h = hg * 8 + hh
                            g = h // 4
                            t.matmul(
                                bank.t[:, hh * 64:(hh + 1) * 64],
                                v_tm[:, cc * 512 + g * 128:
                                     cc * 512 + (g + 1) * 128],
                                ssc[0:64, ((cc % 2) * HQ + h) * 64:
                                    ((cc % 2) * HQ + h + 1) * 64],
                                start=True, stop=False)
                            ins = t.matmul(
                                bank.t[:, hh * 64:(hh + 1) * 64],
                                kv_bf[:, h * DH:(h + 1) * DH],
                                qhat[:, h * TB + cc * 64: h * TB + (cc + 1) * 64],
                                start=False, stop=True)
                        ins.then_inc(bank.rdy.sem, 1)
                    planA.add("PE", pe_o)
                    bank.rdy.val += 1
                    bank.fre.val += 1
                    RDY = bank.rdy.val

                    FREV = bank.fre.val

                    def act_oev(s, bank=bank, hg=hg, cc=cc, RDY=RDY, FREV=FREV):
                        s.wait_ge(bank.rdy.sem, RDY)
                        dst = bass.AP(o_fm, hg * 8 * TB + cc * 64,
                                      [[HQ * TB, 128], [TB, 8], [1, 64]])
                        s.copy(dst, bank.t[:, :]).then_inc(bank.fre.sem, 1)
                        s.wait_ge(bank.fre.sem, FREV)
                        s.nop().then_inc(c_oev.sem, 1)
                    planA.add("ACT", act_oev)
                    c_oev.val += 1

                o_rdy_last = pbk[3].rdy.val

                def pe_och(t, o_rdy_last=o_rdy_last):
                    t.wait_ge(pbk[3].rdy.sem, o_rdy_last)
                    t.nop().then_inc(c_och.sem, 1)
                planA.add("PE", pe_och)
                c_och.val += 1
                n_own_chunk += 1

            KH = KHAT_CC
            for quad in range(4):
                bank = pbk[6]
                gen0 = bank.fre.val

                def pe_delta(t, bank=bank, quad=quad, cc=cc,
                             gen0=gen0, KH=KH, VR=VR):
                    t.wait_ge(c_khat.sem, KH)
                    t.wait_ge(c_vtm.sem, VR)
                    if gen0 > 0:
                        t.wait_ge(bank.fre.sem, gen0)
                    ins = None
                    for r in range(4):
                        h = quad * 4 + r
                        g = h // 4
                        ins = t.matmul(
                            bank.t[:, r * 128:(r + 1) * 128],
                            khat[0:64, ((cc % 2) * HQ + h) * 128:
                                 ((cc % 2) * HQ + h + 1) * 128],
                            v_tm[:, cc * 512 + g * 128: cc * 512 + (g + 1) * 128],
                            start=True, stop=True)
                    ins.then_inc(bank.rdy.sem, 1)
                planA.add("PE", pe_delta)
                bank.rdy.val += 1
                bank.fre.val += 1
                RDY = bank.rdy.val

                def dve_kv(v, bank=bank, quad=quad, RDY=RDY,
                           kv_base=kv_base):
                    v.wait_ge(bank.rdy.sem, RDY)
                    if kv_base > 1:
                        v.wait_ge(c_kvs.sem, kv_base)
                    ins = None
                    for r in range(4):
                        h = quad * 4 + r
                        ins = v.scalar_tensor_tensor(
                            kv_bf[:, h * DH:(h + 1) * DH],
                            kv_bf[:, h * DH:(h + 1) * DH],
                            _BLKD[h],
                            bank.t[:, r * 128:(r + 1) * 128],
                            ALU.mult, ALU.add)
                        ins.then_inc(c_kvs.sem, 1)
                    v.engine_nop().then_inc(bank.fre.sem, 1)
                planA.add("DVE", dve_kv)
                c_kvs.val += 4

            d_rdy_last = pbk[6].rdy.val

            def pe_dch(t, d_rdy_last=d_rdy_last):
                t.wait_ge(pbk[6].rdy.sem, d_rdy_last)
                t.nop().then_inc(c_dch.sem, 1)
            planA.add("PE", pe_dch)
            c_dch.val += 1

        if warm:
            continue

        OEV = c_oev.val
        X1ST_prev = c_x1st.val
        woh = [ring.load(woT.ap()[dc, :, :]) for dc in range(DC)]
        for dc in range(DC):
            h = woh[dc]
            bank = pbk[dc % 2]
            gen0 = bank.fre.val

            RDY = bank.rdy.val + 1

            def pe_oproj(t, h=h, bank=bank, dc=dc, gen0=gen0, OEV=OEV, RDY=RDY):
                h.wait(t)
                if dc == 0:
                    t.wait_ge(c_oev.sem, OEV)
                if gen0 > 0:
                    t.wait_ge(bank.fre.sem, gen0)
                ins = None
                for ec in range(16):
                    ins = t.matmul(bank.t[:, :],
                                   h.ap[:, ec * 128:(ec + 1) * 128],
                                   o_fm[:, ec * TB:(ec + 1) * TB],
                                   start=(ec == 0), stop=(ec == 15))
                ins.then_inc(bank.rdy.sem, 1)
                t.wait_ge(bank.rdy.sem, RDY)
                h.free_on(t.nop())
            planA.add("PE", pe_oproj)
            bank.rdy.val += 1
            bank.fre.val += 1

            def dve_pre(v, bank=bank, dc=dc, xs=xs, RDY=RDY, X1ST_prev=X1ST_prev):
                v.wait_ge(bank.rdy.sem, RDY)
                if dc == 0 and X1ST_prev > 0:
                    v.wait_ge(c_x1st.sem, X1ST_prev)
                v.tensor_tensor(pre[:, dc * TB:(dc + 1) * TB],
                                xs[:, dc * TB:(dc + 1) * TB],
                                bank.t[:, :], ALU.add).then_inc(
                    bank.fre.sem, 1)
                v.engine_nop().then_inc(c_pre.sem, 1)
            planA.add("DVE", dve_pre)
            c_pre.val += 1

        def dve_xfree(v, xslot=xslot):
            v.engine_nop().then_inc(x_fr[xslot].sem, 1)
        planA.add("DVE", dve_xfree)
        x_fr[xslot].val += 1

        PRE_ALL = c_pre.val
        g4 = pbk[4].fre.val
        g5 = pbk[5].fre.val
        sq_base = c_sq.val
        sqr_base = c_sqr.val
        for dc in range(DC):
            def act_sq(s, dc=dc, PRE_ALL=PRE_ALL, sqr_base=sqr_base):
                s.wait_ge(c_pre.sem, PRE_ALL - (DC - 1) + dc)
                if dc >= 2:
                    s.wait_ge(c_sqr.sem, sqr_base + dc - 1)
                s.square(sqb[:, (dc % 2) * TB:(dc % 2 + 1) * TB],
                         pre[:, dc * TB:(dc + 1) * TB]).then_inc(c_sq.sem, 1)
            planA.add("ACT", act_sq)
            c_sq.val += 1
            SQ = c_sq.val

            def pe_stats(t, dc=dc, SQ=SQ, PRE_ALL=PRE_ALL, g4=g4, g5=g5,
                         sqr_base=sqr_base):
                if dc == 0:
                    t.wait_ge(c_pre.sem, PRE_ALL)
                    if g4 > 0:
                        t.wait_ge(pbk[4].fre.sem, g4)
                    if g5 > 0:
                        t.wait_ge(pbk[5].fre.sem, g5)
                t.matmul(pbk[4].t[0:1, :], onesc[:, :],
                         pre[:, dc * TB:(dc + 1) * TB],
                         start=(dc == 0), stop=(dc == 15))
                t.wait_ge(c_sq.sem, SQ)
                ins = t.matmul(pbk[5].t[0:1, :], onesc[:, :],
                               sqb[:, (dc % 2) * TB:(dc % 2 + 1) * TB],
                               start=(dc == 0), stop=(dc == 15))
                ins.then_inc(c_sqr.sem, 1)
                if dc == 15:
                    t.wait_ge(c_sqr.sem, sqr_base + 16)
                    t.nop().then_inc(pbk[5].rdy.sem, 1)
            planA.add("PE", pe_stats)
            c_sqr.val += 1
        pbk[4].fre.val += 1
        pbk[5].rdy.val += 1
        pbk[5].fre.val += 1
        SRDY = pbk[5].rdy.val

        vs0 = mkvs(3)

        def dve_lnstats(v, SRDY=SRDY, vs0=vs0):
            v.wait_ge(pbk[5].rdy.sem, SRDY)
            v.tensor_scalar(lnrow[0:1, 0:TB], pbk[4].t[0:1, :], RN, None,
                            ALU.mult).then_inc(pbk[4].fre.sem, 1)
            v.tensor_scalar(lnrow[0:1, 5 * TB:6 * TB], pbk[5].t[0:1, :], RN,
                            None, ALU.mult).then_inc(pbk[5].fre.sem, 1)
            VS(v, vs0, 0)
            v.tensor_tensor(lnrow[0:1, TB:2 * TB], lnrow[0:1, 0:TB],
                            lnrow[0:1, 0:TB], ALU.mult)
            VS(v, vs0, 1)
            v.tensor_tensor(lnrow[0:1, 2 * TB:3 * TB], lnrow[0:1, 5 * TB:6 * TB],
                            lnrow[0:1, TB:2 * TB], ALU.subtract)
            VS(v, vs0, 2)
            v.tensor_scalar(lnrow[0:1, 2 * TB:3 * TB],
                            lnrow[0:1, 2 * TB:3 * TB], EPS, None,
                            ALU.add).then_inc(c_ln.sem, 1)
        planA.add("DVE", dve_lnstats)
        c_ln.val += 1
        LN1 = c_ln.val

        def act_lnsqrt(s, LN1=LN1):
            s.wait_ge(c_ln.sem, LN1)
            s.activation(lnrow[0:1, 3 * TB:4 * TB], lnrow[0:1, 2 * TB:3 * TB],
                         AF.Sqrt).then_inc(c_ln.sem, 1)
        planA.add("ACT", act_lnsqrt)
        c_ln.val += 1
        LNS = c_ln.val

        vs0 = mkvs(2)

        def dve_lnrecip(v, LNS=LNS, vs0=vs0):
            v.wait_ge(c_ln.sem, LNS)
            v.reciprocal(lnrow[0:1, 4 * TB:5 * TB], lnrow[0:1, 3 * TB:4 * TB])
            VS(v, vs0, 0)
            v.tensor_tensor(lnrow[0:1, 6 * TB:7 * TB], lnrow[0:1, 0:TB],
                            lnrow[0:1, 4 * TB:5 * TB], ALU.mult)
            VS(v, vs0, 1)
            v.tensor_copy(lnbfr[0:1, 0:TB], lnrow[0:1, 4 * TB:5 * TB])
            v.tensor_copy(lnbfr[0:1, TB:2 * TB], lnrow[0:1, 6 * TB:7 * TB])
            v.engine_nop().then_inc(c_ln.sem, 1)
        planA.add("DVE", dve_lnrecip)
        c_ln.val += 1
        LN3 = c_ln.val

        for j in range(2):
            bank = pbk[2 + j]
            gen0 = bank.fre.val

            def pe_bc(t, j=j, bank=bank, gen0=gen0, LN3=LN3):
                t.wait_ge(c_ln.sem, LN3)
                if gen0 > 0:
                    t.wait_ge(bank.fre.sem, gen0)
                t.matmul(bank.t[:, :], onesr[:, :],
                         lnbfr[0:1, j * TB:(j + 1) * TB],
                         start=True, stop=True).then_inc(bank.rdy.sem, 1)
            planA.add("PE", pe_bc)
            bank.rdy.val += 1
            bank.fre.val += 1
            RDY = bank.rdy.val

            FREV = bank.fre.val

            def act_bc(s, j=j, bank=bank, RDY=RDY, FREV=FREV):
                s.wait_ge(bank.rdy.sem, RDY)
                s.copy(rbb[:, j * TB:(j + 1) * TB], bank.t[:, :]).then_inc(
                    bank.fre.sem, 1)
                s.wait_ge(bank.fre.sem, FREV)
                s.nop().then_inc(c_ln.sem, 1)
            planA.add("ACT", act_bc)
            c_ln.val += 1
        LN4 = c_ln.val

        X1ST0 = c_x1st.val
        for dc in range(DC):
            vs0 = mkvs(1)

            X1P = c_x1a.val

            def dve_x1(v, dc=dc, LN4=LN4, X1ST0=X1ST0, vs0=vs0, X1P=X1P):
                if dc == 0:
                    v.wait_ge(c_ln.sem, LN4)
                    if X1ST0 > 0:
                        v.wait_ge(c_x1st.sem, X1ST0)
                else:
                    v.wait_ge(c_x1a.sem, X1P)
                v.tensor_tensor(rs2[:, :], pre[:, dc * TB:(dc + 1) * TB],
                                rbb[:, 0:TB], ALU.mult)
                VS(v, vs0, 0)
                v.tensor_tensor(x1b[:, dc * TB:(dc + 1) * TB], rs2[:, :],
                                rbb[:, TB:2 * TB], ALU.subtract).then_inc(
                    c_x1a.sem, 1)
            planA.add("DVE", dve_x1)
            c_x1a.val += 1
            X1A = c_x1a.val

            def sy_x1st(sy, dc=dc, blk=blk, X1A=X1A):
                sy.wait_ge(c_x1a.sem, X1A)
                sy.dma_start(x1d.ap()[dc, :, blk * TB:(blk + 1) * TB],
                             x1b[:, dc * TB:(dc + 1) * TB]).then_inc(
                    c_x1st.sem, 16)
            planA.add("SYNC", sy_x1st)
            c_x1st.val += 16

        rh = ring.load(rtr.ap()[:, :], w=128)
        bank = pbk[6]
        gen0 = bank.fre.val
        X1A_ALL = c_x1a.val

        RRDY = bank.rdy.val + 1

        def pe_router(t, rh=rh, bank=bank, gen0=gen0, X1A_ALL=X1A_ALL,
                      RRDY=RRDY):
            rh.wait(t)
            t.wait_ge(c_x1a.sem, X1A_ALL)
            if gen0 > 0:
                t.wait_ge(bank.fre.sem, gen0)
            ins = None
            for dc in range(DC):
                ins = t.matmul(bank.t[0:8, :], rh.ap[:, dc * 8:(dc + 1) * 8],
                               x1b[:, dc * TB:(dc + 1) * TB],
                               start=(dc == 0), stop=(dc == 15))
            ins.then_inc(bank.rdy.sem, 1)
            t.wait_ge(bank.rdy.sem, RRDY)
            rh.free_on(t.nop())
        planA.add("PE", pe_router)
        bank.rdy.val += 1
        bank.fre.val += 1

        def dve_logit(v, bank=bank, blk=blk, RRDY=RRDY):
            v.wait_ge(bank.rdy.sem, RRDY)
            v.tensor_scalar(logit_sb[:, blk * TB:(blk + 1) * TB],
                            bank.t[0:8, :], 0.0, None, ALU.add).then_inc(
                bank.fre.sem, 1)
        planA.add("DVE", dve_logit)

    X1ST_ALL = c_x1st.val

    def sy_enda(sy):
        sy.wait_ge(c_x1st.sem, X1ST_ALL)
    planA.add("SYNC", sy_enda)

    if DEBUG2:
        dbg_d = {
            "dbg_q": nc.dram_tensor("dq", [128, HQ * TB], BF16, kind="ExternalOutput"),
            "dbg_k": nc.dram_tensor("dk", [128, HKV * TB], BF16, kind="ExternalOutput"),
            "dbg_qh": nc.dram_tensor("dqh", [128, HQ * TB], BF16, kind="ExternalOutput"),
            "dbg_o": nc.dram_tensor("do", [128, HQ * TB], BF16, kind="ExternalOutput"),
            "dbg_kv": nc.dram_tensor("dkv", [128, HQ * DH], BF16, kind="ExternalOutput"),
            "dbg_pre": nc.dram_tensor("dpre", [128, DC * TB], BF16, kind="ExternalOutput"),
            "dbg_vtm": nc.dram_tensor("dvt", [64, CPB * 512], BF16, kind="ExternalOutput"),
            "dbg_lg": nc.dram_tensor("dlg", [8, T], F32, kind="ExternalOutput"),
        }

        def sy_dbg(sy):
            sy.dma_start(dbg_d["dbg_q"].ap(), q_fm[:, :]).then_inc(c_x1st.sem, 16)
            sy.dma_start(dbg_d["dbg_k"].ap(), k_fm[:, :]).then_inc(c_x1st.sem, 16)
            sy.dma_start(dbg_d["dbg_qh"].ap(), qhat[:, :]).then_inc(c_x1st.sem, 16)
            sy.dma_start(dbg_d["dbg_o"].ap(), o_fm[:, :]).then_inc(c_x1st.sem, 16)
            sy.dma_start(dbg_d["dbg_kv"].ap(), kv_bf[:, :]).then_inc(c_x1st.sem, 16)
            sy.dma_start(dbg_d["dbg_pre"].ap(), pre[:, :]).then_inc(c_x1st.sem, 16)
            sy.dma_start(dbg_d["dbg_vtm"].ap(), v_tm[:, :]).then_inc(c_x1st.sem, 16)
            sy.dma_start(dbg_d["dbg_lg"].ap(), logit_sb[:, :]).then_inc(c_x1st.sem, 16)
            sy.wait_ge(c_x1st.sem, X1ST_ALL + 16 * 8)
        planA.add("SYNC", sy_dbg)
        c_x1st.val += 16 * 8
    scopeA.close()

    if SKIP_B:
        with nc.Block("phaseA") as blk:
            @blk.tensor
            def _(t):
                planA.emit("PE", t)

            @blk.vector
            def _(v):
                planA.emit("DVE", v)

            @blk.scalar
            def _(s):
                planA.emit("ACT", s)

            @blk.sync
            def _(sy):
                planA.emit("SYNC", sy)

            @blk.gpsimd
            def _(g):
                planA.emit("POOL", g)
        nc.ctx.close()
        return nc

    scopeB = contextlib.ExitStack()
    sbB = lambda n, s, d=BF16: scopeB.enter_context(nc.sbuf_tensor(n, s, d))
    ringB = Ring(nc, planB, scopeB, "wringB", 5, 2048)
    x1th = sbB("x1th", [128, DC * THW])
    y_acc = sbB("y_acc", [128, DC * THW])
    h_buf = [sbB(f"h_buf{i}", [128, FC * TSW]) for i in range(TS)]
    msb = sbB("msbb", [128, 2 * THW])
    tmpA = [sbB(f"tmpA{i}", [128, 2 * TSW]) for i in range(2)]
    tmpSA = [sbB(f"tmpSA{i}", [128, 2 * TSW]) for i in range(2)]
    tmpB = [sbB(f"tmpB{i}", [128, 2 * TSW]) for i in range(2)]
    l2x = [sbB(f"l2x{i}", [128, THW]) for i in range(2)]
    sq2 = [sbB(f"sq2_{i}", [128, THW]) for i in range(2)]
    rowsc = sbB("rowsc", [1, 16 * 512], F32)
    eqs = sbB("eqs", [1, 8 * 512])
    rb2 = sbB("rb2", [128, 2 * THW])
    ostg = [sbB(f"ostg{i}", [128, THW], F32) for i in range(2)]

    def rsl(i, n=1):
        return rowsc[0:1, i * 512:(i + n) * 512]

    for gq in range(0 if NOMASK else 4):
        for e in range(NE):
            bank = pbk[5 + (gq * 8 + e) % 2]
            gen0 = bank.fre.val

            def pe_mrow(t, e=e, gq=gq, bank=bank, gen0=gen0):
                if gen0 > 0:
                    t.wait_ge(bank.fre.sem, gen0)
                t.matmul(bank.t[0:1, :], oneh[:, e:e + 1],
                         logit_sb[:, gq * 512:(gq + 1) * 512],
                         start=True, stop=True).then_inc(bank.rdy.sem, 1)
            planB.add("PE", pe_mrow)
            bank.rdy.val += 1
            bank.fre.val += 1
            RDY = bank.rdy.val

            FREV = bank.fre.val

            def act_mrow(s, e=e, bank=bank, RDY=RDY, FREV=FREV):
                s.wait_ge(bank.rdy.sem, RDY)
                s.copy(rsl(e), bank.t[0:1, :]).then_inc(bank.fre.sem, 1)
                s.wait_ge(bank.fre.sem, FREV)
                s.nop().then_inc(c_msk.sem, 1)
            planB.add("ACT", act_mrow)
            c_msk.val += 1
        MSK1 = c_msk.val

        vs0 = mkvs(2)

        def dve_mtree(v, MSK1=MSK1, vs0=vs0):
            v.wait_ge(c_msk.sem, MSK1)
            for j in range(4):
                v.tensor_tensor(rsl(8 + j), rsl(2 * j), rsl(2 * j + 1), ALU.max)
            VS(v, vs0, 0)
            v.tensor_tensor(rsl(14), rsl(8), rsl(9), ALU.max)
            v.tensor_tensor(rsl(15), rsl(10), rsl(11), ALU.max)
            VS(v, vs0, 1)
            v.tensor_tensor(rsl(12), rsl(14), rsl(15), ALU.max)
            v.engine_nop().then_inc(c_msk.sem, 1)
        planB.add("DVE", dve_mtree)
        c_msk.val += 1
        MSK2 = c_msk.val

        def act_score(s, MSK2=MSK2):
            s.wait_ge(c_msk.sem, MSK2)
            s.activation(rsl(13), rsl(12), AF.Sigmoid).then_inc(c_msk.sem, 1)
        planB.add("ACT", act_score)
        c_msk.val += 1
        MSK3 = c_msk.val

        vs0 = mkvs(NE)

        def dve_msc(v, MSK3=MSK3, vs0=vs0):
            v.wait_ge(c_msk.sem, MSK3)
            for e in range(NE):
                v.tensor_tensor(rsl(8 + (e % 2)), rsl(e), rsl(12), ALU.is_ge)
                VS(v, vs0, e)
                v.tensor_tensor(eqs[0:1, e * 512:(e + 1) * 512],
                                rsl(8 + (e % 2)), rsl(13), ALU.mult)
            v.engine_nop().then_inc(c_msk.sem, 1)
        planB.add("DVE", dve_msc)
        c_msk.val += 1
        MSK4 = c_msk.val

        bank = pbk[5 + gq % 2]
        gen0 = bank.fre.val

        def pe_mscp(t, gq=gq, bank=bank, gen0=gen0, MSK4=MSK4):
            t.wait_ge(c_msk.sem, MSK4)
            if gen0 > 0:
                t.wait_ge(bank.fre.sem, gen0)
            ins = None
            for e in range(NE):
                ins = t.matmul(bank.t[0:8, :],
                               onehr[0:1, e * 8:(e + 1) * 8],
                               eqs[0:1, e * 512:(e + 1) * 512],
                               start=(e == 0), stop=(e == NE - 1))
            ins.then_inc(bank.rdy.sem, 1)
        planB.add("PE", pe_mscp)
        bank.rdy.val += 1
        bank.fre.val += 1
        RDY = bank.rdy.val

        FREV = bank.fre.val

        def act_mscp(s, gq=gq, bank=bank, RDY=RDY, FREV=FREV):
            s.wait_ge(bank.rdy.sem, RDY)
            s.copy(mscP[:, gq * 512:(gq + 1) * 512],
                   bank.t[0:8, :]).then_inc(bank.fre.sem, 1)
            s.wait_ge(bank.fre.sem, FREV)
            s.nop().then_inc(c_msk.sem, 1)
        planB.add("ACT", act_mscp)
        c_msk.val += 1
    MSKF = c_msk.val

    sav_hist = {}
    y_hist = {}
    hb_hist = {}
    gu_rot = [0]
    y_rot = [0]
    for th in range(0 if NOTAIL else MKTH):
        OUT_prev = c_out.val

        def sy_x1ld(sy, th=th, OUT_prev=OUT_prev):
            if th > 0:
                sy.wait_ge(c_out.sem, OUT_prev)
            for dq in range(4):
                src = bass.AP(x1d, dq * 4 * 128 * T + th * THW,
                              [[T, 128], [128 * T, 4], [1, THW]])
                dst = bass.AP(x1th, dq * 4 * THW,
                              [[DC * THW, 128], [THW, 4], [1, THW]])
                sy.dma_start(dst, src).then_inc(c_x1ld.sem, 16)
        planB.add("SYNC", sy_x1ld)
        c_x1ld.val += 64
        X1LD = c_x1ld.val

        for e in range(MKNE):
            if e < NE:
                for ts in range(TS):
                    bank = pbk[6]
                    gen0 = bank.fre.val

                    def pe_msb(t, e=e, th=th, ts=ts, bank=bank, gen0=gen0,
                               MSKF=MSKF):
                        t.wait_ge(c_msk.sem, MSKF)
                        if gen0 > 0:
                            t.wait_ge(bank.fre.sem, gen0)
                        t.matmul(bank.t[:, :],
                                 selb[:, e * 128:(e + 1) * 128],
                                 mscP[:, th * THW + ts * TSW:
                                      th * THW + (ts + 1) * TSW],
                                 start=True, stop=True).then_inc(bank.rdy.sem, 1)
                    planB.add("PE", pe_msb)
                    bank.rdy.val += 1
                    bank.fre.val += 1
                    RDY = bank.rdy.val
                    MSBW = max(0, c_msbf.val - 1)

                    FREV = bank.fre.val

                    def act_msb(s, e=e, ts=ts, bank=bank, RDY=RDY, MSBW=MSBW,
                                FREV=FREV):
                        s.wait_ge(bank.rdy.sem, RDY)
                        if MSBW > 0:
                            s.wait_ge(c_msbf.sem, MSBW)
                        s.copy(msb[:, (e % 2) * THW + ts * TSW:
                                   (e % 2) * THW + (ts + 1) * TSW],
                               bank.t[:, :]).then_inc(bank.fre.sem, 1)
                        s.wait_ge(bank.fre.sem, FREV)
                        s.nop().then_inc(c_msb.sem, 1)
                    planB.add("ACT", act_msb)
                    c_msb.val += 1
            MSB = c_msb.val

            for fc in range(FC):
                hG = ringB.load(gW.ap()[e, fc, :, :])
                hU = ringB.load(uW.ap()[e, fc, :, :])
                for ts in range(TS):
                    bankG = pbk[gu_rot[0] % 4]
                    gu_rot[0] += 1
                    bankU = pbk[gu_rot[0] % 4]
                    gu_rot[0] += 1
                    genG = bankG.fre.val
                    genU = bankU.fre.val

                    RG = bankG.rdy.val + 1
                    RU = bankU.rdy.val + 1

                    def pe_gu(t, e=e, fc=fc, ts=ts, hG=hG, hU=hU,
                              bankG=bankG, bankU=bankU, genG=genG, genU=genU,
                              X1LD=X1LD, RG=RG, RU=RU):
                        if fc == 0 and ts == 0:
                            t.wait_ge(c_x1ld.sem, X1LD)
                        if ts == 0:
                            hG.wait(t)
                            hU.wait(t)
                        if genG > 0:
                            t.wait_ge(bankG.fre.sem, genG)
                        ins = None
                        for dc in range(DC):
                            ins = t.matmul(
                                bankG.t[:, :],
                                hG.ap[:, dc * 128:(dc + 1) * 128],
                                x1th[:, dc * THW + ts * TSW:
                                     dc * THW + (ts + 1) * TSW],
                                start=(dc == 0), stop=(dc == DC - 1))
                        ins.then_inc(bankG.rdy.sem, 1)
                        if ts == TS - 1:
                            t.wait_ge(bankG.rdy.sem, RG)
                            hG.free_on(t.nop())
                        if genU > 0:
                            t.wait_ge(bankU.fre.sem, genU)
                        for dc in range(DC):
                            ins = t.matmul(
                                bankU.t[:, :],
                                hU.ap[:, dc * 128:(dc + 1) * 128],
                                x1th[:, dc * THW + ts * TSW:
                                     dc * THW + (ts + 1) * TSW],
                                start=(dc == 0), stop=(dc == DC - 1))
                        ins.then_inc(bankU.rdy.sem, 1)
                        if ts == TS - 1:
                            t.wait_ge(bankU.rdy.sem, RU)
                            hU.free_on(t.nop())
                    planB.add("PE", pe_gu)
                    bankG.rdy.val += 1
                    bankG.fre.val += 1
                    bankU.rdy.val += 1
                    bankU.fre.val += 1
                    par = fc % 2
                    SAVW = sav_hist.get((ts, par), 0)
                    HBW = hb_hist.get((ts, par), 0)

                    if e < NE:
                        def dve_ab(v, e=e, ts=ts, bankG=bankG, bankU=bankU,
                                   RG=RG, RU=RU, MSB=MSB, SAVW=SAVW, fc=fc,
                                   HBW=HBW, par=par):
                            v.wait_ge(bankG.rdy.sem, RG)
                            if fc == 0 and ts == 0:
                                v.wait_ge(c_msb.sem, MSB)
                            if SAVW > 0:
                                v.wait_ge(c_silu.sem, SAVW)
                            if HBW > 0:
                                v.wait_ge(c_hb.sem, HBW)
                            v.tensor_tensor(
                                tmpA[ts][:, par * TSW:(par + 1) * TSW], bankG.t[:, :],
                                msb[:, (e % 2) * THW + ts * TSW:
                                    (e % 2) * THW + (ts + 1) * TSW],
                                ALU.mult).then_inc(bankG.fre.sem, 1)
                            v.engine_nop().then_inc(c_silu.sem, 1)
                            v.wait_ge(bankU.rdy.sem, RU)
                            v.tensor_tensor(
                                tmpB[ts][:, par * TSW:(par + 1) * TSW], bankU.t[:, :],
                                msb[:, (e % 2) * THW + ts * TSW:
                                    (e % 2) * THW + (ts + 1) * TSW],
                                ALU.mult).then_inc(bankU.fre.sem, 1)
                            v.engine_nop().then_inc(c_silu.sem, 1)
                        planB.add("DVE", dve_ab)
                        c_silu.val += 2
                    else:
                        FG = bankG.fre.val
                        FU = bankU.fre.val

                        def act_ab(s, bankG=bankG, bankU=bankU, RG=RG, RU=RU,
                                   ts=ts, SAVW=SAVW, FG=FG, FU=FU, HBW=HBW, par=par):
                            s.wait_ge(bankG.rdy.sem, RG)
                            if SAVW > 0:
                                s.wait_ge(c_silu.sem, SAVW)
                            if HBW > 0:
                                s.wait_ge(c_hb.sem, HBW)
                            s.copy(tmpA[ts][:, par * TSW:(par + 1) * TSW], bankG.t[:, :]).then_inc(
                                bankG.fre.sem, 1)
                            s.wait_ge(bankG.fre.sem, FG)
                            s.nop().then_inc(c_silu.sem, 1)
                            s.wait_ge(bankU.rdy.sem, RU)
                            s.copy(tmpB[ts][:, par * TSW:(par + 1) * TSW], bankU.t[:, :]).then_inc(
                                bankU.fre.sem, 1)
                            s.wait_ge(bankU.fre.sem, FU)
                            s.nop().then_inc(c_silu.sem, 1)
                        planB.add("ACT", act_ab)
                        c_silu.val += 2
                    AB = c_silu.val

                    def act_silu(s, ts=ts, AB=AB, par=par):
                        s.wait_ge(c_silu.sem, AB)
                        s.activation(tmpSA[ts][:, par * TSW:(par + 1) * TSW], tmpA[ts][:, par * TSW:(par + 1) * TSW],
                                     AF.Silu).then_inc(c_silu.sem, 1)
                    planB.add("ACT", act_silu)
                    c_silu.val += 1
                    SAV = c_silu.val
                    sav_hist[(ts, par)] = SAV

                    def dve_h(v, fc=fc, ts=ts, SAV=SAV, par=par):
                        v.wait_ge(c_silu.sem, SAV)
                        v.engine_nop()
                        v.tensor_tensor(h_buf[ts][:, fc * TSW:(fc + 1) * TSW],
                                        tmpSA[ts][:, par * TSW:(par + 1) * TSW], tmpB[ts][:, par * TSW:(par + 1) * TSW],
                                        ALU.mult).then_inc(c_hb.sem, 1)
                    planB.add("DVE", dve_h)
                    c_hb.val += 1
                    hb_hist[(ts, par)] = c_hb.val
            HB = c_hb.val

            if e < NE:
                def dve_msbf(v):
                    v.engine_nop().then_inc(c_msbf.sem, 1)
                planB.add("DVE", dve_msbf)
                c_msbf.val += 1

            for dc in range(DC):
                hD = ringB.load(dW.ap()[e, dc, :, :])
                for ts in range(TS):
                    bank = pbk[4 + y_rot[0] % 3]
                    y_rot[0] += 1
                    gen0 = bank.fre.val

                    RDY = bank.rdy.val + 1

                    def pe_down(t, e=e, dc=dc, ts=ts, hD=hD, bank=bank,
                                gen0=gen0, HB=HB, RDY=RDY):
                        if ts == 0:
                            hD.wait(t)
                        if dc == 0 and ts == 0:
                            t.wait_ge(c_hb.sem, HB)
                        if gen0 > 0:
                            t.wait_ge(bank.fre.sem, gen0)
                        ins = None
                        for fc in range(FC):
                            ins = t.matmul(
                                bank.t[:, :],
                                hD.ap[:, fc * 128:(fc + 1) * 128],
                                h_buf[ts][:, fc * TSW:(fc + 1) * TSW],
                                start=(fc == 0), stop=(fc == FC - 1))
                        ins.then_inc(bank.rdy.sem, 1)
                        if ts == TS - 1:
                            t.wait_ge(bank.rdy.sem, RDY)
                            hD.free_on(t.nop())
                    planB.add("PE", pe_down)
                    bank.rdy.val += 1
                    bank.fre.val += 1

                    YPREV = y_hist.get((dc, ts), 0)
                    y_hist[(dc, ts)] = c_yw.val + 1

                    def dve_y(v, e=e, dc=dc, ts=ts, bank=bank, RDY=RDY,
                              YPREV=YPREV):
                        v.wait_ge(bank.rdy.sem, RDY)
                        if YPREV > 0:
                            v.wait_ge(c_yw.sem, YPREV)
                        dst = y_acc[:, dc * THW + ts * TSW:
                                    dc * THW + (ts + 1) * TSW]
                        if e == 0:
                            v.tensor_scalar(dst, bank.t[:, :], 0.0, None,
                                            ALU.add)
                        else:
                            v.tensor_tensor(dst, dst, bank.t[:, :], ALU.add)
                        v.engine_nop().then_inc(bank.fre.sem, 1)
                        v.engine_nop().then_inc(c_yw.sem, 1)
                    planB.add("DVE", dve_y)
                    c_yw.val += 1
        YW = c_yw.val

        gf = [pbk[i].fre.val for i in range(7)]
        l2w_base = c_l2w.val
        sq2w_base = c_sq2w.val
        st2r_base = c_st2r.val
        for dc in range(DC if TAILCUT >= 2 else 0):
            def dve_l2(v, dc=dc, YW=YW, st2r_base=st2r_base):
                if dc == 0:
                    v.wait_ge(c_yw.sem, YW)
                if dc >= 2:
                    v.wait_ge(c_st2r.sem, st2r_base + dc - 1)
                v.tensor_tensor(l2x[dc % 2][:, :],
                                x1th[:, dc * THW:(dc + 1) * THW],
                                y_acc[:, dc * THW:(dc + 1) * THW],
                                ALU.add).then_inc(c_l2w.sem, 1)
            planB.add("DVE", dve_l2)
            c_l2w.val += 1
            L2 = c_l2w.val

            def act_sq2(s, dc=dc, L2=L2, st2r_base=st2r_base):
                s.wait_ge(c_l2w.sem, L2)
                if dc >= 2:
                    s.wait_ge(c_st2r.sem, st2r_base + dc - 1)
                s.square(sq2[dc % 2][:, :], l2x[dc % 2][:, :]).then_inc(
                    c_sq2w.sem, 1)
            planB.add("ACT", act_sq2)
            c_sq2w.val += 1
            L3 = c_sq2w.val

            def pe_st2(t, dc=dc, L2=L2, L3=L3, gf=gf, st2r_base=st2r_base):
                t.wait_ge(c_l2w.sem, L2)
                if dc == 0:
                    for i in (2, 3, 4, 5):
                        if gf[i] > 0:
                            t.wait_ge(pbk[i].fre.sem, gf[i])
                for half in range(2):
                    t.matmul(pbk[2 + half].t[0:1, :], onesc[:, :],
                             l2x[dc % 2][:, half * 512:(half + 1) * 512],
                             start=(dc == 0), stop=(dc == 15))
                t.wait_ge(c_sq2w.sem, L3)
                ins = None
                for half in range(2):
                    ins = t.matmul(pbk[4 + half].t[0:1, :], onesc[:, :],
                                   sq2[dc % 2][:, half * 512:(half + 1) * 512],
                                   start=(dc == 0), stop=(dc == 15))
                ins.then_inc(c_st2r.sem, 1)
                if dc == 15:
                    t.wait_ge(c_st2r.sem, st2r_base + 16)
                    t.nop().then_inc(pbk[5].rdy.sem, 1)
            planB.add("PE", pe_st2)
            c_st2r.val += 1
        for i in (2, 3, 4):
            pbk[i].fre.val += 1
        pbk[5].rdy.val += 1
        pbk[5].fre.val += 1
        S2 = pbk[5].rdy.val

        if TAILCUT < 3:
            continue

        vs0 = mkvs(3)

        def dve_ln2s(v, S2=S2, vs0=vs0):
            v.wait_ge(pbk[5].rdy.sem, S2)
            v.tensor_scalar(rsl(0), pbk[2].t[0:1, :], RN, None,
                            ALU.mult).then_inc(pbk[2].fre.sem, 1)
            v.tensor_scalar(rsl(1), pbk[3].t[0:1, :], RN, None,
                            ALU.mult).then_inc(pbk[3].fre.sem, 1)
            v.tensor_scalar(rsl(2), pbk[4].t[0:1, :], RN, None,
                            ALU.mult).then_inc(pbk[4].fre.sem, 1)
            v.tensor_scalar(rsl(3), pbk[5].t[0:1, :], RN, None,
                            ALU.mult).then_inc(pbk[5].fre.sem, 1)
            VS(v, vs0, 0)
            v.tensor_tensor(rsl(4, 2), rsl(0, 2), rsl(0, 2), ALU.mult)
            VS(v, vs0, 1)
            v.tensor_tensor(rsl(6, 2), rsl(2, 2), rsl(4, 2), ALU.subtract)
            VS(v, vs0, 2)
            v.tensor_scalar(rsl(6, 2), rsl(6, 2), EPS, None,
                            ALU.add).then_inc(c_ln2.sem, 1)
        planB.add("DVE", dve_ln2s)
        c_ln2.val += 1
        L4 = c_ln2.val

        def act_ln2q(s, L4=L4):
            s.wait_ge(c_ln2.sem, L4)
            s.activation(rsl(8, 2), rsl(6, 2), AF.Sqrt).then_inc(c_ln2.sem, 1)
        planB.add("ACT", act_ln2q)
        c_ln2.val += 1
        L5 = c_ln2.val

        vs0 = mkvs(2)

        def dve_ln2r(v, L5=L5, vs0=vs0):
            v.wait_ge(c_ln2.sem, L5)
            v.reciprocal(rsl(10, 2), rsl(8, 2))
            VS(v, vs0, 0)
            v.tensor_tensor(rsl(12, 2), rsl(0, 2), rsl(10, 2), ALU.mult)
            VS(v, vs0, 1)
            v.tensor_copy(eqs[0:1, 0:THW], rsl(10, 2))
            v.tensor_copy(eqs[0:1, THW:2 * THW], rsl(12, 2))
            v.engine_nop().then_inc(c_ln2.sem, 1)
        planB.add("DVE", dve_ln2r)
        c_ln2.val += 1
        L6 = c_ln2.val

        if TAILCUT < 4:
            continue
        for j in range(2):
            for half in range(2):
                bank = pbk[2 * j + half]
                gen0 = bank.fre.val

                def pe_bc2(t, j=j, half=half, bank=bank, gen0=gen0, L6=L6):
                    t.wait_ge(c_ln2.sem, L6)
                    if gen0 > 0:
                        t.wait_ge(bank.fre.sem, gen0)
                    t.matmul(bank.t[:, :], onesr[:, :],
                             eqs[0:1, j * THW + half * 512:
                                 j * THW + (half + 1) * 512],
                             start=True, stop=True).then_inc(bank.rdy.sem, 1)
                planB.add("PE", pe_bc2)
                bank.rdy.val += 1
                bank.fre.val += 1
                RDY = bank.rdy.val

                FREV = bank.fre.val

                def act_bc2(s, j=j, half=half, bank=bank, RDY=RDY, FREV=FREV):
                    s.wait_ge(bank.rdy.sem, RDY)
                    s.copy(rb2[:, j * THW + half * 512:
                               j * THW + (half + 1) * 512],
                           bank.t[:, :]).then_inc(bank.fre.sem, 1)
                    s.wait_ge(bank.fre.sem, FREV)
                    s.nop().then_inc(c_ln2.sem, 1)
                planB.add("ACT", act_bc2)
                c_ln2.val += 1
        L7 = c_ln2.val

        if TAILCUT < 5:
            continue
        ost_base = c_ost.val
        for dc in range(DC):
            vs0 = mkvs(2)
            OSTP = c_ostp[dc % 2].val

            def dve_out(v, dc=dc, th=th, L7=L7, vs0=vs0, OSTP=OSTP):
                if dc == 0:
                    v.wait_ge(c_ln2.sem, L7)
                if OSTP > 0:
                    v.wait_ge(c_ostp[dc % 2].sem, OSTP)
                v.tensor_tensor(l2x[dc % 2][:, :],
                                x1th[:, dc * THW:(dc + 1) * THW],
                                y_acc[:, dc * THW:(dc + 1) * THW], ALU.add)
                VS(v, vs0, 0)
                v.tensor_tensor(sq2[dc % 2][:, :], l2x[dc % 2][:, :],
                                rb2[:, 0:THW], ALU.mult)
                VS(v, vs0, 1)
                v.tensor_tensor(ostg[dc % 2][:, :], sq2[dc % 2][:, :],
                                rb2[:, THW:2 * THW],
                                ALU.subtract).then_inc(c_out.sem, 1)
            planB.add("DVE", dve_out)
            c_out.val += 1
            OUTV = c_out.val

            def sy_out(sy, dc=dc, th=th, OUTV=OUTV):
                sy.wait_ge(c_out.sem, OUTV)
                sy.dma_start(
                    out_d.ap()[:, dc * T + th * THW: dc * T + (th + 1) * THW],
                    ostg[dc % 2][:, :]).then_inc(c_ostp[dc % 2].sem, 16)
            planB.add("SYNC", sy_out)
            c_ostp[dc % 2].val += 16

    OST0 = c_ostp[0].val
    OST1 = c_ostp[1].val
    MSKF2 = c_msk.val

    def sy_final(sy):
        sy.wait_ge(c_ostp[0].sem, OST0)
        sy.wait_ge(c_ostp[1].sem, OST1)
        sy.wait_ge(c_msk.sem, MSKF2)
    planB.add("SYNC", sy_final)

    with nc.Block("phaseA") as blk:
        @blk.tensor
        def _(t):
            planA.emit("PE", t)

        @blk.vector
        def _(v):
            planA.emit("DVE", v)

        @blk.scalar
        def _(s):
            planA.emit("ACT", s)

        @blk.sync
        def _(sy):
            planA.emit("SYNC", sy)

        @blk.gpsimd
        def _(g):
            planA.emit("POOL", g)

    with nc.Block("phaseB") as blk:
        @blk.tensor
        def _(t):
            planB.emit("PE", t)

        @blk.vector
        def _(v):
            planB.emit("DVE", v)

        @blk.scalar
        def _(s):
            planB.emit("ACT", s)

        @blk.sync
        def _(sy):
            planB.emit("SYNC", sy)

        @blk.gpsimd
        def _(g):
            planB.emit("POOL", g)

    scopeB.close()
    nc.ctx.close()
    return nc



def _get_nc():
    if "nc" not in _CACHE:
        _CACHE["nc"] = _build()
    return _CACHE["nc"]


def kernel(L0_attn_x, L0_attn_wq, L0_attn_wk, L0_attn_wv, L0_attn_wo,
           L0_ffn_router_w, L0_ffn_gate_w, L0_ffn_up_w, L0_ffn_down_w,
           L0_ffn_shared_gate_w, L0_ffn_shared_up_w, L0_ffn_shared_down_w):
    x = np.ascontiguousarray(np.asarray(L0_attn_x, np.float32))
    wq = np.asarray(L0_attn_wq, np.float32)
    wk = np.asarray(L0_attn_wk, np.float32)
    wv = np.asarray(L0_attn_wv, np.float32)
    wo = np.asarray(L0_attn_wo, np.float32)
    rw = np.asarray(L0_ffn_router_w, np.float32)
    gw = np.asarray(L0_ffn_gate_w, np.float32)
    uw = np.asarray(L0_ffn_up_w, np.float32)
    dw = np.asarray(L0_ffn_down_w, np.float32)
    sgw = np.asarray(L0_ffn_shared_gate_w, np.float32)
    suw = np.asarray(L0_ffn_shared_up_w, np.float32)
    sdw = np.asarray(L0_ffn_shared_down_w, np.float32)

    tabs = _host_tables()
    shared = {
        "wqT": _swz_lhsT(wq.T), "wkT": _swz_lhsT(wk.T), "woT": _swz_lhsT(wo.T),
        "wvR": np.ascontiguousarray(
            np.ascontiguousarray(wv.T).reshape(4, 4, 128, 512).transpose(
                0, 2, 1, 3).reshape(4, 128, 2048)),
        "rtr": np.ascontiguousarray(
            rw.reshape(16, 128, 8).transpose(1, 0, 2).reshape(128, 128)),
        "gW": np.stack([_swz_lhsT(gw[e]) for e in range(NE)] + [_swz_lhsT(sgw)]),
        "uW": np.stack([_swz_lhsT(uw[e]) for e in range(NE)] + [_swz_lhsT(suw)]),
        "dW": np.stack([_swz_lhsT(dw[e]) for e in range(NE)] + [_swz_lhsT(sdw)]),
        "diagTd": tabs["diagT"], "qdecbd": tabs["qdecb"], "kdecd": tabs["kdec"],
        "identd": tabs["ident"], "onescd": tabs["onesc"],
        "onesrd": tabs["onesr"], "onehd": tabs["oneh"],
        "onehrd": tabs["onehr"], "selbd": tabs["selb"],
    }

    in_maps = []
    zero_tab = np.zeros((128, T), BF)
    zero_x = np.zeros((NBLK, 128, DC * TB), np.float32)
    for c in range(8):
        b, half = c // 2, c % 2
        xt = np.ascontiguousarray(x[b, half * T:(half + 1) * T, :].T)
        m = dict(shared)
        m["xb"] = _swz_x(xt)
        cos_o_t, sin_o_t = _rope_tables(half * T + np.arange(T))
        m["cosb"], m["sinb"] = cos_o_t, sin_o_t
        if half == 1:
            xtp = np.ascontiguousarray(x[b, 0:T, :].T)
            m["xp"] = _swz_x(xtp)
            cos_p_t, sin_p_t = _rope_tables(np.arange(T))
            m["cosp"], m["sinp"] = cos_p_t, sin_p_t
        else:
            m["xp"] = zero_x
            m["cosp"], m["sinp"] = zero_tab, zero_tab
        in_maps.append(m)

    nc = _get_nc()
    res = run_bass_kernel_spmd(nc, in_maps, core_ids=list(range(8)))

    out = np.empty((B, S, D), np.float32)
    for c in range(8):
        r = np.asarray(res.results[c]["out"], np.float32)
        blkm = r.reshape(128, DC, T).transpose(2, 1, 0).reshape(T, D)
        out[c // 2, (c % 2) * T:(c % 2 + 1) * T, :] = blkm
    return out
